# version 10
# baseline (speedup 1.0000x reference)
import contextlib
import math
import numpy as np
import ml_dtypes

import concourse.bass as bass
import concourse.mybir as mybir
from concourse.bass_utils import run_bass_kernel_spmd

F32 = mybir.dt.float32
BF16 = mybir.dt.bfloat16
I32 = mybir.dt.int32
AF = mybir.ActivationFunctionType
ALU = mybir.AluOpType
AX = mybir.AxisListType
bfloat16 = ml_dtypes.bfloat16

S = 2048
D = 1024
NT = S // 128
NC = D // 128
NSEQ = 2
NHEAD = 16
ENGS = ("tensor", "vector", "scalar", "gpsimd", "sync")
LNEPS = 1e-5
RMSEPS = float(np.finfo(np.float64).eps)
BITEPS = 1e-8
MASKNEG = -1.0e30

ALL_STAGES = (1, 2, 3, 4, 5, 6, 7, 8)
DBG = None



class Prog:
    def __init__(self, nc):
        self.nc = nc
        self.ops = []
        self.lw = {}
        self.rd = {}

    def add(self, eng, fn, r=(), w=(), dma=False, tag=""):
        deps = set()
        for k in r:
            if k in self.lw:
                deps.add(self.lw[k])
        for k in w:
            if k in self.lw:
                deps.add(self.lw[k])
            deps.update(self.rd.get(k, ()))
        i = len(self.ops)
        deps.discard(i)
        self.ops.append({"id": i, "eng": eng, "fn": fn, "deps": sorted(deps),
                         "dma": dma, "tag": tag})
        for k in r:
            self.rd.setdefault(k, set()).add(i)
        for k in w:
            self.lw[k] = i
            self.rd[k] = set()
        return i

    def barrier(self):
        n = self._barn = getattr(self, "_barn", 0) + 1
        for e in ENGS:
            self.add(e, lambda eng: eng.nop(), r=[], w=[f"bar{n}_{e}"])
        for e in ENGS:
            self.add(e, lambda eng: eng.nop(),
                     r=[f"bar{n}_{x}" for x in ENGS], w=[])

    def emit(self, sems, heat_ap=None):
        ops = self.ops
        need = set()
        for op in ops:
            if op["dma"]:
                need.add(op["id"])
            for dd in op["deps"]:
                de = ops[dd]["eng"]
                if de != op["eng"]:
                    need.add(dd)
                elif de != "tensor" and not ops[dd]["dma"]:
                    need.add(dd)
        def sem_of(op):
            return "gdma" if (op["eng"] == "gpsimd" and op["dma"]) else op["eng"]
        cnt = {e: 0 for e in list(ENGS) + ["gdma"]}
        val = {}
        for op in ops:
            if op["id"] in need:
                cnt[sem_of(op)] += 16 if op["dma"] else 1
                val[op["id"]] = cnt[sem_of(op)]
        per = {e: [op for op in ops if op["eng"] == e] for e in ENGS}
        nc = self.nc
        with nc.Block() as blk:
            for e in ENGS:
                if not per[e]:
                    continue

                def body(eng, e=e):
                    wm = {x: 0 for x in list(ENGS) + ["gdma"]}
                    cum = {x: 0 for x in list(ENGS) + ["gdma"]}
                    for op in per[e]:
                        so = sem_of(op)
                        for dd in op["deps"]:
                            ds = sem_of(ops[dd])
                            if ds == e and ops[dd]["eng"] == e:
                                continue
                            v = val.get(dd)
                            if v is None or wm[ds] >= v:
                                continue
                            if e == "tensor" and heat_ap is not None:
                                hv, hl, hr = heat_ap
                                step = 16 if ds in ("sync", "gdma") else 1
                                for j in range(48):
                                    u = wm[ds] + (j + 1) * step
                                    if u >= v:
                                        break
                                    eng.wait_ge(sems[ds], u)
                                    eng.matmul(hv(), hl(), hr(),
                                               start=True, stop=True)
                            eng.wait_ge(sems[ds], v)
                            wm[ds] = v
                        if op["dma"] and cum[so] > wm[so]:
                            eng.wait_ge(sems[so], cum[so])
                            wm[so] = cum[so]
                        ins = op["fn"](eng)
                        if op["id"] in need:
                            ins.then_inc(sems[so], 16 if op["dma"] else 1)
                            cum[so] += 16 if op["dma"] else 1
                            if not op["dma"]:
                                eng.wait_ge(sems[so], cum[so])
                                wm[so] = cum[so]
                    if e == "sync" and cnt["sync"] > wm["sync"]:
                        eng.wait_ge(sems["sync"], cnt["sync"])
                    if e == "gpsimd" and cnt["gdma"] > wm["gdma"]:
                        eng.wait_ge(sems["gdma"], cnt["gdma"])
                getattr(blk, e)(body)



def _bk(a):
    return np.ascontiguousarray(a)


def host_prepare(inputs):
    g = {}
    g["w_l0aa_qkv"] = _bk(np.concatenate(
        [inputs["L0_attn_a_Wq"].T, inputs["L0_attn_a_Wk"].T,
         inputs["L0_attn_a_Wv"].T], axis=1))
    g["w_l0aa_p"] = _bk(inputs["L0_attn_a_Wp"].T)
    g["gain_l0aa"] = _bk(np.asarray(inputs["L0_attn_a_q_gain"])[None, :])
    g["w_l0ab_qkv"] = _bk(np.concatenate(
        [inputs["L0_attn_b_W_q"].T, inputs["L0_attn_b_W_k"].T,
         inputs["L0_attn_b_W_v"].T], axis=1))
    g["w_l0ab_o"] = _bk(inputs["L0_attn_b_W_o"].T)
    g["w_l1aa_qkv"] = _bk(inputs["L1_attn_a_Wqkv"])
    g["b_l1aa_qkv"] = _bk(np.asarray(inputs["L1_attn_a_bqkv"])[None, :])
    g["w_l1aa_o"] = _bk(inputs["L1_attn_a_Wout"])
    g["b_l1aa_o"] = _bk(np.asarray(inputs["L1_attn_a_bout"])[None, :])
    g["w_l1ab_qkv"] = _bk(np.concatenate(
        [inputs["L1_attn_b_Wq"], inputs["L1_attn_b_Wk"],
         inputs["L1_attn_b_Wv"]], axis=1))
    g["w_l1ab_o"] = _bk(inputs["L1_attn_b_Wo"])
    g["b_l1ab_o"] = _bk(np.asarray(inputs["L1_attn_b_bo"])[None, :])
    g["rel_emb"] = _bk(np.asarray(inputs["L1_attn_b_rel_emb"])[None, :])
    g["w_l1ffn_1"] = _bk(np.asarray(inputs["L1_ffn_w1"]).T)
    g["b_l1ffn_1"] = _bk(np.asarray(inputs["L1_ffn_b1"]).reshape(32, 128).T)
    g["w_l1ffn_2"] = _bk(np.asarray(inputs["L1_ffn_w2"]).T)
    g["b_l1ffn_2"] = _bk(np.asarray(inputs["L1_ffn_b2"])[None, :])
    g["w_ffn_node"] = _bk(np.concatenate(
        [np.asarray(inputs[f"L0_ffn_{p}_node_w"]).T for p in "qkv"], axis=1))
    g["b_ffn_node"] = _bk(np.concatenate(
        [np.asarray(inputs[f"L0_ffn_{p}_node_b"]) for p in "qkv"])[None, :])
    for pp in "qkv":
        g[f"w_ffn{pp}_1"] = _bk(np.asarray(inputs[f"L0_ffn_{pp}_w1"])
                                .transpose(1, 0, 2).reshape(D, D))
        g[f"b_ffn{pp}_1"] = _bk(np.asarray(inputs[f"L0_ffn_{pp}_b1"])
                                .reshape(-1).reshape(NC, 128).T)
        g[f"w_ffn{pp}_2"] = _bk(np.asarray(inputs[f"L0_ffn_{pp}_w2"])
                                .reshape(D, D))
        g[f"b2_{pp}"] = _bk(np.asarray(inputs[f"L0_ffn_{pp}_b2"]))
        g[f"b1r_{pp}"] = _bk(np.asarray(inputs[f"L0_ffn_{pp}_b1"])
                             .reshape(1, D))
    g["ident_f"] = np.eye(128, dtype=np.float32)
    g["ident_b"] = np.eye(128, dtype=np.float32).astype(bfloat16)
    qi = np.arange(128)[:, None]
    ki = np.arange(128)[None, :]
    g["cmask"] = np.where(ki <= qi, 0.0, MASKNEG).astype(np.float32)
    g["cmask_t"] = np.where(qi <= ki, 0.0, MASKNEG).astype(np.float32)
    e2 = np.zeros((2, 128), dtype=np.float32)
    e2[0, 0:64] = 1.0
    e2[1, 64:128] = 1.0
    g["e2"] = e2.astype(bfloat16)
    e16 = np.zeros((16, 8 * 128), dtype=np.float32)
    for hb in range(8):
        e16[2 * hb, hb * 128: hb * 128 + 64] = 1.0
        e16[2 * hb + 1, hb * 128 + 64: hb * 128 + 128] = 1.0
    g["e16"] = _bk(e16.astype(bfloat16))
    g["ones_f1"] = np.ones((1, 128), dtype=np.float32)
    g["ones_b1"] = np.ones((1, 128), dtype=np.float32).astype(bfloat16)
    d0 = np.minimum(np.abs(qi - ki), 16)
    d128 = np.minimum(np.abs(qi - ki + 128), 16)
    ind0 = np.stack([(d0 == v).astype(np.float32) for v in range(17)])
    ind128 = np.stack([(d128 == v).astype(np.float32) for v in range(17)])
    g["ind0"] = _bk(ind0.transpose(1, 0, 2).reshape(128, 17 * 128))
    g["ind128"] = _bk(ind128.transpose(1, 0, 2).reshape(128, 17 * 128))

    x = np.asarray(inputs["L0_attn_a_x"])

    def per_core(c):
        m = dict(g)
        m["x_in"] = _bk(x[2 * c: 2 * c + 2])
        return m
    return per_core


DRAM_SPECS = [
    ("x_in", [NSEQ, S, D], F32, "ExternalInput"),
    ("w_l0aa_qkv", [D, 1536], F32, "ExternalInput"),
    ("w_l0aa_p", [D, D], F32, "ExternalInput"),
    ("gain_l0aa", [1, 16], F32, "ExternalInput"),
    ("w_l0ab_qkv", [D, 3 * D], F32, "ExternalInput"),
    ("w_l0ab_o", [D, D], F32, "ExternalInput"),
    ("w_l1aa_qkv", [D, 3 * D], F32, "ExternalInput"),
    ("b_l1aa_qkv", [1, 3 * D], F32, "ExternalInput"),
    ("w_l1aa_o", [D, D], F32, "ExternalInput"),
    ("b_l1aa_o", [1, D], F32, "ExternalInput"),
    ("w_l1ab_qkv", [D, 3 * D], F32, "ExternalInput"),
    ("w_l1ab_o", [D, D], F32, "ExternalInput"),
    ("b_l1ab_o", [1, D], F32, "ExternalInput"),
    ("rel_emb", [1, 17], F32, "ExternalInput"),
    ("w_l1ffn_1", [D, 4096], F32, "ExternalInput"),
    ("b_l1ffn_1", [128, 32], F32, "ExternalInput"),
    ("w_l1ffn_2", [4096, D], F32, "ExternalInput"),
    ("b_l1ffn_2", [1, D], F32, "ExternalInput"),
    ("w_ffn_node", [D, 45], F32, "ExternalInput"),
    ("b_ffn_node", [1, 45], F32, "ExternalInput"),
    *[(f"w_ffn{pp}_1", [D, D], F32, "ExternalInput") for pp in "qkv"],
    *[(f"b_ffn{pp}_1", [128, 8], F32, "ExternalInput") for pp in "qkv"],
    *[(f"w_ffn{pp}_2", [D, D], F32, "ExternalInput") for pp in "qkv"],
    *[(f"b2_{pp}", [16, D], F32, "ExternalInput") for pp in "qkv"],
    *[(f"b1r_{pp}", [1, D], F32, "ExternalInput") for pp in "qkv"],
    ("ident_f", [128, 128], F32, "ExternalInput"),
    ("ident_b", [128, 128], BF16, "ExternalInput"),
    ("cmask", [128, 128], F32, "ExternalInput"),
    ("cmask_t", [128, 128], F32, "ExternalInput"),
    ("e2", [2, 128], BF16, "ExternalInput"),
    ("e16", [16, 1024], BF16, "ExternalInput"),
    ("ones_f1", [1, 128], F32, "ExternalInput"),
    ("ones_b1", [1, 128], BF16, "ExternalInput"),
    ("ind0", [128, 17 * 128], F32, "ExternalInput"),
    ("ind128", [128, 17 * 128], F32, "ExternalInput"),
    ("x_out", [NSEQ, S, D], F32, "ExternalOutput"),
    ("dbg_out", [128, 16384], F32, "ExternalOutput"),
    ("q_scr", [NC, S, 128], BF16, "Internal"),
    ("k_scr", [NC, S, 128], BF16, "Internal"),
    ("v_scr", [NC, S, 128], BF16, "Internal"),
    ("qf_scr", [NC, 128, S], BF16, "Internal"),
    ("kf_scr", [NC, 128, S], BF16, "Internal"),
    ("shift_scr", [64, S], BF16, "Internal"),
]


class B:
    pass


def build_nc(stages=ALL_STAGES, seqs=(0, 1)):
    nc = bass.Bass("TRN2", target_bir_lowering=False,
                   detect_race_conditions=False)
    b = B()
    b.nc = nc
    b.dram = {}
    for nm, shp, dt, kind in DRAM_SPECS:
        b.dram[nm] = nc.dram_tensor(nm, shp, dt, kind=kind).ap()
    es = contextlib.ExitStack()

    def sb(name, shape, dt):
        return es.enter_context(nc.sbuf_tensor(name, shape, dt))
    b.sb = sb
    b.R = sb("R", [128, NT * D], F32)
    b.ident_f = sb("identf", [128, 128], F32)
    b.ident_b = sb("identb", [128, 128], BF16)
    b.cmask = sb("cmask_s", [128, 128], F32)
    b.cmask_t = sb("cmaskt_s", [128, 128], F32)
    b.e2 = sb("e2sb", [2, 128], BF16)
    b.e16 = sb("e16sb", [16, 1024], BF16)
    b.ones_f1 = sb("onesf1", [1, 128], F32)
    b.ones_b1 = sb("onesb1", [1, 128], BF16)
    b.ones_cf = sb("onescf", [128, 1], F32)
    b.ones_b512 = sb("onesb512", [1, 512], BF16)
    b.ones_colb = sb("onescolb", [128, 1], BF16)
    b.ones_b64 = sb("onesb64", [64, 64], BF16)
    b.lvconst = sb("lvconst", [128, 16], F32)
    b.sinc = sb("sinc", [128, NT * 32], F32)
    b.cosc = sb("cosc", [128, NT * 32], F32)
    b.stat = sb("stat", [128, 64], F32)
    b.c_ln_eps = sb("clneps", [128, 1], F32)
    b.c_rms_eps = sb("crmseps", [128, 1], F32)
    b.c_half_pi = sb("chalfpi", [128, 1], F32)
    b.sq_dump = sb("sqdump", [128, D], BF16)
    b.rs_i = sb("rsi", [128, D], I32)
    b.rs_f = sb("rsf", [128, D], F32)
    b.rs_e = sb("rse", [128, D], F32)
    b.rs_c = sb("rsc", [128, D], F32)
    b.rs_c2 = sb("rsc2", [128, D], F32)
    b.d0_8 = sb("d08", [128, 128], F32)
    b.d128_8 = sb("d1288", [128, 128], F32)
    b.d0t_8 = sb("d0t8", [128, 128], F32)
    b.d128t_8 = sb("d128t8", [128, 128], F32)
    b.a16_8 = sb("a168", [128, 1], F32)
    b.absa16_8 = sb("absa168", [128, 1], F32)
    b.relb = sb("relb", [128, 17], F32)
    b.gainb = sb("gainb", [128, 16], F32)
    b.PS4 = es.enter_context(nc.psum_tensor("ps4b", [128, 2048], F32))
    b.pse = [es.enter_context(nc.psum_tensor(f"pse{i}", [128, 512], F32))
             for i in range(2)]
    b.psb = [es.enter_context(nc.psum_tensor("psbb0", [128, 1024], BF16)),
             es.enter_context(nc.psum_tensor("psbb1", [128, 512], F32))]
    b.sems = {e: es.enter_context(nc.semaphore(f"sem_{e}"))
              for e in list(ENGS) + ["gdma"]}
    b.p = Prog(nc)
    with es:
        _build_body(b, stages, seqs)
        b.p.emit(b.sems, heat_ap=(lambda: b.psb[1][0:2, 0:64],
                                  lambda: b.ident_b[0:2, 0:2],
                                  lambda: b.ident_b[0:2, 0:64]))
    return nc


def PQ(b, i):
    return b.PS4[:, i * 512:(i + 1) * 512]


def dbg_tap(b, name, ap_fn, rkeys, n, cast=False):
    if DBG != name or getattr(b, "_dbg_done", False):
        return
    b._dbg_done = True
    eng = "gpsimd" if cast else "sync"
    b.p.add(eng, lambda e: e.dma_start(b.dram["dbg_out"][:, 0:n], ap_fn()),
            r=list(rkeys), w=["DBGOUT"], dma=True)



def _load_consts(b):
    p, d = b.p, b.dram
    for nm, dst in [("ident_f", b.ident_f), ("ident_b", b.ident_b),
                    ("cmask", b.cmask), ("cmask_t", b.cmask_t),
                    ("e2", b.e2), ("e16", b.e16), ("ones_f1", b.ones_f1),
                    ("ones_b1", b.ones_b1)]:
        p.add("sync", lambda e, nm=nm, dst=dst: e.dma_start(dst[:, :], d[nm]),
              w=[nm], dma=True)
    p.add("vector", lambda e: e.memset(b.ones_cf[:, :], 1.0), w=["ones_cf"])
    p.add("vector", lambda e: e.memset(b.ones_b512[:, :], 1.0),
          w=["ones_b512"])
    p.add("vector", lambda e: e.memset(b.ones_colb[:, :], 1.0),
          w=["ones_colb"])
    p.add("vector", lambda e: e.memset(b.ones_b64[:, :], 1.0),
          w=["ones_b64"])
    p.add("gpsimd", lambda e: e.iota(b.lvconst[:, 0:16], [[1, 16]],
                                     channel_multiplier=0,
                                     allow_small_or_imprecise_dtypes=True),
          w=["lvconst"])
    p.add("vector", lambda e: e.memset(b.c_ln_eps[:, :], LNEPS), w=["c_ln_eps"])
    p.add("vector", lambda e: e.memset(b.c_rms_eps[:, :], RMSEPS),
          w=["c_rms_eps"])
    p.add("vector", lambda e: e.memset(b.c_half_pi[:, :], math.pi / 2),
          w=["c_half_pi"])


def _build_trig(b):
    p, nc = b.p, b.nc
    _st = contextlib.ExitStack()
    sb = lambda nm, shp, dt: _st.enter_context(nc.sbuf_tensor(nm, shp, dt))
    b._trig_st = _st
    invf = sb("invf", [1, 32], F32)
    invfb = sb("invfb", [128, 32], F32)
    pos = sb("pos", [128, NT], F32)
    ph = sb("ph", [128, NT * 32], F32)
    kk = sb("trig_k", [128, NT * 32], F32)
    wv = sb("trig_w", [128, NT * 32], F32)
    n = NT * 32
    p.add("gpsimd", lambda e: e.iota(invf[0:1, 0:32], [[1, 32]],
                                     channel_multiplier=0,
                                     allow_small_or_imprecise_dtypes=True),
          w=["invf"])
    p.add("scalar", lambda e: e.activation(invf[0:1, 0:32], invf[0:1, 0:32],
                                           AF.Exp, bias=0.0,
                                           scale=-math.log(10000.0) / 32.0),
          r=["invf"], w=["invf"])
    p.add("tensor", lambda e: e.matmul(b.pse[0][0:128, 0:32],
                                       b.ones_f1[0:1, 0:128],
                                       invf[0:1, 0:32], start=True, stop=True),
          r=["invf", "ones_f1"], w=["PE0"])
    p.add("scalar", lambda e: e.copy(invfb[:, :], b.pse[0][0:128, 0:32]),
          r=["PE0"], w=["invfb"])
    p.add("gpsimd", lambda e: e.iota(pos[:, 0:NT], [[128, NT]],
                                     channel_multiplier=1,
                                     allow_small_or_imprecise_dtypes=True),
          w=["pos"])
    p.add("vector", lambda e: e.tensor_tensor(
        ph[:, :].rearrange("a (t j) -> a t j", j=32),
        pos[:, 0:NT].unsqueeze(2).broadcast_to([128, NT, 32]),
        invfb[:, 0:32].unsqueeze(1).broadcast_to([128, NT, 32]), ALU.mult),
        r=["pos", "invfb"], w=["ph"])
    twopi = 2.0 * math.pi
    p.add("vector", lambda e: e.tensor_scalar(kk[:, :], ph[:, :],
                                              1.0 / twopi, None, ALU.mult),
          r=["ph"], w=["kk"])
    p.add("vector", lambda e: e.tensor_copy(b.rs_i[:, 0:n], kk[:, :]),
          r=["kk"], w=["rs_i"])
    p.add("vector", lambda e: e.tensor_copy(kk[:, :], b.rs_i[:, 0:n]),
          r=["rs_i"], w=["kk"])
    p.add("vector", lambda e: e.scalar_tensor_tensor(
        b.rs_e[:, 0:n], ph[:, :], 1.0 / twopi, kk[:, :],
        ALU.mult, ALU.subtract), r=["ph", "kk"], w=["rs_e"])
    p.add("vector", lambda e: e.tensor_single_scalar(
        b.rs_c[:, 0:n], b.rs_e[:, 0:n], 0.5, ALU.is_gt),
        r=["rs_e"], w=["rs_c"])
    p.add("vector", lambda e: e.tensor_single_scalar(
        b.rs_c2[:, 0:n], b.rs_e[:, 0:n], -0.5, ALU.is_lt),
        r=["rs_e"], w=["rs_c2"])
    p.add("vector", lambda e: e.tensor_tensor(
        kk[:, :], kk[:, :], b.rs_c[:, 0:n], ALU.add),
        r=["kk", "rs_c"], w=["kk"])
    p.add("vector", lambda e: e.tensor_tensor(
        kk[:, :], kk[:, :], b.rs_c2[:, 0:n], ALU.subtract),
        r=["kk", "rs_c2"], w=["kk"])
    p.add("vector", lambda e: e.scalar_tensor_tensor(
        wv[:, :], kk[:, :], -twopi, ph[:, :], ALU.mult, ALU.add),
        r=["kk", "ph"], w=["wv"])
    p.add("vector", lambda e: e.tensor_scalar(
        wv[:, :], wv[:, :], -math.pi, math.pi, ALU.max, ALU.min),
        r=["wv"], w=["wv"])
    p.add("scalar", lambda e: e.activation(b.sinc[:, :], wv[:, :], AF.Sin,
                                           bias=0.0, scale=1.0),
          r=["wv"], w=["sinc"])
    p.add("scalar", lambda e: e.activation(kk[:, :], wv[:, :], AF.Abs,
                                           bias=0.0, scale=1.0),
          r=["wv"], w=["kk"])
    p.add("scalar", lambda e: e.activation(b.cosc[:, :], kk[:, :], AF.Sin,
                                           bias=b.c_half_pi[:, 0:1],
                                           scale=-1.0),
          r=["kk", "c_half_pi"], w=["cosc"])
    p.barrier()
    _st.close()


def _build_relpos(b):
    p, d = b.p, b.dram
    _st = contextlib.ExitStack()
    sb = lambda nm, shp, dt: _st.enter_context(b.nc.sbuf_tensor(nm, shp, dt))
    relr = sb("relr", [1, 17], F32)
    ind = sb("indsb", [128, 17 * 128], F32)
    gr = sb("gainr", [1, 16], F32)
    p.add("sync", lambda e: e.dma_start(relr[:, :], d["rel_emb"]),
          w=["relr"], dma=True)
    p.add("tensor", lambda e: e.matmul(b.pse[0][0:128, 0:17],
                                       b.ones_f1[0:1, 0:128],
                                       relr[0:1, 0:17], start=True, stop=True),
          r=["relr", "ones_f1"], w=["PE0"])
    p.add("scalar", lambda e: e.copy(b.relb[:, :], b.pse[0][0:128, 0:17]),
          r=["PE0"], w=["relb"])
    p.add("vector", lambda e: e.tensor_scalar(
        b.a16_8[:, :], b.relb[:, 16:17], 0.125, None, ALU.mult),
        r=["relb"], w=["a16_8"])
    p.add("scalar", lambda e: e.activation(b.absa16_8[:, :], b.relb[:, 16:17],
                                           AF.Abs, bias=0.0, scale=0.125),
          r=["relb"], w=["absa16_8"])
    for nm, dst, kd in [("ind0", b.d0_8, "Dind0"), ("ind128", b.d128_8,
                                                    "Dind128")]:
        p.add("sync", lambda e, nm=nm: e.dma_start(ind[:, :], d[nm]),
              w=["indsb"], dma=True)
        p.add("vector", lambda e, dst=dst: e.memset(dst[:, :], 0.0), w=[kd])
        for v in range(17):
            p.add("vector", lambda e, v=v, dst=dst: e.scalar_tensor_tensor(
                dst[:, :], ind[:, v * 128:(v + 1) * 128], b.relb[:, v:v + 1],
                dst[:, :], ALU.mult, ALU.add),
                r=["indsb", "relb", kd], w=[kd])
        p.add("vector", lambda e, dst=dst: e.tensor_scalar(
            dst[:, :], dst[:, :], 0.125, None, ALU.mult), r=[kd], w=[kd])
    for src, dst, ks, kd in [(b.d0_8, b.d0t_8, "Dind0", "Dt0"),
                             (b.d128_8, b.d128t_8, "Dind128", "Dt128")]:
        p.add("tensor", lambda e, src=src: e.transpose(
            b.pse[1][0:128, 0:128].bitcast(F32) if False else
            b.pse[1][0:128, 0:128], src[:, :], b.ident_f[:, :]),
            r=[ks, "ident_f"], w=["PE1"])
        p.add("scalar", lambda e, dst=dst: e.copy(
            dst[:, :], b.pse[1][0:128, 0:128]), r=["PE1"], w=[kd])
    p.add("sync", lambda e: e.dma_start(gr[:, :], d["gain_l0aa"]),
          w=["gainr"], dma=True)
    p.add("tensor", lambda e: e.matmul(b.pse[0][0:128, 0:16],
                                       b.ones_f1[0:1, 0:128],
                                       gr[0:1, 0:16], start=True, stop=True),
          r=["gainr", "ones_f1"], w=["PE0"])
    p.add("scalar", lambda e: e.copy(b.gainb[:, :], b.pse[0][0:128, 0:16]),
          r=["PE0"], w=["gainb"])
    p.barrier()
    _st.close()



def rint_ladder(b, src_key, src_ap_fn, out_ap_fn, n, out_key,
                lo=None, hi=None):
    p = b.p
    p.add("vector", lambda e: e.tensor_copy(b.rs_i[:, 0:n], src_ap_fn()),
          r=[src_key], w=["rs_i"])
    p.add("vector", lambda e: e.tensor_tensor(
        b.rs_e[:, 0:n], src_ap_fn(), b.rs_i[:, 0:n], ALU.subtract),
        r=[src_key, "rs_i"], w=["rs_e"])
    p.add("vector", lambda e: e.tensor_single_scalar(
        b.rs_c[:, 0:n], b.rs_e[:, 0:n], 0.5, ALU.is_gt),
        r=["rs_e"], w=["rs_c"])
    p.add("vector", lambda e: e.tensor_single_scalar(
        b.rs_c2[:, 0:n], b.rs_e[:, 0:n], -0.5, ALU.is_lt),
        r=["rs_e"], w=["rs_c2"])
    p.add("vector", lambda e: e.tensor_tensor(
        b.rs_c[:, 0:n], b.rs_c[:, 0:n], b.rs_c2[:, 0:n], ALU.subtract),
        r=["rs_c", "rs_c2"], w=["rs_c"])
    p.add("vector", lambda e: e.tensor_tensor(
        b.rs_c[:, 0:n], b.rs_c[:, 0:n], b.rs_i[:, 0:n], ALU.add),
        r=["rs_c", "rs_i"], w=["rs_c"])
    if lo is not None:
        p.add("vector", lambda e: e.tensor_scalar(
            out_ap_fn(), b.rs_c[:, 0:n], float(lo), float(hi),
            ALU.max, ALU.min), r=["rs_c"], w=[out_key])
    else:
        p.add("vector", lambda e: e.tensor_copy(out_ap_fn(), b.rs_c[:, 0:n]),
              r=["rs_c"], w=[out_key])


def ln_tile(b, t, out_ap_fn, out_key):
    p = b.p
    st = b.stat
    Rt = lambda: b.R[:, t * D:(t + 1) * D]
    rk = f"R.{t}"
    p.add("vector", lambda e: e.reduce_sum(st[:, 0:1], Rt(), AX.X),
          r=[rk], w=["st0"])
    p.add("scalar", lambda e: e.activation(b.sq_dump[:, :], Rt(), AF.Square,
                                           bias=0.0, scale=1.0,
                                           accum_out=st[:, 1:2]),
          r=[rk], w=["st1", "sq_dump"])
    p.add("vector", lambda e: e.tensor_scalar(st[:, 2:3], st[:, 0:1],
                                              1.0 / D, None, ALU.mult),
          r=["st0"], w=["st2"])
    p.add("vector", lambda e: e.tensor_scalar(st[:, 3:4], st[:, 2:3],
                                              st[:, 2:3], None, ALU.mult),
          r=["st2"], w=["st3"])
    p.add("vector", lambda e: e.scalar_tensor_tensor(
        st[:, 4:5], st[:, 1:2], 1.0 / D, st[:, 3:4], ALU.mult, ALU.subtract),
        r=["st1", "st3"], w=["st4"])
    p.add("scalar", lambda e: e.activation(st[:, 5:6], st[:, 4:5], AF.Sqrt,
                                           bias=b.c_ln_eps[:, 0:1], scale=1.0),
          r=["st4", "c_ln_eps"], w=["st5"])
    p.add("vector", lambda e: e.reciprocal(st[:, 6:7], st[:, 5:6]),
          r=["st5"], w=["st6"])
    p.add("vector", lambda e: e.tensor_scalar(
        out_ap_fn(), Rt(), st[:, 2:3], st[:, 6:7], ALU.subtract, ALU.mult),
        r=[rk, "st2", "st6"], w=[out_key])


def transpose_tile(b, srcf, src_key, dst_ap_fn, dst_key, nchunk=NC, pi=0):
    p = b.p
    ps = b.psb[0]
    pk = "PSB0"
    for c in range(nchunk):
        p.add("tensor", lambda e, c=c: e.transpose(
            ps[:, c * 128:(c + 1) * 128], srcf(c), b.ident_b[:, :]),
            r=[src_key, "ident_b", pk + "r"], w=[pk], tag="tp")
    p.add("scalar", lambda e: e.copy(dst_ap_fn(), ps[:, 0:nchunk * 128]),
          r=[pk], w=[dst_key, pk + "r"])



def mha_stage(b, seq, kind):
    p, d, nc = b.p, b.dram, b.nc
    stx = contextlib.ExitStack()
    mk = lambda stk: (lambda nm, shp, dt: stk.enter_context(
        nc.sbuf_tensor(f"{kind}s{seq}_{nm}", shp, dt)))
    sb = mk(stx)

    causal = kind in ("l0aa", "l1ab")
    rope = {"l0aa": "neg_half", "l0ab": "inter", "l1aa": "pos_half",
            "l1ab": None}[kind]
    gqa = kind == "l0aa"
    nkv = 4 if gqa else NHEAD
    bit = kind == "l0aa"
    rel = kind == "l1ab"
    scale = 1.0 if bit else 0.125
    wq_nm = {"l0aa": "w_l0aa_qkv", "l0ab": "w_l0ab_qkv",
             "l1aa": "w_l1aa_qkv", "l1ab": "w_l1ab_qkv"}[kind]
    wo_nm = {"l0aa": "w_l0aa_p", "l0ab": "w_l0ab_o",
             "l1aa": "w_l1aa_o", "l1ab": "w_l1ab_o"}[kind]
    bqkv_nm = "b_l1aa_qkv" if kind == "l1aa" else None
    bo_nm = {"l1aa": "b_l1aa_o", "l1ab": "b_l1ab_o"}.get(kind)
    NDOUT = 1536 if bit else 3 * D

    attn = sb("attn", [128, NC * S], BF16)
    gscl = sb("gscl", [128, 4], F32)
    wsb_p = sb("wp", [128, NC * D], BF16) if bit else None
    ascl = sb("ascl", [128, 4], F32) if bit else None
    b._cur_attn = attn
    b._cur_wp = wsb_p
    b._cur_ascl = ascl
    b._cur_gscl = gscl

    stA = contextlib.ExitStack()
    sbA = mk(stA)
    wsb = sbA("w", [128, NC * NDOUT], BF16)
    nb_ = 2
    nq_ = 1 if bit else 2
    xt2 = [sbA("xt0", [128, D], F32)]
    xb2 = [sbA(f"xb{i}", [128, D], BF16) for i in range(nb_)]
    xf2 = [sbA(f"xf{i}", [128, NC * 128], BF16) for i in range(nb_)]
    qkvt2 = [sbA("qkvt0", [128, NDOUT], F32)]
    spill2 = [sbA("spill0", [128, D], BF16)]
    if nq_ == 2:
        qkvt2.append(attn[:, 4096:4096 + 2 * NDOUT].bitcast(F32))
        spill2.append(attn[:, 4096 + 2 * NDOUT:4096 + 2 * NDOUT + D])
    ga2 = [sbA(f"ga{i}", [128, 4], F32) for i in range(nb_)] if bit else None
    bias_r = attn[0:1, 0:3 * D] if bqkv_nm else None

    if not bit:
        p.add("gpsimd", lambda e: e.dma_start(
            wsb[:, :].rearrange("a (c n) -> a c n", c=NC),
            d[wq_nm].rearrange("(c a) n -> a c n", c=NC)),
            w=["Wqkv"], dma=True)
    if bqkv_nm:
        p.add("gpsimd", lambda e: e.dma_start(bias_r[:, :], d[bqkv_nm]),
              w=["biasr"], dma=True)

    if bit:
        _l0aa_quant_weights(b, sbA, wsb, wsb_p, ascl)
        dbg_tap(b, "wsb", lambda: wsb[:, 0:12288], ["Wqkv"], 12288, cast=True)
        dbg_tap(b, "wp", lambda: wsb_p[:, 0:8192], ["Wp"], 8192, cast=True)
        dbg_tap(b, "ascl", lambda: ascl[:, 0:4], ["ascl"], 4)

    def emit_proj(t, xf, xfk, qkvt, qk_, ga, gk):
        nsl = NDOUT // 512
        for s_ in range(nsl):
            ps = b.pse[s_ % 2]
            pk = f"PE{s_ % 2}"
            for c in range(NC):
                p.add("tensor", lambda e, c=c, s_=s_, ps=ps, xf=xf: e.matmul(
                    ps[:, :], xf[:, c * 128:(c + 1) * 128],
                    wsb[:, c * NDOUT + s_ * 512: c * NDOUT + s_ * 512 + 512],
                    start=(c == 0), stop=(c == NC - 1 and not bqkv_nm)),
                    r=[xfk, "Wqkv", pk + "r"] if c == 0 else [xfk, "Wqkv"],
                    w=[pk], tag="mm")
            if bqkv_nm:
                p.add("tensor", lambda e, s_=s_, ps=ps: e.matmul(
                    ps[:, :], b.ones_b1[0:1, 0:128],
                    bias_r[0:1, s_ * 512:s_ * 512 + 512],
                    start=False, stop=True),
                    r=["biasr", "ones_b1", pk], w=[pk])
            if not bit:
                p.add("scalar", lambda e, s_=s_, ps=ps, qkvt=qkvt: e.copy(
                    qkvt[:, s_ * 512:s_ * 512 + 512], ps[:, :]),
                    r=[pk, qk_], w=[qk_, pk + "r"])
            else:
                for (a0, a1, idx) in [(0, 1024, 0), (1024, 1280, 1),
                                      (1280, 1536, 2)]:
                    lo = max(s_ * 512, a0)
                    hi = min(s_ * 512 + 512, a1)
                    if lo >= hi:
                        continue
                    p.add("vector", lambda e, idx=idx, ga=ga: e.tensor_scalar(
                        ga[:, 3:4], ga[:, 1:2], ascl[:, idx:idx + 1],
                        None, ALU.mult), r=[gk, "ascl"], w=[gk])
                    p.add("vector",
                          lambda e, lo=lo, hi=hi, s_=s_, ps=ps, qkvt=qkvt,
                          ga=ga:
                          e.tensor_scalar(
                              qkvt[:, lo:hi],
                              ps[:, lo - s_ * 512:hi - s_ * 512],
                              ga[:, 3:4], None, ALU.mult),
                          r=[pk, gk, qk_], w=[qk_, pk + "r"])

    pend = None
    for t in range(NT):
        i = t % nb_
        iq = t % nq_
        xt, xb, xf = xt2[0], xb2[i], xf2[i]
        qkvt, spill = qkvt2[iq], spill2[iq]
        xtk, xbk, xfk = "xt0", f"xb{i}", f"xf{i}"
        qk_, spk = f"qkvt{iq}", f"spill{iq}"
        ln_tile(b, t, lambda xt=xt: xt[:, :], xtk)
        ga = gk = None
        if bit:
            ga = ga2[i]
            gk = f"ga{i}"
            p.add("vector", lambda e, xt=xt, ga=ga: e.reduce_max(
                ga[:, 0:1], xt[:, :], AX.X, apply_absolute_value=True),
                r=[xtk, gk], w=[gk])
            p.add("vector", lambda e, ga=ga: e.tensor_scalar(
                ga[:, 1:2], ga[:, 0:1], BITEPS, 1.0 / 127.0,
                ALU.max, ALU.mult), r=[gk], w=[gk])
            p.add("vector", lambda e, ga=ga: e.reciprocal(
                ga[:, 2:3], ga[:, 1:2]), r=[gk], w=[gk])
            p.add("vector", lambda e, xt=xt, ga=ga: e.tensor_scalar(
                b.rs_f[:, 0:D], xt[:, :], ga[:, 2:3], None, ALU.mult),
                r=[xtk, gk], w=["rs_f"])
            rint_ladder(b, "rs_f", lambda: b.rs_f[:, 0:D],
                        lambda xb=xb: xb[:, :], D, xbk, lo=-128, hi=127)
        else:
            p.add("vector", lambda e, xb=xb, xt=xt: e.tensor_copy(
                xb[:, :], xt[:, :]), r=[xtk], w=[xbk])
        transpose_tile(b, lambda c, xb=xb: xb[:, c * 128:(c + 1) * 128], xbk,
                       lambda xf=xf: xf[:, :], xfk, pi=i)
        if bit:
            if pend is not None:
                _rope_and_spill(b, seq, *pend)
            emit_proj(t, xf, xfk, qkvt, qk_, ga, gk)
        else:
            emit_proj(t, xf, xfk, qkvt, qk_, ga, gk)
            if pend is not None:
                _rope_and_spill(b, seq, *pend)
        pend = (t, kind, qkvt, spill, nkv, rope, qk_, spk)
    _rope_and_spill(b, seq, *pend)
    p.barrier()
    stA.close()

    stB = contextlib.ExitStack()
    _attention(b, mk(stB), seq, kind, causal, gqa, rel, scale)
    p.barrier()
    stB.close()

    dbg_tap(b, "attn", lambda: attn[:, 0:16384], ["attn"], 16384, cast=True)
    stC = contextlib.ExitStack()
    _out_proj(b, mk(stC), seq, kind, wo_nm, bo_nm, bit)
    p.barrier()
    stC.close()
    stx.close()


def _l0aa_quant_weights(b, sb, wsb, wsb_p, ascl):
    p, d = b.p, b.dram
    wf = b._cur_attn[:, 0:5120].bitcast(F32)
    aw = b._cur_attn[:, 5120:10240].bitcast(F32)
    alph = sb("alph", [128, 8], F32)
    colsum = sb("colsum", [128, 4], F32)
    tot = sb("tot", [1, 8], F32)
    segs = [(0, 1024, D * D), (1024, 1280, 256 * D),
            (1280, 1536, 256 * D), (1536, 2560, D * D)]

    def load_chunk(c):
        p.add("sync", lambda e, c=c: e.dma_start(
            wf[:, 0:1536], d["w_l0aa_qkv"][c * 128:(c + 1) * 128, :]),
            w=["WF"], dma=True)
        p.add("sync", lambda e, c=c: e.dma_start(
            wf[:, 1536:2560], d["w_l0aa_p"][c * 128:(c + 1) * 128, :]),
            w=["WF"], dma=True)

    p.add("vector", lambda e: e.memset(colsum[:, :], 0.0), w=["colsum"])
    for c in range(NC):
        load_chunk(c)
        p.add("scalar", lambda e: e.activation(aw[:, :], wf[:, :], AF.Abs,
                                               bias=0.0, scale=1.0),
              r=["WF"], w=["aw"])
        for i, (j0, j1, cnt) in enumerate(segs):
            p.add("vector", lambda e, i=i, j0=j0, j1=j1: e.reduce_sum(
                b.stat[:, 48 + i:49 + i], aw[:, j0:j1], AX.X),
                r=["aw"], w=["qs"])
        p.add("vector", lambda e: e.tensor_tensor(
            colsum[:, :], colsum[:, :], b.stat[:, 48:52], ALU.add),
            r=["colsum", "qs"], w=["colsum"])
    p.add("tensor", lambda e: e.matmul(
        b.pse[0][0:1, 0:4], b.ones_cf[:, 0:1], colsum[:, 0:4],
        start=True, stop=True), r=["colsum", "ones_cf", "PE0r"], w=["PE0"])
    p.add("scalar", lambda e: e.copy(tot[0:1, 0:4], b.pse[0][0:1, 0:4]),
          r=["PE0"], w=["tot", "PE0r"])
    for i, (j0, j1, cnt) in enumerate(segs):
        p.add("vector", lambda e, i=i, cnt=cnt: e.tensor_scalar(
            tot[0:1, 4 + i:5 + i], tot[0:1, i:i + 1], 1.0 / cnt, BITEPS,
            ALU.mult, ALU.max), r=["tot"], w=["tot"])
    p.add("vector", lambda e: e.reciprocal(tot[0:1, 0:4], tot[0:1, 4:8]),
          r=["tot"], w=["tot"])
    p.add("tensor", lambda e: e.matmul(
        b.pse[0][0:128, 0:8], b.ones_f1[0:1, 0:128], tot[0:1, 0:8],
        start=True, stop=True), r=["tot", "ones_f1", "PE0r"], w=["PE0"])
    p.add("scalar", lambda e: e.copy(alph[:, 0:8], b.pse[0][0:128, 0:8]),
          r=["PE0"], w=["alph", "PE0r"])
    p.add("scalar", lambda e: e.copy(ascl[:, 0:4], alph[:, 4:8]),
          r=["alph"], w=["ascl"])
    for c in range(NC):
        load_chunk(c)
        for i, (j0, j1, cnt) in enumerate(segs):
            p.add("vector", lambda e, i=i, j0=j0, j1=j1: e.tensor_scalar(
                aw[:, j0:j1], wf[:, j0:j1], alph[:, i:i + 1], None,
                ALU.mult), r=["WF", "alph"], w=["aw"])
        p.add("vector", lambda e: e.tensor_single_scalar(
            wf[:, :], aw[:, :], 0.5, ALU.is_ge), r=["aw"], w=["WF"])
        p.add("vector", lambda e: e.tensor_single_scalar(
            aw[:, :], aw[:, :], -0.5, ALU.is_le), r=["aw"], w=["aw"])
        p.add("vector", lambda e, c=c: e.tensor_tensor(
            wsb[:, c * 1536:(c + 1) * 1536], wf[:, 0:1536], aw[:, 0:1536],
            ALU.subtract), r=["WF", "aw"], w=["Wqkv"])
        p.add("vector", lambda e, c=c: e.tensor_tensor(
            wsb_p[:, c * D:(c + 1) * D], wf[:, 1536:2560], aw[:, 1536:2560],
            ALU.subtract), r=["WF", "aw"], w=["Wp"])


def _rope_and_spill(b, seq, t, kind, qkvt, spill, nkv, rope,
                    qk_="qkvt", spk="spill"):
    p, d = b.p, b.dram
    stt = b.stat
    if kind == "l0aa":
        q3 = lambda: qkvt[:, 0:D].rearrange("a (h e) -> a h e", h=16)
        k3 = lambda: qkvt[:, D:D + 256].rearrange("a (h e) -> a h e", h=4)
        p.add("vector", lambda e: e.scalar_tensor_tensor(
            b.sq_dump[:, 0:D].rearrange("a (h e) -> a h e", h=16),
            q3(), 1.0, q3(), ALU.mult, ALU.mult), r=[qk_], w=["sq_dump"])
        p.add("vector", lambda e: e.reduce_sum(
            stt[:, 16:32], b.sq_dump[:, 0:D].rearrange(
                "a (h e) -> a h e", h=16), AX.X), r=["sq_dump"], w=["rmsq"])
        p.add("scalar", lambda e: e.activation(
            stt[:, 16:32], stt[:, 16:32], AF.Sqrt,
            bias=b.c_rms_eps[:, 0:1], scale=1.0 / 64.0),
            r=["rmsq", "c_rms_eps"], w=["rmsq"])
        p.add("vector", lambda e: e.reciprocal(stt[:, 16:32], stt[:, 16:32]),
              r=["rmsq"], w=["rmsq"])
        p.add("vector", lambda e: e.tensor_tensor(
            stt[:, 16:32], stt[:, 16:32], b.gainb[:, 0:16], ALU.mult),
            r=["rmsq", "gainb"], w=["rmsq"])
        p.add("vector", lambda e: e.tensor_scalar(
            stt[:, 16:32], stt[:, 16:32], 0.125, None, ALU.mult),
            r=["rmsq"], w=["rmsq"])
        p.add("vector", lambda e: e.tensor_tensor(
            q3(), q3(), stt[:, 16:32].unsqueeze(2).broadcast_to(
                [128, 16, 64]), ALU.mult), r=[qk_, "rmsq"], w=[qk_])
        p.add("vector", lambda e: e.scalar_tensor_tensor(
            b.sq_dump[:, 0:256].rearrange("a (h e) -> a h e", h=4),
            k3(), 1.0, k3(), ALU.mult, ALU.mult), r=[qk_], w=["sq_dump"])
        p.add("vector", lambda e: e.reduce_sum(
            stt[:, 32:36], b.sq_dump[:, 0:256].rearrange(
                "a (h e) -> a h e", h=4), AX.X), r=["sq_dump"], w=["rmsk"])
        p.add("scalar", lambda e: e.activation(
            stt[:, 32:36], stt[:, 32:36], AF.Sqrt,
            bias=b.c_rms_eps[:, 0:1], scale=1.0 / 64.0),
            r=["rmsk", "c_rms_eps"], w=["rmsk"])
        p.add("vector", lambda e: e.reciprocal(stt[:, 32:36], stt[:, 32:36]),
              r=["rmsk"], w=["rmsk"])
        p.add("vector", lambda e: e.tensor_tensor(
            k3(), k3(), stt[:, 32:36].unsqueeze(2).broadcast_to(
                [128, 4, 64]), ALU.mult), r=[qk_, "rmsk"], w=[qk_])

    if rope is not None:
        cosb = lambda: b.cosc[:, t * 32:(t + 1) * 32]
        sinb = lambda: b.sinc[:, t * 32:(t + 1) * 32]
        for off, nh in [(0, NHEAD), (D, nkv)]:
            h3 = lambda off=off, nh=nh: qkvt[:, off:off + nh * 64].rearrange(
                "a (h e) -> a h e", h=nh)
            cb = lambda nh=nh: cosb().unsqueeze(1).broadcast_to([128, nh, 32])
            sbb = lambda nh=nh: sinb().unsqueeze(1).broadcast_to([128, nh, 32])
            t1 = lambda nh=nh: b.rs_f[:, 0:nh * 32].rearrange(
                "a (h e) -> a h e", h=nh)
            t2 = lambda nh=nh: b.rs_e[:, 0:nh * 32].rearrange(
                "a (h e) -> a h e", h=nh)
            t3 = lambda nh=nh: b.rs_c[:, 0:nh * 32].rearrange(
                "a (h e) -> a h e", h=nh)
            if rope in ("neg_half", "pos_half"):
                x1 = lambda h3=h3: h3()[:, :, 0:32]
                x2 = lambda h3=h3: h3()[:, :, 32:64]
                sgn = 1.0 if rope == "neg_half" else -1.0
                p.add("vector", lambda e, x1=x1, cb=cb, t1=t1: e.tensor_tensor(
                    t1(), x1(), cb(), ALU.mult), r=[qk_, "cosc"], w=["rt1"])
                p.add("vector", lambda e, x2=x2, sbb=sbb, t2=t2:
                      e.tensor_tensor(t2(), x2(), sbb(), ALU.mult),
                      r=[qk_, "sinc"], w=["rt2"])
                p.add("vector", lambda e, x1=x1, sbb=sbb, t3=t3:
                      e.tensor_tensor(t3(), x1(), sbb(), ALU.mult),
                      r=[qk_, "sinc"], w=["rt3"])
                p.add("vector", lambda e, x2=x2, cb=cb: e.tensor_tensor(
                    x2(), x2(), cb(), ALU.mult), r=[qk_, "cosc"],
                    w=[qk_])
                p.add("vector", lambda e, x2=x2, t3=t3, sgn=sgn:
                      e.scalar_tensor_tensor(x2(), t3(), -sgn, x2(),
                                             ALU.mult, ALU.add),
                      r=["rt3", qk_], w=[qk_])
                p.add("vector", lambda e, x1=x1, t1=t1, t2=t2, sgn=sgn:
                      e.scalar_tensor_tensor(x1(), t2(), sgn, t1(),
                                             ALU.mult, ALU.add),
                      r=["rt1", "rt2"], w=[qk_])
            else:
                ev = lambda h3=h3: h3().rearrange(
                    "a h (e two) -> a h e two", two=2)[:, :, :, 0:1]
                od = lambda h3=h3: h3().rearrange(
                    "a h (e two) -> a h e two", two=2)[:, :, :, 1:2]
                cb4 = lambda nh=nh: cosb().unsqueeze(1).unsqueeze(
                    3).broadcast_to([128, nh, 32, 1])
                sb4 = lambda nh=nh: sinb().unsqueeze(1).unsqueeze(
                    3).broadcast_to([128, nh, 32, 1])
                t14 = lambda t1=t1: t1().unsqueeze(3)
                t24 = lambda t2=t2: t2().unsqueeze(3)
                t34 = lambda t3=t3: t3().unsqueeze(3)
                p.add("vector", lambda e, ev=ev, cb4=cb4, t14=t14:
                      e.tensor_tensor(t14(), ev(), cb4(), ALU.mult),
                      r=[qk_, "cosc"], w=["rt1"])
                p.add("vector", lambda e, od=od, sb4=sb4, t24=t24:
                      e.tensor_tensor(t24(), od(), sb4(), ALU.mult),
                      r=[qk_, "sinc"], w=["rt2"])
                p.add("vector", lambda e, ev=ev, sb4=sb4, t34=t34:
                      e.tensor_tensor(t34(), ev(), sb4(), ALU.mult),
                      r=[qk_, "sinc"], w=["rt3"])
                p.add("vector", lambda e, od=od, cb4=cb4: e.tensor_tensor(
                    od(), od(), cb4(), ALU.mult), r=[qk_, "cosc"],
                    w=[qk_])
                p.add("vector", lambda e, od=od, t34=t34: e.tensor_tensor(
                    od(), od(), t34(), ALU.add), r=["rt3", qk_],
                    w=[qk_])
                p.add("vector", lambda e, ev=ev, t14=t14, t24=t24:
                      e.tensor_tensor(ev(), t14(), t24(), ALU.subtract),
                      r=["rt1", "rt2"], w=[qk_])

    def spill_seg(src_off, n_pairs, scr_nm):
        p.add("vector", lambda e: e.tensor_copy(
            spill[:, 0:n_pairs * 128],
            qkvt[:, src_off:src_off + n_pairs * 128]),
            r=[qk_], w=[spk])
        p.add("sync", lambda e, t=t, n_pairs=n_pairs, scr_nm=scr_nm:
              e.dma_start(
                  d[scr_nm][:, t * 128:(t + 1) * 128, :]
                  .transpose([1, 0, 2])[0:128, 0:n_pairs, 0:128],
                  spill[:, 0:n_pairs * 128].rearrange(
                      "a (c n) -> a c n", c=n_pairs)),
              r=[spk], w=[scr_nm], dma=True)
    spill_seg(0, NC, "q_scr")
    spill_seg(D, nkv // 2, "k_scr")
    spill_seg(D + nkv * 64, nkv // 2, "v_scr")


def _attention(b, sb, seq, kind, causal, gqa, rel, scale):
    p, d, nc = b.p, b.dram, b.nc
    attn = b._cur_attn
    nkv_ = 1 if (gqa or rel) else 2
    qp2 = [sb(f"qp{i}", [128, NT * 128], BF16) for i in range(nkv_)]
    kp2 = [sb(f"kp{i}", [128, NT * 128], BF16) for i in range(nkv_)]
    vp2 = [sb(f"vp{i}", [128, NT * 128], BF16) for i in range(nkv_)]
    qq = [sb(f"qq{i}", [128, S], BF16) for i in range(2)]
    kq = [sb(f"kq{i}", [128, S], BF16) for i in range(2)]
    va = [sb(f"va{i}", [128, NT * 65], BF16) for i in range(2)]
    expT = sb("expT", [128, NT * 512], BF16)
    mrow = sb("mrow", [128, 1], F32)
    mbf = sb("mbf", [128, 1], BF16)
    srow = sb("srow", [1, 512], F32)
    rb = sb("rb", [64, 512], F32)
    if not rel:
        ksqb = sb("ksqb", [64, S], BF16)
        kmx = sb("kmx", [64, 1], F32)
        ks1 = sb("ks1", [1, 4], F32)
    stg = sb("stg", [128, S], BF16)
    if rel:
        shs = sb("shs", [128, 512], F32)
        m8row = sb("m8row", [1, 2048], F32)
        m8b = sb("m8b", [128, 512], F32)
        bandt = sb("bandt", [128, 256], F32)

    for hp in range(NC):
        hpi = (hp % 2) % nkv_
        kvi = hpi
        qp, kp, vp = qp2[hpi], kp2[kvi], vp2[kvi]
        qpk, kpk, vpk = f"qp{hpi}", f"kp{kvi}", f"vp{kvi}"
        p.add("sync", lambda e, hp=hp, qp=qp: e.dma_start(
            qp[:, :].rearrange("a (t n) -> a t n", t=NT),
            d["q_scr"][hp].rearrange("(t a) n -> a t n", t=NT)),
            r=["q_scr", qpk], w=[qpk], dma=True)
        kvp = (hp // 2) // 2 if gqa else hp
        if not gqa or hp % 4 == 0:
            p.add("sync", lambda e, kvp=kvp, kp=kp: e.dma_start(
                kp[:, :].rearrange("a (t n) -> a t n", t=NT),
                d["k_scr"][kvp].rearrange("(t a) n -> a t n", t=NT)),
                r=["k_scr", kpk], w=[kpk], dma=True)
            p.add("sync", lambda e, kvp=kvp, vp=vp: e.dma_start(
                vp[:, :].rearrange("a (t n) -> a t n", t=NT),
                d["v_scr"][kvp].rearrange("(t a) n -> a t n", t=NT)),
                r=["v_scr", vpk], w=[vpk], dma=True)

        for src, bufs, srck, basek in ((qp, qq, qpk, "QQ"),
                                       (kp, kq, kpk, "KB")):
            for t in range(NT):
                ps = b.psb[0]
                p.add("tensor", lambda e, src=src, t=t, ps=ps: e.transpose(
                    ps[:, 0:128], src[:, t * 128:(t + 1) * 128],
                    b.ident_b[:, :]),
                    r=[srck, "ident_b", "PSB0r"], w=["PSB0"])
                p.add("scalar", lambda e, bufs=bufs, t=t, ps=ps: e.copy(
                    bufs[0][0:64, t * 128:(t + 1) * 128], ps[0:64, 0:128]),
                    r=["PSB0"], w=[basek + "0"])
                p.add("scalar", lambda e, t=t, ps=ps: e.copy(
                    stg[64:128, t * 128:(t + 1) * 128], ps[64:128, 0:128]),
                    r=["PSB0"], w=["stg", "PSB0r"])
            p.add("sync", lambda e: e.dma_start(
                d["shift_scr"][:, :], stg[64:128, :]),
                r=["stg"], w=["shift_scr"], dma=True)
            p.add("sync", lambda e, bufs=bufs, basek=basek: e.dma_start(
                bufs[1][0:64, :], d["shift_scr"][:, :]),
                r=["shift_scr"], w=[basek + "1"], dma=True)
        for i in range(2):
            p.add("vector", lambda e, i=i: e.memset(kq[i][64:65, :], -1.0),
                  w=[f"KB{i}"])
            p.add("vector", lambda e, i=i, vp=vp: e.tensor_copy(
                va[i][:, :].rearrange("a (t n) -> a t n", n=65)[:, :, 0:64],
                vp[:, :].rearrange("a (t n) -> a t n", n=128)
                [:, :, i * 64:(i + 1) * 64]),
                r=[vpk], w=[f"VA{i}"])
            p.add("vector", lambda e, i=i: e.memset(
                va[i][:, :].rearrange("a (t n) -> a t n", n=65)[:, :, 64:65],
                1.0), w=[f"VA{i}"])

        if not rel:
            for i in range(2):
                p.add("vector", lambda e, i=i: e.tensor_tensor(
                    ksqb[0:64, :], kq[i][0:64, :], kq[i][0:64, :], ALU.mult),
                    r=[f"KB{i}", "ksqb"], w=["ksqb"])
                for kb in range(4):
                    p.add("tensor", lambda e, kb=kb: e.matmul(
                        b.PS4[0:64, kb * 512:(kb + 1) * 512],
                        b.ones_b64[0:64, 0:64],
                        ksqb[0:64, kb * 512:(kb + 1) * 512],
                        start=True, stop=True),
                        r=["ksqb", "ones_b64", f"PQ{kb}r"], w=[f"PQ{kb}"])
                p.add("vector", lambda e: e.reduce_max(
                    kmx[0:64, 0:1], b.PS4[0:64, 0:S], AX.X),
                    r=[f"PQ{kb}" for kb in range(4)],
                    w=["kmx"] + [f"PQ{kb}r" for kb in range(4)])
                p.add("tensor", lambda e: e.transpose(
                    b.pse[0][0:1, 0:64], kmx[0:64, 0:1],
                    b.ident_f[0:64, 0:64]),
                    r=["kmx", "ident_f", "PE0r"], w=["PE0"])
                p.add("vector", lambda e, i=i: e.reduce_max(
                    ks1[0:1, i:i + 1], b.pse[0][0:1, 0:64], AX.X),
                    r=["PE0", "ks1"], w=["ks1", "PE0r"])
            for hi in range(2):
                ki_ = (((2 * hp + hi) // 4) % 2) if gqa else hi
                QQ, qqk = qq[hi], f"QQ{hi}"
                p.add("vector", lambda e, QQ=QQ, qqk=qqk: e.tensor_tensor(
                    ksqb[0:64, :], QQ[0:64, :], QQ[0:64, :], ALU.mult),
                    r=[qqk, "ksqb"], w=["ksqb"])
                for kb in range(4):
                    p.add("tensor", lambda e, kb=kb: e.matmul(
                        b.pse[1][0:1, 0:512], b.ones_colb[0:64, 0:1],
                        ksqb[0:64, kb * 512:(kb + 1) * 512],
                        start=True, stop=True),
                        r=["ksqb", "ones_colb", "PE1r"], w=["PE1"])
                    p.add("scalar", lambda e, kb=kb, QQ=QQ, ki_=ki_:
                          e.activation(
                              QQ[64:65, kb * 512:(kb + 1) * 512],
                              b.pse[1][0:1, 0:512], AF.Sqrt,
                              bias=0.0, scale=ks1[0:1, ki_:ki_ + 1]),
                          r=["PE1", "ks1"], w=[qqk + "m", "PE1r"])
        for hi in range(2):
            h = 2 * hp + hi
            ki_ = ((h // 4) % 2) if gqa else hi
            QQ, KQB, VAB = qq[hi], kq[ki_], va[ki_]
            qqk, kqk, vak = f"QQ{hi}", f"KB{ki_}", f"VA{ki_}"
            if rel:
                for qt in range(NT):
                    nk = (qt + 1) * 128
                    nkb = (nk + 511) // 512
                    for kb in range(nkb):
                        w_ = min(512, nk - kb * 512)
                        p.add("tensor",
                              lambda e, qt=qt, kb=kb, w_=w_, QQ=QQ, KQB=KQB:
                              e.matmul(b.PS4[:, kb * 512: kb * 512 + w_],
                                       QQ[0:64, qt * 128:(qt + 1) * 128],
                                       KQB[0:64, kb * 512: kb * 512 + w_],
                                       start=True, stop=True),
                              r=[qqk, kqk, "PQr"], w=[f"PQ{kb}"], tag="qk")
                    _l1ab_maxpass(b, qt, mrow, bandt)
                    p.add("vector", lambda e: e.tensor_scalar(
                        mbf[:, 0:1], mrow[:, 0:1], 1.0, None, ALU.mult),
                        r=["mrow"], w=["mbf"])
                    ps = b.psb[0]
                    p.add("tensor", lambda e, ps=ps: e.transpose(
                        ps[0:1, 0:128], mbf[:, 0:1], b.ident_b[:, :]),
                        r=["mbf", "ident_b", "PSB0r"], w=["PSB0"])
                    p.add("scalar", lambda e, qt=qt, ps=ps: e.copy(
                        m8row[0:1, qt * 128:(qt + 1) * 128], ps[0:1, 0:128]),
                        r=["PSB0"], w=["m8row", "PSB0r"])

            for qg in range(4):
                q0t = qg * 4
                kt_hi = q0t + 4 if causal else NT
                if rel:
                    p.add("tensor", lambda e, qg=qg: e.matmul(
                        b.pse[0][:, :], b.ones_f1[0:1, 0:128],
                        m8row[0:1, qg * 512:(qg + 1) * 512],
                        start=True, stop=True),
                        r=["m8row", "ones_f1", "PE0r"], w=["PE0"])
                    p.add("scalar", lambda e: e.copy(m8b[:, :],
                                                     b.pse[0][:, :]),
                          r=["PE0"], w=["m8b", "PE0r"])
                for kt in range(kt_hi):
                    ps, pk = [(b.pse[0], "PE0"), (b.pse[1], "PE1"),
                              (PQ(b, 0), "PQ0")][kt % 3]
                    kcon = 64 if rel else 65
                    p.add("tensor",
                          lambda e, kt=kt, qg=qg, ps=ps, KQB=KQB, QQ=QQ,
                          kcon=kcon:
                          e.matmul(ps[:, :],
                                   KQB[0:kcon, kt * 128:(kt + 1) * 128],
                                   QQ[0:kcon, qg * 512:(qg + 1) * 512],
                                   start=True, stop=True),
                          r=[kqk, qqk, qqk + "m", pk + "r"], w=[pk],
                          tag="kq")
                    if rel:
                        _l1ab_expprep(b, kt, qg, q0t, ps, pk, shs, m8b)
                        p.add("scalar", lambda e, kt=kt: e.activation(
                            expT[:, kt * 512:(kt + 1) * 512], shs[:, :],
                            AF.Exp, bias=0.0, scale=1.0),
                            r=["shs"], w=["expT", pk + "r"])
                    else:
                        if causal:
                            for j in range(4):
                                if q0t + j == kt:
                                    p.add("vector",
                                          lambda e, j=j, ps=ps:
                                          e.tensor_tensor(
                                              ps[:, j * 128:(j + 1) * 128],
                                              ps[:, j * 128:(j + 1) * 128],
                                              b.cmask_t[:, :], ALU.add),
                                          r=[pk, "cmask_t"], w=[pk])
                            if kt > q0t:
                                nmask = min(kt - q0t, 4)
                                p.add("vector",
                                      lambda e, nmask=nmask, ps=ps:
                                      e.memset(ps[:, 0:nmask * 128],
                                               MASKNEG), w=[pk])
                        p.add("scalar", lambda e, kt=kt, ps=ps: e.activation(
                            expT[:, kt * 512:(kt + 1) * 512], ps[:, :],
                            AF.Exp, bias=0.0, scale=scale),
                            r=[pk], w=["expT", pk + "r"])
                nkt = kt_hi
                avc = 1024 if qg % 2 == 0 else 512
                avk = "PQ2" if qg % 2 == 0 else "PQ1"
                for kt in range(nkt):
                    p.add("tensor",
                          lambda e, kt=kt, VAB=VAB, nkt=nkt, avc=avc:
                          e.matmul(b.PS4[0:65, avc:avc + 512],
                                   VAB[:, kt * 65:(kt + 1) * 65],
                                   expT[:, kt * 512:(kt + 1) * 512],
                                   start=(kt == 0), stop=(kt == nkt - 1)),
                          r=[vak, "expT", avk + "r"] if kt == 0 else
                          [vak, "expT"], w=[avk], tag="av")
                p.add("scalar", lambda e, avc=avc: e.copy(
                    srow[0:1, :], b.PS4[64:65, avc:avc + 512]),
                    r=[avk], w=["srow"])
                p.add("tensor", lambda e: e.matmul(
                    b.PS4[0:64, 1536:2048], b.ones_f1[0:1, 0:64],
                    srow[0:1, :], start=True, stop=True),
                    r=["srow", "ones_f1", "PQ3r"], w=["PQ3"])
                p.add("vector", lambda e: e.reciprocal(
                    rb[:, :], b.PS4[0:64, 1536:2048]),
                    r=["PQ3"], w=["rb", "PQ3r"])
                p.add("vector", lambda e, hi=hi, hp=hp, qg=qg, avc=avc:
                      e.tensor_tensor(
                          attn[hi * 64:(hi + 1) * 64,
                               hp * S + qg * 512: hp * S + (qg + 1) * 512],
                          b.PS4[0:64, avc:avc + 512], rb[:, :], ALU.mult),
                      r=[avk, "rb"], w=["attn", avk + "r"])


def _l1ab_maxpass(b, qt, mrow, bandt):
    p = b.p
    sc = b.PS4
    p.add("vector", lambda e, qt=qt: e.tensor_tensor(
        bandt[:, 0:128], sc[:, qt * 128:(qt + 1) * 128], b.d0_8[:, :],
        ALU.mult), r=[f"PQ{qt // 4}", "Dind0"], w=["bandt"])
    p.add("vector", lambda e: e.tensor_tensor(
        bandt[:, 0:128], bandt[:, 0:128], b.cmask[:, :], ALU.add),
        r=["bandt", "cmask"], w=["bandt"])
    nb = 128
    if qt > 0:
        p.add("vector", lambda e, qt=qt: e.tensor_tensor(
            bandt[:, 128:256], sc[:, (qt - 1) * 128: qt * 128],
            b.d128_8[:, :], ALU.mult),
            r=[f"PQ{(qt - 1) // 4}", "Dind128"], w=["bandt"])
        nb = 256
    p.add("vector", lambda e, nb=nb: e.reduce_max(
        mrow[:, 0:1], bandt[:, 0:nb], AX.X), r=["bandt"], w=["mrow"])
    nkb = (qt * 128 + 128 + 511) // 512
    p.add("vector", lambda e, qt=qt, nkb=nkb: e.reduce_max(
        b.stat[:, 40:41], sc[:, 0:(qt + 1) * 128], AX.X,
        apply_absolute_value=True),
        r=[f"PQ{kb}" for kb in range(nkb)], w=["obmax", "PQr"])
    p.add("vector", lambda e: e.tensor_scalar(
        b.stat[:, 41:42], b.stat[:, 40:41], b.absa16_8[:, 0:1], None,
        ALU.mult), r=["obmax", "absa16_8"], w=["obmax2"])
    p.add("vector", lambda e: e.tensor_tensor(
        mrow[:, 0:1], mrow[:, 0:1], b.stat[:, 41:42], ALU.max),
        r=["mrow", "obmax2"], w=["mrow"])


def _l1ab_expprep(b, kt, qg, q0t, ps, pk, shs, m8b):
    p = b.p
    p.add("vector", lambda e, ps=ps: e.scalar_tensor_tensor(
        shs[:, :], ps[:, :], b.a16_8[:, 0:1], m8b[:, :],
        ALU.mult, ALU.subtract), r=[pk, "a16_8", "m8b"], w=["shs"])
    for j in range(4):
        qt = q0t + j
        if qt == kt:
            p.add("vector", lambda e, j=j, ps=ps: e.tensor_tensor(
                shs[:, j * 128:(j + 1) * 128], ps[:, j * 128:(j + 1) * 128],
                b.d0t_8[:, :], ALU.mult), r=[pk, "Dt0"], w=["shs"])
            p.add("vector", lambda e, j=j: e.tensor_tensor(
                shs[:, j * 128:(j + 1) * 128],
                shs[:, j * 128:(j + 1) * 128],
                b.cmask_t[:, :], ALU.add), r=["shs", "cmask_t"], w=["shs"])
            p.add("vector", lambda e, j=j: e.tensor_tensor(
                shs[:, j * 128:(j + 1) * 128],
                shs[:, j * 128:(j + 1) * 128],
                m8b[:, j * 128:(j + 1) * 128], ALU.subtract),
                r=["shs", "m8b"], w=["shs"])
        elif qt == kt + 1:
            p.add("vector", lambda e, j=j, ps=ps: e.tensor_tensor(
                shs[:, j * 128:(j + 1) * 128], ps[:, j * 128:(j + 1) * 128],
                b.d128t_8[:, :], ALU.mult), r=[pk, "Dt128"], w=["shs"])
            p.add("vector", lambda e, j=j: e.tensor_tensor(
                shs[:, j * 128:(j + 1) * 128],
                shs[:, j * 128:(j + 1) * 128],
                m8b[:, j * 128:(j + 1) * 128], ALU.subtract),
                r=["shs", "m8b"], w=["shs"])
        elif qt < kt:
            p.add("vector", lambda e, j=j: e.memset(
                shs[:, j * 128:(j + 1) * 128], MASKNEG), w=["shs"])


def _out_proj(b, sb, seq, kind, wo_nm, bo_nm, bit):
    p, d = b.p, b.dram
    attn = b._cur_attn
    if bit:
        _l0aa_outproj(b, sb, seq)
        return
    wo = sb("wo", [128, NC * D], BF16)
    bo_r = sb("bor", [1, D], BF16) if bo_nm else None
    if bo_nm:
        p.add("gpsimd", lambda e: e.dma_start(bo_r[:, :], d[bo_nm]),
              w=["bor"], dma=True)
    p.add("gpsimd", lambda e: e.dma_start(
        wo[:, :].rearrange("a (c n) -> a c n", c=NC),
        d[wo_nm].rearrange("(c a) n -> a c n", c=NC)),
        w=["Wo"], dma=True)
    for t in range(NT):
        for half in range(2):
            ps = b.pse[half]
            pk = f"PE{half}"
            for c in range(NC):
                p.add("tensor", lambda e, t=t, c=c, half=half, ps=ps:
                      e.matmul(
                          ps[:, :],
                          attn[:, c * S + t * 128:c * S + (t + 1) * 128],
                          wo[:, c * D + half * 512: c * D + half * 512 + 512],
                          start=(c == 0),
                          stop=(c == NC - 1 and bo_r is None)),
                      r=["attn", "Wo", pk + "r"] if c == 0 else
                      ["attn", "Wo"], w=[pk], tag="op")
            if bo_r is not None:
                p.add("tensor", lambda e, half=half, ps=ps: e.matmul(
                    ps[:, :], b.ones_b1[0:1, 0:128],
                    bo_r[0:1, half * 512: half * 512 + 512],
                    start=False, stop=True),
                    r=["bor", "ones_b1", pk], w=[pk])
            rk = f"R.{t}"
            p.add("vector", lambda e, t=t, half=half, ps=ps: e.tensor_tensor(
                b.R[:, t * D + half * 512: t * D + half * 512 + 512],
                b.R[:, t * D + half * 512: t * D + half * 512 + 512],
                ps[:, :], ALU.add), r=[pk, rk], w=[rk, pk + "r"])


def _l0aa_outproj(b, sb, seq):
    p = b.p
    attn = b._cur_attn
    yt = sb("yt", [128, D], F32)
    yb = sb("yb", [128, D], BF16)
    yf = sb("yf", [128, NC * 128], BF16)
    gscl = b._cur_gscl
    wsb_p = b._cur_wp
    ascl = b._cur_ascl
    for t in range(NT):
        ps = b.psb[0]
        for c in range(NC):
            p.add("tensor", lambda e, c=c, t=t, ps=ps: e.transpose(
                ps[:, c * 128:(c + 1) * 128],
                attn[:, c * S + t * 128: c * S + (t + 1) * 128],
                b.ident_b[:, :]), r=["attn", "ident_b", "PSB0r"],
                w=["PSB0"], tag="tb")
        p.add("scalar", lambda e, ps=ps: e.copy(yt[:, :], ps[:, 0:D]),
              r=["PSB0"], w=["yt", "PSB0r"])
        p.add("vector", lambda e: e.reduce_max(
            gscl[:, 0:1], yt[:, :], AX.X, apply_absolute_value=True),
            r=["yt"], w=["g0"])
        p.add("vector", lambda e: e.tensor_scalar(
            gscl[:, 1:2], gscl[:, 0:1], BITEPS, 1.0 / 127.0,
            ALU.max, ALU.mult), r=["g0"], w=["g1"])
        p.add("vector", lambda e: e.reciprocal(gscl[:, 2:3], gscl[:, 1:2]),
              r=["g1"], w=["g2"])
        p.add("vector", lambda e: e.tensor_scalar(
            b.rs_f[:, 0:D], yt[:, :], gscl[:, 2:3], None, ALU.mult),
            r=["yt", "g2"], w=["rs_f"])
        rint_ladder(b, "rs_f", lambda: b.rs_f[:, 0:D],
                    lambda: yb[:, :], D, "yb", lo=-128, hi=127)
        transpose_tile(b, lambda c: yb[:, c * 128:(c + 1) * 128], "yb",
                       lambda: yf[:, :], "yf")
        p.add("vector", lambda e: e.tensor_scalar(
            gscl[:, 3:4], gscl[:, 1:2], ascl[:, 3:4], None, ALU.mult),
            r=["g1", "ascl"], w=["g3"])
        for half in range(2):
            ps2 = b.pse[half]
            pk = f"PE{half}"
            for c in range(NC):
                p.add("tensor", lambda e, c=c, half=half, ps2=ps2: e.matmul(
                    ps2[:, :], yf[:, c * 128:(c + 1) * 128],
                    wsb_p[:, c * D + half * 512: c * D + half * 512 + 512],
                    start=(c == 0), stop=(c == NC - 1)),
                    r=["yf", "Wp", pk + "r"] if c == 0 else ["yf", "Wp"],
                    w=[pk], tag="pp")
            rk = f"R.{t}"
            p.add("vector", lambda e, t=t, half=half, ps2=ps2:
                  e.scalar_tensor_tensor(
                      b.R[:, t * D + half * 512: t * D + half * 512 + 512],
                      ps2[:, :], gscl[:, 3:4],
                      b.R[:, t * D + half * 512: t * D + half * 512 + 512],
                      ALU.mult, ALU.add), r=[pk, "g3", rk],
                  w=[rk, pk + "r"])



def ffn_quant_stage(b, seq, stg_i):
    p, d, nc = b.p, b.dram, b.nc
    stx = contextlib.ExitStack()
    mk = lambda stk: (lambda nm, shp, dt: stk.enter_context(
        nc.sbuf_tensor(f"qf{stg_i}s{seq}_{nm}", shp, dt)))
    sb = mk(stx)
    xq = sb("xq", [128, NC * S], BF16)
    bq16x = sb("bq16x", [128, 32], F32)
    bq2r = sb("bq2r", [1, D], F32)
    bq2b = sb("bq2b", [1, D], BF16)

    st0 = contextlib.ExitStack()
    sb0 = mk(st0)
    xt2 = [sb0(f"xt{i}", [128, D], F32) for i in range(2)]
    xb2 = [sb0(f"xb{i}", [128, D], BF16) for i in range(2)]
    for t in range(NT):
        i = t % 2
        xt, xb = xt2[i], xb2[i]
        xtk, xbk = f"xt{i}", f"xb{i}"
        ln_tile(b, t, lambda xt=xt: xt[:, :], xtk)
        p.add("vector", lambda e, xt=xt: e.tensor_scalar(
            b.rs_f[:, 0:D], xt[:, :], 16.0, None, ALU.mult),
            r=[xtk], w=["rs_f"])
        rint_ladder(b, "rs_f", lambda: b.rs_f[:, 0:D],
                    lambda xb=xb: xb[:, :], D, xbk, lo=-128, hi=127)
        transpose_tile(b, lambda c, xb=xb: xb[:, c * 128:(c + 1) * 128], xbk,
                       lambda t=t: xq[:, :].rearrange(
                           "a (c n) -> a c n", c=NC)[:, :, t * 128:(t + 1) * 128],
                       "XQ", pi=i)
    b1s = sb0("b1s", [128, 32], F32)
    p.add("sync", lambda e: e.dma_start(b1s[:, :], d["b_l1ffn_1"]),
          w=["b1s"], dma=True)
    p.add("vector", lambda e: e.tensor_scalar(
        b.rs_f[:, 0:32], b1s[:, :], 64.0, None, ALU.mult),
        r=["b1s"], w=["rs_f"])
    p.add("vector", lambda e: e.tensor_copy(b.rs_i[:, 0:32], b.rs_f[:, 0:32]),
          r=["rs_f"], w=["rs_i"])
    p.add("vector", lambda e: e.tensor_scalar(
        bq16x[:, :], b.rs_i[:, 0:32], 0.25, None, ALU.mult),
        r=["rs_i"], w=["bq16x"])
    p.add("sync", lambda e: e.dma_start(bq2r[:, :], d["b_l1ffn_2"]),
          w=["bq2r"], dma=True)
    p.add("vector", lambda e: e.tensor_scalar(
        bq2r[:, :], bq2r[:, :], 64.0, None, ALU.mult), r=["bq2r"], w=["bq2r"])
    p.add("vector", lambda e: e.tensor_copy(b.rs_i[0:1, 0:D], bq2r[:, :]),
          r=["bq2r"], w=["rs_i"])
    p.add("vector", lambda e: e.tensor_scalar(
        bq2b[:, :], b.rs_i[0:1, 0:D], 16.0, None, ALU.mult),
        r=["rs_i"], w=["bq2b"])
    p.barrier()
    st0.close()

    for qr in range(4):
        st1 = contextlib.ExitStack()
        sb1 = mk(st1)
        w1q = sb1(f"w1q{qr}", [128, NC * 1024], BF16)
        w2q = sb1(f"w2q{qr}", [128, 8 * D], BF16)
        wst2 = [sb1(f"wst{qr}{i}", [128, 1024], F32) for i in range(2)]
        hq = sb1(f"hq{qr}", [128, 8 * 512], BF16)
        tq2 = [sb1(f"tq{qr}{i}", [128, 512], F32) for i in range(2)]

        def quantize_block(wi, dst_ap_fn, dk, nn=1024):
            wk = f"wstq{wi}"
            wst = wst2[wi]
            p.add("vector", lambda e, nn=nn, wst=wst: e.tensor_scalar(
                b.rs_f[:, 0:nn], wst[:, 0:nn], 64.0, None, ALU.mult),
                r=[wk], w=["rs_f"])
            p.add("vector", lambda e, nn=nn: e.tensor_copy(
                b.rs_i[:, 0:nn], b.rs_f[:, 0:nn]), r=["rs_f"], w=["rs_i"])
            p.add("vector", lambda e, nn=nn: e.tensor_copy(
                b.rs_f[:, 0:nn], b.rs_i[:, 0:nn]), r=["rs_i"], w=["rs_f"])
            p.add("vector", lambda e, nn=nn: e.tensor_scalar(
                dst_ap_fn(), b.rs_f[:, 0:nn], -128.0, 127.0,
                ALU.max, ALU.min), r=["rs_f"], w=[dk])

        for c in range(NC):
            wi = c % 2
            p.add("sync", lambda e, c=c, qr=qr, wi=wi: e.dma_start(
                wst2[wi][:, :], d["w_l1ffn_1"][c * 128:(c + 1) * 128,
                                               qr * 1024:(qr + 1) * 1024]),
                r=[f"wstq{wi}"], w=[f"wstq{wi}"], dma=True)
            quantize_block(wi, lambda c=c: w1q[:, c * 1024:(c + 1) * 1024],
                           "W1Q")
        for hc in range(8):
            wi = hc % 2
            p.add("sync", lambda e, hc=hc, qr=qr, wi=wi: e.dma_start(
                wst2[wi][:, :], d["w_l1ffn_2"][qr * 1024 + hc * 128:
                                               qr * 1024 + (hc + 1) * 128, :]),
                r=[f"wstq{wi}"], w=[f"wstq{wi}"], dma=True)
            quantize_block(wi, lambda hc=hc: w2q[:, hc * D:(hc + 1) * D],
                           "W2Q")

        for tg in range(4):
            for hc in range(8):
                ps = b.pse[hc % 2]
                pk = f"PE{hc % 2}"
                for fc in range(NC):
                    p.add("tensor",
                          lambda e, fc=fc, hc=hc, tg=tg, ps=ps: e.matmul(
                              ps[:, :],
                              w1q[:, fc * 1024 + hc * 128:
                                  fc * 1024 + hc * 128 + 128],
                              xq[:, fc * S + tg * 512: fc * S + tg * 512 + 512],
                              start=(fc == 0), stop=(fc == NC - 1)),
                          r=["W1Q", "XQ", pk + "r"] if fc == 0 else
                          ["W1Q", "XQ"], w=[pk], tag="m1")
                hcg = qr * 8 + hc
                tq = tq2[hc % 2]
                tk = f"tq{hc % 2}"
                p.add("scalar", lambda e, hcg=hcg, ps=ps, tq=tq: e.activation(
                    tq[:, :], ps[:, :], AF.Identity,
                    bias=bq16x[:, hcg:hcg + 1], scale=2.0 ** -6),
                    r=[pk, "bq16x", tk], w=[tk, pk + "r"])
                rint_ladder(b, tk, lambda tq=tq: tq[:, :],
                            lambda hc=hc: hq[:, hc * 512:(hc + 1) * 512],
                            512, "HQ", lo=0, hi=127)
            for ttl in range(4):
                tt = tg * 4 + ttl
                for dh in range(2):
                    pq = PQ(b, dh)
                    pk = f"PQ{dh}"
                    for hc in range(8):
                        p.add("tensor",
                              lambda e, hc=hc, ttl=ttl, dh=dh, pq=pq:
                              e.matmul(
                                  pq[:, :],
                                  hq[:, hc * 512 + ttl * 128:
                                     hc * 512 + ttl * 128 + 128],
                                  w2q[:, hc * D + dh * 512:
                                      hc * D + dh * 512 + 512],
                                  start=(hc == 0),
                                  stop=(hc == 7 and qr != 0)),
                              r=["HQ", "W2Q", pk + "r"] if hc == 0 else
                              ["HQ", "W2Q"], w=[pk], tag="m2")
                    if qr == 0:
                        p.add("tensor", lambda e, dh=dh, pq=pq: e.matmul(
                            pq[:, :], b.ones_b1[0:1, 0:128],
                            bq2b[0:1, dh * 512: dh * 512 + 512],
                            start=False, stop=True),
                            r=["bq2b", "ones_b1", pk], w=[pk])
                    rk = f"R.{tt}"
                    p.add("vector", lambda e, tt=tt, dh=dh, pq=pq:
                          e.scalar_tensor_tensor(
                              b.R[:, tt * D + dh * 512: tt * D + dh * 512 + 512],
                              pq[:, :], 2.0 ** -10,
                              b.R[:, tt * D + dh * 512: tt * D + dh * 512 + 512],
                              ALU.mult, ALU.add),
                          r=[pk, rk], w=[rk, pk + "r"])
        p.barrier()
        st1.close()
    p.barrier()
    stx.close()


def ffn_fff_stage(b, seq, stg_i):
    p, d, nc = b.p, b.dram, b.nc
    stx = contextlib.ExitStack()
    mk = lambda stk: (lambda nm, shp, dt: stk.enter_context(
        nc.sbuf_tensor(f"fff{stg_i}s{seq}_{nm}", shp, dt)))
    sb = mk(stx)
    qsqr = sb("qsqr", [1, S], BF16)
    ksqr = sb("ksqr", [1, S], BF16)
    ks1f = sb("ks1f", [1, 1], F32)
    stRF = contextlib.ExitStack()
    sbRF = mk(stRF)
    xf = sbRF("xf", [128, NC * S], BF16)
    scb = sbRF("scb", [128, 45 * NT], F32)
    leafs = sbRF("leafs", [128, 3 * NT], F32)

    st0 = contextlib.ExitStack()
    sb0 = mk(st0)
    xt2 = [sb0(f"xt{i}", [128, D], F32) for i in range(2)]
    xb2 = [sb0(f"xb{i}", [128, D], BF16) for i in range(2)]
    xff2 = [sb0(f"xff{i}", [128, D], F32) for i in range(2)]
    wn = sb0("wn", [128, NC * 45], F32)
    nbr = sb0("nbr", [1, 45], F32)
    selw = sb0("selw", [128, 10 * NT], F32)
    p.add("gpsimd", lambda e: e.dma_start(
        wn[:, :].rearrange("a (c n) -> a c n", c=NC),
        d["w_ffn_node"].rearrange("(c a) n -> a c n", c=NC)),
        w=["WN"], dma=True)
    p.add("gpsimd", lambda e: e.dma_start(nbr[:, :], d["b_ffn_node"]),
          w=["NB"], dma=True)
    for t in range(NT):
        i = t % 2
        xt, xb, xff = xt2[i], xb2[i], xff2[i]
        xtk, xbk, xffk = f"xt{i}", f"xb{i}", f"xff{i}"
        pq0, pq1 = (0, 1) if i == 0 else (2, 3)
        psc = b.pse[i]
        pck = f"PE{i}"
        ln_tile(b, t, lambda xt=xt: xt[:, :], xtk)
        p.add("vector", lambda e, xb=xb, xt=xt: e.tensor_copy(
            xb[:, :], xt[:, :]), r=[xtk], w=[xbk])
        transpose_tile(b, lambda c, xb=xb: xb[:, c * 128:(c + 1) * 128], xbk,
                       lambda t=t: xf[:, :].rearrange(
                           "a (c n) -> a c n", c=NC)
                       [:, :, t * 128:(t + 1) * 128], "XF", pi=i)
        for c in range(NC):
            p.add("tensor", lambda e, c=c, xt=xt, i=i: e.transpose(
                b.PS4[:, i * 1024 + c * 128: i * 1024 + (c + 1) * 128],
                xt[:, c * 128:(c + 1) * 128], b.ident_f[:, :]),
                r=[xtk, "ident_f", f"PQ{pq0}r", f"PQ{pq1}r"] if c == 0 else
                [xtk, "ident_f"], w=[f"PQ{pq0 + c // 4}"], tag="ft")
        p.add("scalar", lambda e, xff=xff, i=i: e.copy(
            xff[:, :], b.PS4[:, i * 1024: i * 1024 + D]),
            r=[f"PQ{pq0}", f"PQ{pq1}"],
            w=[xffk, f"PQ{pq0}r", f"PQ{pq1}r"])
        for c in range(NC):
            p.add("tensor", lambda e, c=c, xff=xff, psc=psc: e.matmul(
                psc[:, 0:45], xff[:, c * 128:(c + 1) * 128],
                wn[:, c * 45:(c + 1) * 45], start=(c == 0), stop=False),
                r=[xffk, "WN", pck + "r"] if c == 0 else [xffk, "WN"],
                w=[pck], tag="sc")
        p.add("tensor", lambda e, psc=psc: e.matmul(
            psc[:, 0:45], b.ones_f1[0:1, 0:128], nbr[0:1, 0:45],
            start=False, stop=True), r=["NB", "ones_f1", pck], w=[pck])
        p.add("scalar", lambda e, t=t, psc=psc: e.copy(
            scb[:, :].rearrange("a (n t) -> a n t", t=NT)[:, :, t:t + 1],
            psc[:, 0:45].unsqueeze(2)), r=[pck], w=["scb", pck + "r"])
    dbg_tap(b, "scb", lambda: scb[:, 0:45 * NT], ["scb"], 45 * NT)

    Wv = lambda j: selw[:, j * NT:(j + 1) * NT]

    def NV(fam, i):
        o = 15 * fam + i
        return scb[:, o * NT:(o + 1) * NT]

    def vop(fn, rr, ww):
        p.add("vector", fn, r=rr, w=ww)

    def mux(dst_j, a_hi_fn, a_lo_fn, bit_j, rr):
        vop(lambda e: e.tensor_tensor(Wv(dst_j), a_hi_fn(), a_lo_fn(),
                                      ALU.subtract), rr + ["selw"], ["selw"])
        vop(lambda e: e.tensor_tensor(Wv(dst_j), Wv(dst_j), Wv(bit_j),
                                      ALU.mult), ["selw"], ["selw"])
        vop(lambda e: e.tensor_tensor(Wv(dst_j), Wv(dst_j), a_lo_fn(),
                                      ALU.add), rr + ["selw"], ["selw"])

    for fam in range(3):
        vop(lambda e, fam=fam: e.tensor_single_scalar(
            Wv(0), NV(fam, 0), 0.0, ALU.is_ge), ["scb", "selw"], ["selw"])
        mux(3, lambda fam=fam: NV(fam, 2), lambda fam=fam: NV(fam, 1), 0,
            ["scb"])
        vop(lambda e: e.tensor_single_scalar(Wv(1), Wv(3), 0.0, ALU.is_ge),
            ["selw"], ["selw"])
        mux(4, lambda fam=fam: NV(fam, 4), lambda fam=fam: NV(fam, 3), 1,
            ["scb"])
        mux(5, lambda fam=fam: NV(fam, 6), lambda fam=fam: NV(fam, 5), 1,
            ["scb"])
        mux(3, lambda: Wv(5), lambda: Wv(4), 0, [])
        vop(lambda e: e.tensor_single_scalar(Wv(2), Wv(3), 0.0, ALU.is_ge),
            ["selw"], ["selw"])
        mux(4, lambda fam=fam: NV(fam, 8), lambda fam=fam: NV(fam, 7), 2,
            ["scb"])
        mux(5, lambda fam=fam: NV(fam, 10), lambda fam=fam: NV(fam, 9), 2,
            ["scb"])
        mux(6, lambda fam=fam: NV(fam, 12), lambda fam=fam: NV(fam, 11), 2,
            ["scb"])
        mux(7, lambda fam=fam: NV(fam, 14), lambda fam=fam: NV(fam, 13), 2,
            ["scb"])
        mux(8, lambda: Wv(5), lambda: Wv(4), 1, [])
        mux(9, lambda: Wv(7), lambda: Wv(6), 1, [])
        mux(3, lambda: Wv(9), lambda: Wv(8), 0, [])
        vop(lambda e: e.tensor_single_scalar(Wv(4), Wv(3), 0.0, ALU.is_ge),
            ["selw"], ["selw"])
        vop(lambda e: e.tensor_scalar(Wv(5), Wv(0), 8.0, None, ALU.mult),
            ["selw"], ["selw"])
        vop(lambda e: e.scalar_tensor_tensor(Wv(5), Wv(1), 4.0, Wv(5),
                                             ALU.mult, ALU.add),
            ["selw"], ["selw"])
        vop(lambda e: e.scalar_tensor_tensor(Wv(5), Wv(2), 2.0, Wv(5),
                                             ALU.mult, ALU.add),
            ["selw"], ["selw"])
        vop(lambda e, fam=fam: e.tensor_tensor(
            leafs[:, fam * NT:(fam + 1) * NT], Wv(5), Wv(4), ALU.add),
            ["selw"], ["leafs"])
    dbg_tap(b, "leafs", lambda: leafs[:, 0:3 * NT], ["leafs"], 3 * NT)
    p.barrier()
    st0.close()

    stF = contextlib.ExitStack()
    sbF = mk(stF)
    selT = sbF("selT", [16, S], BF16)
    selm1 = sbF("selm1", [16, S], BF16)
    seltb = sbF("seltb", [128, 16], BF16)
    hfm = sbF("hfm", [128, NC * S], BF16)
    w1hb2 = [sbF(f"w1hb{i}", [128, NC * 128], BF16) for i in range(2)]
    w2cb2 = [sbF(f"w2cb{i}", [128, NC * 128], BF16) for i in range(2)]
    wst2 = [sbF(f"wst{i}", [128, D], F32) for i in range(2)]
    b2s = sbF("b2s", [16, D], BF16)
    b1rb = sbF("b1rb", [1, D], BF16)
    oo = [sbF(f"oo{i}", [128, 512], BF16) for i in range(2)]
    oq = [sbF(f"oq{i}", [128, 512], BF16) for i in range(2)]
    vtk = sbF("vtk", [128, 512], BF16)

    for fam in (2, 1, 0):
        pp = "qkv"[fam]
        scr_nm = {0: "qf_scr", 1: "kf_scr"}.get(fam)
        for t in range(NT):
            p.add("vector", lambda e, fam=fam, t=t: e.tensor_tensor(
                seltb[:, :],
                leafs[:, fam * NT + t: fam * NT + t + 1]
                .broadcast_to([128, 16]), b.lvconst[:, 0:16], ALU.is_equal),
                r=["leafs", "lvconst", "seltb"], w=["seltb"])
            p.add("tensor", lambda e: e.transpose(
                b.psb[0][0:16, 0:128], seltb[:, :], b.ident_b[:, :]),
                r=["seltb", "ident_b", "PSB0r"], w=["PSB0"])
            p.add("scalar", lambda e, t=t: e.copy(
                selT[0:16, t * 128:(t + 1) * 128], b.psb[0][0:16, 0:128]),
                r=["PSB0"], w=["selT", "PSB0r"])
        p.add("vector", lambda e: e.tensor_scalar(
            selm1[0:16, :], selT[0:16, :], -1.0, 1.0e6,
            ALU.add, ALU.mult), r=["selT", "selm1"], w=["selm1"])
        p.add("gpsimd", lambda e, pp=pp: e.dma_start(
            wst2[0][0:16, :], d[f"b2_{pp}"]), r=["wstb0"], w=["wstb0"],
            dma=True)
        p.add("vector", lambda e: e.tensor_copy(b2s[:, :], wst2[0][0:16, :]),
              r=["wstb0", "b2s"], w=["b2s"])
        p.add("gpsimd", lambda e, pp=pp: e.dma_start(
            wst2[1][0:1, :], d[f"b1r_{pp}"]), r=["wstb1"], w=["wstb1"],
            dma=True)
        p.add("vector", lambda e: e.tensor_copy(b1rb[:, :], wst2[1][0:1, :]),
              r=["wstb1", "b1rb"], w=["b1rb"])
        for hb in range(NC):
            wi = hb % 2
            wst, wsk = wst2[wi], f"wstb{wi}"
            w1hb, w1k = w1hb2[wi], f"w1hb{wi}"
            p.add("gpsimd", lambda e, pp=pp, hb=hb, wst=wst: e.dma_start(
                wst[:, :].rearrange("a (c n) -> a c n", c=NC),
                d[f"w_ffn{pp}_1"][:, hb * 128:(hb + 1) * 128]
                .rearrange("(c a) n -> a c n", c=NC)),
                r=[wsk], w=[wsk], dma=True)
            p.add("vector", lambda e, w1hb=w1hb, wst=wst: e.tensor_copy(
                w1hb[:, :], wst[:, :]), r=[wsk, w1k], w=[w1k])
            for ts in range(4):
                ps = b.pse[ts % 2]
                pk = f"PE{ts % 2}"
                for c in range(NC):
                    p.add("tensor",
                          lambda e, c=c, ts=ts, ps=ps, w1hb=w1hb: e.matmul(
                              ps[:, :], w1hb[:, c * 128:(c + 1) * 128],
                              xf[:, c * S + ts * 512: c * S + ts * 512 + 512],
                              start=(c == 0), stop=False),
                          r=[w1k, "XF", pk + "r"] if c == 0 else
                          [w1k, "XF"], w=[pk], tag="h1")
                p.add("tensor", lambda e, hb=hb, ps=ps: e.matmul(
                    ps[:, :], b1rb[0:1, hb * 128:(hb + 1) * 128],
                    b.ones_b512[0:1, 0:512], start=False, stop=False),
                    r=["b1rb", "ones_b512", pk], w=[pk])
                p.add("tensor", lambda e, hb=hb, ts=ts, ps=ps: e.matmul(
                    ps[:, :], b.e16[0:16, hb * 128:(hb + 1) * 128],
                    selm1[0:16, ts * 512: ts * 512 + 512],
                    start=False, stop=True),
                    r=["e16", "selm1", pk], w=[pk])
                p.add("scalar", lambda e, hb=hb, ts=ts, ps=ps: e.activation(
                    hfm[:, hb * S + ts * 512: hb * S + ts * 512 + 512],
                    ps[:, :], AF.Relu, bias=0.0, scale=1.0),
                    r=[pk], w=["HFM", pk + "r"])
        for c in range(NC):
            wi = c % 2
            wst, wsk = wst2[wi], f"wstb{wi}"
            w2cb, w2k = w2cb2[wi], f"w2cb{wi}"
            p.add("gpsimd", lambda e, pp=pp, c=c, wst=wst: e.dma_start(
                wst[:, :].rearrange("a (h n) -> a h n", h=NC),
                d[f"w_ffn{pp}_2"][:, c * 128:(c + 1) * 128]
                .rearrange("(h a) n -> a h n", h=NC)),
                r=[wsk], w=[wsk], dma=True)
            p.add("vector", lambda e, w2cb=w2cb, wst=wst: e.tensor_copy(
                w2cb[:, :], wst[:, :]), r=[wsk, w2k], w=[w2k])
            for ts in range(4):
                ps = b.pse[ts % 2]
                pk = f"PE{ts % 2}"
                ob = oo[ts % 2]
                ok = f"oo{ts % 2}"
                for hc in range(NC):
                    p.add("tensor",
                          lambda e, hc=hc, ts=ts, ps=ps, w2cb=w2cb: e.matmul(
                              ps[:, :], w2cb[:, hc * 128:(hc + 1) * 128],
                              hfm[:, hc * S + ts * 512:
                                  hc * S + ts * 512 + 512],
                              start=(hc == 0), stop=False),
                          r=[w2k, "HFM", pk + "r"] if hc == 0 else
                          [w2k, "HFM"], w=[pk], tag="h2")
                p.add("tensor", lambda e, c=c, ts=ts, ps=ps: e.matmul(
                    ps[:, :], b2s[0:16, c * 128:(c + 1) * 128],
                    selT[0:16, ts * 512: ts * 512 + 512],
                    start=False, stop=True),
                    r=["b2s", "selT", pk], w=[pk])
                p.add("scalar", lambda e, ps=ps, ob=ob: e.copy(
                    ob[:, :], ps[:, :]), r=[pk, ok], w=[ok, pk + "r"])
                if fam != 2:
                    p.add("sync", lambda e, c=c, ts=ts, scr_nm=scr_nm, ob=ob:
                          e.dma_start(
                              d[scr_nm][c][:, ts * 512:(ts + 1) * 512],
                              ob[:, :]), r=[ok], w=[scr_nm], dma=True)
                    oqb = oq[ts % 2]
                    oqn = f"oq{ts % 2}"
                    p.add("scalar", lambda e, ob=ob, oqb=oqb: e.activation(
                        oqb[:, :], ob[:, :], AF.Square, bias=0.0, scale=1.0),
                        r=[ok, oqn], w=[oqn])
                    p.add("tensor", lambda e, c=c, ts=ts, oqb=oqb: e.matmul(
                        b.PS4[0:1, ts * 512:(ts + 1) * 512],
                        b.ones_colb[:, 0:1], oqb[:, :],
                        start=(c == 0), stop=(c == NC - 1)),
                        r=[oqn, "ones_colb", f"PQ{ts}r"] if c == 0 else
                        [oqn, "ones_colb"], w=[f"PQ{ts}"], tag="nsq")
                else:
                    for j in range(4):
                        p.add("tensor", lambda e, j=j, ob=ob: e.transpose(
                            b.psb[0][:, j * 128:(j + 1) * 128],
                            ob[:, j * 128:(j + 1) * 128], b.ident_b[:, :]),
                            r=[ok, "ident_b", "PSB0r"], w=["PSB0"])
                    p.add("scalar", lambda e: e.copy(
                        vtk[:, :], b.psb[0][:, 0:512]),
                        r=["PSB0", "vtk"], w=["vtk", "PSB0r"])
                    p.add("sync", lambda e, c=c, ts=ts: e.dma_start(
                        d["v_scr"][c, ts * 512:(ts + 1) * 512, :]
                        .rearrange("(j a) n -> a j n", j=4),
                        vtk[:, :].rearrange("a (j n) -> a j n", j=4)),
                        r=["vtk"], w=["v_scr"], dma=True)
        if fam == 1:
            p.add("scalar", lambda e: e.copy(ksqr[0:1, :], b.PS4[0:1, 0:S]),
                  r=[f"PQ{kb}" for kb in range(4)],
                  w=["ksqr"] + [f"PQ{kb}r" for kb in range(4)])
            p.add("vector", lambda e: e.reduce_max(
                ks1f[0:1, 0:1], ksqr[0:1, :], AX.X), r=["ksqr"], w=["ks1f"])
        elif fam == 0:
            p.add("scalar", lambda e: e.copy(qsqr[0:1, :], b.PS4[0:1, 0:S]),
                  r=[f"PQ{kb}" for kb in range(4)],
                  w=["qsqr"] + [f"PQ{kb}r" for kb in range(4)])
    p.barrier()
    stF.close()
    stRF.close()

    stA = contextlib.ExitStack()
    sbA = mk(stA)
    kfm = sbA("kfm", [128, NC * S], BF16)
    vall = sbA("vall", [128, NT * D], BF16)
    qfg = sbA("qfg", [128, NC * 512], BF16)
    expT = sbA("expT", [128, NT * 512], BF16)
    mposr = sbA("mposr", [1, S], BF16)
    mnegb = sbA("mnegb", [1, S], BF16)
    srow = sbA("srow", [1, 512], F32)
    rcol = sbA("rcol", [128, 4], F32)
    p.add("scalar", lambda e: e.activation(
        mposr[0:1, :], qsqr[0:1, :], AF.Sqrt, bias=0.0,
        scale=ks1f[0:1, 0:1]), r=["qsqr", "ks1f"], w=["mposr"])
    p.add("vector", lambda e: e.tensor_scalar(
        mnegb[0:1, :], mposr[0:1, :], -1.0, None, ALU.mult),
        r=["mposr"], w=["mnegb"])
    for c in range(NC):
        p.add("sync", lambda e, c=c: e.dma_start(
            kfm[:, c * S:(c + 1) * S], d["kf_scr"][c]),
            r=["kf_scr"], w=[f"kfm{c}"], dma=True)
        p.add("gpsimd", lambda e, c=c: e.dma_start(
            vall[:, :].rearrange("a (t m) -> a t m", m=D)
            [:, :, c * 128:(c + 1) * 128],
            d["v_scr"][c].rearrange("(t a) n -> a t n", t=NT)),
            r=["v_scr"], w=["vall"], dma=True)
    for qg in range(4):
        for c in range(NC):
            p.add("sync", lambda e, qg=qg, c=c: e.dma_start(
                qfg[:, c * 512:(c + 1) * 512],
                d["qf_scr"][c][:, qg * 512:(qg + 1) * 512]),
                r=["qf_scr", "qfg"], w=["qfg"], dma=True)
        for kt in range(NT):
            ps, pk = [(b.pse[0], "PE0"), (b.pse[1], "PE1"),
                      (PQ(b, 0), "PQ0")][kt % 3]
            for c in range(NC):
                p.add("tensor", lambda e, kt=kt, c=c, ps=ps: e.matmul(
                    ps[:, :],
                    kfm[:, c * S + kt * 128: c * S + kt * 128 + 128],
                    qfg[:, c * 512:(c + 1) * 512],
                    start=(c == 0), stop=False),
                    r=[f"kfm{c}", "qfg", pk + "r"] if c == 0 else
                    [f"kfm{c}", "qfg"], w=[pk], tag="kq")
            p.add("tensor", lambda e, ps=ps, qg=qg: e.matmul(
                ps[:, :], b.ones_b1[0:1, 0:128],
                mnegb[0:1, qg * 512:(qg + 1) * 512],
                start=False, stop=True), r=["mnegb", "ones_b1", pk], w=[pk])
            p.add("scalar", lambda e, kt=kt, ps=ps: e.activation(
                expT[:, kt * 512:(kt + 1) * 512], ps[:, :], AF.Exp,
                bias=0.0, scale=1.0 / 32.0), r=[pk], w=["expT", pk + "r"])
            p.add("tensor", lambda e, kt=kt: e.matmul(
                b.PS4[0:1, 1536:2048], b.ones_colb[:, 0:1],
                expT[:, kt * 512:(kt + 1) * 512],
                start=(kt == 0), stop=(kt == NT - 1)),
                r=["expT", "ones_colb", "PQ3r"] if kt == 0 else
                ["expT", "ones_colb"], w=["PQ3"], tag="sm")
        p.add("scalar", lambda e: e.copy(srow[0:1, :], b.PS4[0:1, 1536:2048]),
              r=["PQ3"], w=["srow", "PQ3r"])
        for j0 in range(4):
            p.add("tensor", lambda e, j0=j0: e.transpose(
                b.pse[0][:, j0:j0 + 1], srow[0:1, j0 * 128:(j0 + 1) * 128],
                b.ident_f[0:1, 0:1]),
                r=["srow", "ident_f", "PE0r"], w=["PE0"])
        p.add("vector", lambda e: e.reciprocal(
            rcol[:, 0:4], b.pse[0][:, 0:4]),
            r=["PE0", "rcol"], w=["rcol", "PE0r"])
        for j0 in range(4):
            qt = qg * 4 + j0
            for h in range(2):
                pqk = 1 + h
                pq = PQ(b, pqk)
                pk = f"PQ{pqk}"
                for kt in range(NT):
                    p.add("tensor", lambda e, kt=kt, j0=j0, h=h, pq=pq:
                          e.matmul(pq[:, :],
                                   expT[:, kt * 512 + j0 * 128:
                                        kt * 512 + j0 * 128 + 128],
                                   vall[:, kt * D + h * 512:
                                        kt * D + h * 512 + 512],
                                   start=(kt == 0), stop=(kt == NT - 1)),
                          r=["expT", "vall", pk + "r"] if kt == 0 else
                          ["expT", "vall"], w=[pk], tag="av")
                rk = f"R.{qt}"
                p.add("vector", lambda e, qt=qt, h=h, j0=j0, pq=pq:
                      e.scalar_tensor_tensor(
                          b.R[:, qt * D + h * 512: qt * D + h * 512 + 512],
                          pq[:, :], rcol[:, j0:j0 + 1],
                          b.R[:, qt * D + h * 512: qt * D + h * 512 + 512],
                          ALU.mult, ALU.add),
                      r=[pk, "rcol", rk], w=[rk, pk + "r"])
    p.barrier()
    stA.close()
    stx.close()



def _build_body(b, stages, seqs):
    p, d = b.p, b.dram
    _load_consts(b)
    _build_trig(b)
    _build_relpos(b)
    p.barrier()
    for seq in seqs:
        p.add("sync", lambda e, seq=seq: e.dma_start(
            b.R[:, :].rearrange("a (t n) -> a t n", t=NT),
            d["x_in"][seq].rearrange("(t a) n -> a t n", t=NT)),
            r=[], w=[f"R.{t}" for t in range(NT)], dma=True)
        for stg_i in stages:
            if stg_i in (1, 3, 5, 7):
                kind = {1: "l0aa", 3: "l0ab", 5: "l1aa", 7: "l1ab"}[stg_i]
                mha_stage(b, seq, kind)
            elif stg_i in (2, 4):
                ffn_fff_stage(b, seq, stg_i)
            else:
                ffn_quant_stage(b, seq, stg_i)
        p.add("sync", lambda e, seq=seq: e.dma_start(
            d["x_out"][seq].rearrange("(t a) n -> a t n", t=NT),
            b.R[:, :].rearrange("a (t n) -> a t n", t=NT)),
            r=[f"R.{t}" for t in range(NT)], w=[f"XOUT{seq}"], dma=True)
    p.add("sync", lambda e: e.nop(),
          r=[f"XOUT{seq}" for seq in seqs])



def _run(inputs, stages=ALL_STAGES, x_override=None):
    ins = dict(inputs)
    if x_override is not None:
        ins["L0_attn_a_x"] = x_override
    per_core = host_prepare(ins)
    nc = build_nc(stages=stages)
    in_maps = [per_core(c) for c in range(8)]
    res = run_bass_kernel_spmd(nc, in_maps, core_ids=list(range(8)))
    out = np.concatenate([np.asarray(r["x_out"]).reshape(NSEQ, S, D)
                          for r in res.results], axis=0)
    return out


def kernel(**inputs):
    return _run(inputs, stages=ALL_STAGES)


# version 11
# speedup vs baseline: 1.0425x; 1.0425x over previous
import contextlib
import math
import numpy as np
import ml_dtypes

import concourse.bass as bass
import concourse.mybir as mybir
from concourse.bass_utils import run_bass_kernel_spmd

F32 = mybir.dt.float32
BF16 = mybir.dt.bfloat16
I32 = mybir.dt.int32
AF = mybir.ActivationFunctionType
ALU = mybir.AluOpType
AX = mybir.AxisListType
bfloat16 = ml_dtypes.bfloat16

S = 2048
D = 1024
NT = S // 128
NC = D // 128
NSEQ = 2
NHEAD = 16
ENGS = ("tensor", "vector", "scalar", "gpsimd", "sync")
LNEPS = 1e-5
RMSEPS = float(np.finfo(np.float64).eps)
BITEPS = 1e-8
MASKNEG = -1.0e30

ALL_STAGES = (1, 2, 3, 4, 5, 6, 7, 8)
DBG = None



class Prog:
    def __init__(self, nc):
        self.nc = nc
        self.ops = []
        self.lw = {}
        self.rd = {}

    def add(self, eng, fn, r=(), w=(), dma=False, tag=""):
        deps = set()
        for k in r:
            if k in self.lw:
                deps.add(self.lw[k])
        for k in w:
            if k in self.lw:
                deps.add(self.lw[k])
            deps.update(self.rd.get(k, ()))
        i = len(self.ops)
        deps.discard(i)
        self.ops.append({"id": i, "eng": eng, "fn": fn, "deps": sorted(deps),
                         "dma": dma, "tag": tag})
        for k in r:
            self.rd.setdefault(k, set()).add(i)
        for k in w:
            self.lw[k] = i
            self.rd[k] = set()
        return i

    def barrier(self):
        n = self._barn = getattr(self, "_barn", 0) + 1
        for e in ENGS:
            self.add(e, lambda eng: eng.nop(), r=[], w=[f"bar{n}_{e}"])
        for e in ENGS:
            self.add(e, lambda eng: eng.nop(),
                     r=[f"bar{n}_{x}" for x in ENGS], w=[])

    def emit(self, sems, heat_ap=None):
        ops = self.ops
        need = set()
        for op in ops:
            if op["dma"]:
                need.add(op["id"])
            for dd in op["deps"]:
                de = ops[dd]["eng"]
                if de != op["eng"]:
                    need.add(dd)
                elif de != "tensor" and not ops[dd]["dma"]:
                    need.add(dd)
        def sem_of(op):
            return "gdma" if (op["eng"] == "gpsimd" and op["dma"]) else op["eng"]
        cnt = {e: 0 for e in list(ENGS) + ["gdma"]}
        val = {}
        for op in ops:
            if op["id"] in need:
                cnt[sem_of(op)] += 16 if op["dma"] else 1
                val[op["id"]] = cnt[sem_of(op)]
        per = {e: [op for op in ops if op["eng"] == e] for e in ENGS}
        nc = self.nc
        with nc.Block() as blk:
            for e in ENGS:
                if not per[e]:
                    continue

                def body(eng, e=e):
                    wm = {x: 0 for x in list(ENGS) + ["gdma"]}
                    cum = {x: 0 for x in list(ENGS) + ["gdma"]}
                    for op in per[e]:
                        so = sem_of(op)
                        for dd in op["deps"]:
                            ds = sem_of(ops[dd])
                            if ds == e and ops[dd]["eng"] == e:
                                continue
                            v = val.get(dd)
                            if v is None or wm[ds] >= v:
                                continue
                            if e == "tensor" and heat_ap is not None:
                                hv, hl, hr = heat_ap
                                step = 16 if ds in ("sync", "gdma") else 1
                                for j in range(48):
                                    u = wm[ds] + (j + 1) * step
                                    if u >= v:
                                        break
                                    eng.wait_ge(sems[ds], u)
                                    eng.matmul(hv(), hl(), hr(),
                                               start=True, stop=True)
                            eng.wait_ge(sems[ds], v)
                            wm[ds] = v
                        if op["dma"] and cum[so] > wm[so]:
                            eng.wait_ge(sems[so], cum[so])
                            wm[so] = cum[so]
                        ins = op["fn"](eng)
                        if op["id"] in need:
                            ins.then_inc(sems[so], 16 if op["dma"] else 1)
                            cum[so] += 16 if op["dma"] else 1
                            if not op["dma"]:
                                eng.wait_ge(sems[so], cum[so])
                                wm[so] = cum[so]
                    if e == "sync" and cnt["sync"] > wm["sync"]:
                        eng.wait_ge(sems["sync"], cnt["sync"])
                    if e == "gpsimd" and cnt["gdma"] > wm["gdma"]:
                        eng.wait_ge(sems["gdma"], cnt["gdma"])
                getattr(blk, e)(body)



def _bk(a):
    return np.ascontiguousarray(a)


def host_prepare(inputs):
    g = {}
    g["w_l0aa_qkv"] = _bk(np.concatenate(
        [inputs["L0_attn_a_Wq"].T, inputs["L0_attn_a_Wk"].T,
         inputs["L0_attn_a_Wv"].T], axis=1))
    g["w_l0aa_p"] = _bk(inputs["L0_attn_a_Wp"].T)
    g["gain_l0aa"] = _bk(np.asarray(inputs["L0_attn_a_q_gain"])[None, :])
    g["w_l0ab_qkv"] = _bk(np.concatenate(
        [inputs["L0_attn_b_W_q"].T, inputs["L0_attn_b_W_k"].T,
         inputs["L0_attn_b_W_v"].T], axis=1))
    g["w_l0ab_o"] = _bk(inputs["L0_attn_b_W_o"].T)
    g["w_l1aa_qkv"] = _bk(inputs["L1_attn_a_Wqkv"])
    g["b_l1aa_qkv"] = _bk(np.asarray(inputs["L1_attn_a_bqkv"])[None, :])
    g["w_l1aa_o"] = _bk(inputs["L1_attn_a_Wout"])
    g["b_l1aa_o"] = _bk(np.asarray(inputs["L1_attn_a_bout"])[None, :])
    g["w_l1ab_qkv"] = _bk(np.concatenate(
        [inputs["L1_attn_b_Wq"], inputs["L1_attn_b_Wk"],
         inputs["L1_attn_b_Wv"]], axis=1))
    g["w_l1ab_o"] = _bk(inputs["L1_attn_b_Wo"])
    g["b_l1ab_o"] = _bk(np.asarray(inputs["L1_attn_b_bo"])[None, :])
    g["rel_emb"] = _bk(np.asarray(inputs["L1_attn_b_rel_emb"])[None, :])
    g["w_l1ffn_1"] = _bk(np.asarray(inputs["L1_ffn_w1"]).T)
    g["b_l1ffn_1"] = _bk(np.asarray(inputs["L1_ffn_b1"]).reshape(32, 128).T)
    g["w_l1ffn_2"] = _bk(np.asarray(inputs["L1_ffn_w2"]).T)
    g["b_l1ffn_2"] = _bk(np.asarray(inputs["L1_ffn_b2"])[None, :])
    g["w_ffn_node"] = _bk(np.concatenate(
        [np.asarray(inputs[f"L0_ffn_{p}_node_w"]).T for p in "qkv"], axis=1))
    g["b_ffn_node"] = _bk(np.concatenate(
        [np.asarray(inputs[f"L0_ffn_{p}_node_b"]) for p in "qkv"])[None, :])
    for pp in "qkv":
        g[f"w_ffn{pp}_1"] = _bk(np.asarray(inputs[f"L0_ffn_{pp}_w1"])
                                .transpose(1, 0, 2).reshape(D, D))
        g[f"b_ffn{pp}_1"] = _bk(np.asarray(inputs[f"L0_ffn_{pp}_b1"])
                                .reshape(-1).reshape(NC, 128).T)
        g[f"w_ffn{pp}_2"] = _bk(np.asarray(inputs[f"L0_ffn_{pp}_w2"])
                                .reshape(D, D))
        g[f"b2_{pp}"] = _bk(np.asarray(inputs[f"L0_ffn_{pp}_b2"]))
        g[f"b1r_{pp}"] = _bk(np.asarray(inputs[f"L0_ffn_{pp}_b1"])
                             .reshape(1, D))
    g["ident_f"] = np.eye(128, dtype=np.float32)
    g["ident_b"] = np.eye(128, dtype=np.float32).astype(bfloat16)
    qi = np.arange(128)[:, None]
    ki = np.arange(128)[None, :]
    g["cmask"] = np.where(ki <= qi, 0.0, MASKNEG).astype(np.float32)
    g["cmask_t"] = np.where(qi <= ki, 0.0, MASKNEG).astype(np.float32)
    e2 = np.zeros((2, 128), dtype=np.float32)
    e2[0, 0:64] = 1.0
    e2[1, 64:128] = 1.0
    g["e2"] = e2.astype(bfloat16)
    e16 = np.zeros((16, 8 * 128), dtype=np.float32)
    for hb in range(8):
        e16[2 * hb, hb * 128: hb * 128 + 64] = 1.0
        e16[2 * hb + 1, hb * 128 + 64: hb * 128 + 128] = 1.0
    g["e16"] = _bk(e16.astype(bfloat16))
    g["ones_f1"] = np.ones((1, 128), dtype=np.float32)
    g["ones_b1"] = np.ones((1, 128), dtype=np.float32).astype(bfloat16)
    d0 = np.minimum(np.abs(qi - ki), 16)
    d128 = np.minimum(np.abs(qi - ki + 128), 16)
    ind0 = np.stack([(d0 == v).astype(np.float32) for v in range(17)])
    ind128 = np.stack([(d128 == v).astype(np.float32) for v in range(17)])
    g["ind0"] = _bk(ind0.transpose(1, 0, 2).reshape(128, 17 * 128))
    g["ind128"] = _bk(ind128.transpose(1, 0, 2).reshape(128, 17 * 128))

    x = np.asarray(inputs["L0_attn_a_x"])

    def per_core(c):
        m = dict(g)
        m["x_in"] = _bk(x[2 * c: 2 * c + 2])
        return m
    return per_core


DRAM_SPECS = [
    ("x_in", [NSEQ, S, D], F32, "ExternalInput"),
    ("w_l0aa_qkv", [D, 1536], F32, "ExternalInput"),
    ("w_l0aa_p", [D, D], F32, "ExternalInput"),
    ("gain_l0aa", [1, 16], F32, "ExternalInput"),
    ("w_l0ab_qkv", [D, 3 * D], F32, "ExternalInput"),
    ("w_l0ab_o", [D, D], F32, "ExternalInput"),
    ("w_l1aa_qkv", [D, 3 * D], F32, "ExternalInput"),
    ("b_l1aa_qkv", [1, 3 * D], F32, "ExternalInput"),
    ("w_l1aa_o", [D, D], F32, "ExternalInput"),
    ("b_l1aa_o", [1, D], F32, "ExternalInput"),
    ("w_l1ab_qkv", [D, 3 * D], F32, "ExternalInput"),
    ("w_l1ab_o", [D, D], F32, "ExternalInput"),
    ("b_l1ab_o", [1, D], F32, "ExternalInput"),
    ("rel_emb", [1, 17], F32, "ExternalInput"),
    ("w_l1ffn_1", [D, 4096], F32, "ExternalInput"),
    ("b_l1ffn_1", [128, 32], F32, "ExternalInput"),
    ("w_l1ffn_2", [4096, D], F32, "ExternalInput"),
    ("b_l1ffn_2", [1, D], F32, "ExternalInput"),
    ("w_ffn_node", [D, 45], F32, "ExternalInput"),
    ("b_ffn_node", [1, 45], F32, "ExternalInput"),
    *[(f"w_ffn{pp}_1", [D, D], F32, "ExternalInput") for pp in "qkv"],
    *[(f"b_ffn{pp}_1", [128, 8], F32, "ExternalInput") for pp in "qkv"],
    *[(f"w_ffn{pp}_2", [D, D], F32, "ExternalInput") for pp in "qkv"],
    *[(f"b2_{pp}", [16, D], F32, "ExternalInput") for pp in "qkv"],
    *[(f"b1r_{pp}", [1, D], F32, "ExternalInput") for pp in "qkv"],
    ("ident_f", [128, 128], F32, "ExternalInput"),
    ("ident_b", [128, 128], BF16, "ExternalInput"),
    ("cmask", [128, 128], F32, "ExternalInput"),
    ("cmask_t", [128, 128], F32, "ExternalInput"),
    ("e2", [2, 128], BF16, "ExternalInput"),
    ("e16", [16, 1024], BF16, "ExternalInput"),
    ("ones_f1", [1, 128], F32, "ExternalInput"),
    ("ones_b1", [1, 128], BF16, "ExternalInput"),
    ("ind0", [128, 17 * 128], F32, "ExternalInput"),
    ("ind128", [128, 17 * 128], F32, "ExternalInput"),
    ("x_out", [NSEQ, S, D], F32, "ExternalOutput"),
    ("dbg_out", [128, 16384], F32, "ExternalOutput"),
    ("q_scr", [NC, S, 128], BF16, "Internal"),
    ("k_scr", [NC, S, 128], BF16, "Internal"),
    ("v_scr", [NC, S, 128], BF16, "Internal"),
    ("qf_scr", [NC, 128, S], BF16, "Internal"),
    ("kf_scr", [NC, 128, S], BF16, "Internal"),
    ("w1q_scr", [4, 128, NC * 1024], BF16, "Internal"),
    ("w2q_scr", [4, 128, 8 * D], BF16, "Internal"),
    ("wt_scr", [128, NC * 1536], BF16, "Internal"),
    ("wtp_scr", [128, NC * D], BF16, "Internal"),
    ("ascl_scr", [128, 4], F32, "Internal"),
    ("shift_scr", [64, S], BF16, "Internal"),
]


class B:
    pass


def build_nc(stages=ALL_STAGES, seqs=(0, 1)):
    nc = bass.Bass("TRN2", target_bir_lowering=False,
                   detect_race_conditions=False)
    b = B()
    b.nc = nc
    b.dram = {}
    for nm, shp, dt, kind in DRAM_SPECS:
        b.dram[nm] = nc.dram_tensor(nm, shp, dt, kind=kind).ap()
    es = contextlib.ExitStack()

    def sb(name, shape, dt):
        return es.enter_context(nc.sbuf_tensor(name, shape, dt))
    b.sb = sb
    b.R = sb("R", [128, NT * D], F32)
    b.ident_f = sb("identf", [128, 128], F32)
    b.ident_b = sb("identb", [128, 128], BF16)
    b.cmask = sb("cmask_s", [128, 128], F32)
    b.cmask_t = sb("cmaskt_s", [128, 128], F32)
    b.e2 = sb("e2sb", [2, 128], BF16)
    b.e16 = sb("e16sb", [16, 1024], BF16)
    b.ones_f1 = sb("onesf1", [1, 128], F32)
    b.ones_b1 = sb("onesb1", [1, 128], BF16)
    b.ones_cf = sb("onescf", [128, 1], F32)
    b.ones_b512 = sb("onesb512", [1, 512], BF16)
    b.ones_colb = sb("onescolb", [128, 1], BF16)
    b.ones_b64 = sb("onesb64", [64, 64], BF16)
    b.lvconst = sb("lvconst", [128, 16], F32)
    b.sinc = sb("sinc", [128, NT * 32], F32)
    b.cosc = sb("cosc", [128, NT * 32], F32)
    b.stat = sb("stat", [128, 64], F32)
    b.c_ln_eps = sb("clneps", [128, 1], F32)
    b.c_rms_eps = sb("crmseps", [128, 1], F32)
    b.c_half_pi = sb("chalfpi", [128, 1], F32)
    b.sq_dump = sb("sqdump", [128, D], BF16)
    b.rs_i = sb("rsi", [128, D], I32)
    b.rs_f = sb("rsf", [128, D], F32)
    b.rs_e = sb("rse", [128, D], F32)
    b.rs_c = sb("rsc", [128, D], F32)
    b.rs_c2 = sb("rsc2", [128, D], F32)
    b.d0_8 = sb("d08", [128, 128], F32)
    b.d128_8 = sb("d1288", [128, 128], F32)
    b.d0t_8 = sb("d0t8", [128, 128], F32)
    b.d128t_8 = sb("d128t8", [128, 128], F32)
    b.a16_8 = sb("a168", [128, 1], F32)
    b.absa16_8 = sb("absa168", [128, 1], F32)
    b.relb = sb("relb", [128, 17], F32)
    b.gainb = sb("gainb", [128, 16], F32)
    b.PS4 = es.enter_context(nc.psum_tensor("ps4b", [128, 2048], F32))
    b.pse = [es.enter_context(nc.psum_tensor(f"pse{i}", [128, 512], F32))
             for i in range(2)]
    b.psb = [es.enter_context(nc.psum_tensor("psbb0", [128, 1024], BF16)),
             es.enter_context(nc.psum_tensor("psbb1", [128, 512], F32))]
    b.sems = {e: es.enter_context(nc.semaphore(f"sem_{e}"))
              for e in list(ENGS) + ["gdma"]}
    b.p = Prog(nc)
    with es:
        _build_body(b, stages, seqs)
        b.p.emit(b.sems)
    return nc


def PQ(b, i):
    return b.PS4[:, i * 512:(i + 1) * 512]


def dbg_tap(b, name, ap_fn, rkeys, n, cast=False):
    if DBG != name or getattr(b, "_dbg_done", False):
        return
    b._dbg_done = True
    eng = "gpsimd" if cast else "sync"
    b.p.add(eng, lambda e: e.dma_start(b.dram["dbg_out"][:, 0:n], ap_fn()),
            r=list(rkeys), w=["DBGOUT"], dma=True)



def _load_consts(b):
    p, d = b.p, b.dram
    for nm, dst in [("ident_f", b.ident_f), ("ident_b", b.ident_b),
                    ("cmask", b.cmask), ("cmask_t", b.cmask_t),
                    ("e2", b.e2), ("e16", b.e16), ("ones_f1", b.ones_f1),
                    ("ones_b1", b.ones_b1)]:
        p.add("sync", lambda e, nm=nm, dst=dst: e.dma_start(dst[:, :], d[nm]),
              w=[nm], dma=True)
    p.add("vector", lambda e: e.memset(b.ones_cf[:, :], 1.0), w=["ones_cf"])
    p.add("vector", lambda e: e.memset(b.ones_b512[:, :], 1.0),
          w=["ones_b512"])
    p.add("vector", lambda e: e.memset(b.ones_colb[:, :], 1.0),
          w=["ones_colb"])
    p.add("vector", lambda e: e.memset(b.ones_b64[:, :], 1.0),
          w=["ones_b64"])
    p.add("gpsimd", lambda e: e.iota(b.lvconst[:, 0:16], [[1, 16]],
                                     channel_multiplier=0,
                                     allow_small_or_imprecise_dtypes=True),
          w=["lvconst"])
    p.add("vector", lambda e: e.memset(b.c_ln_eps[:, :], LNEPS), w=["c_ln_eps"])
    p.add("vector", lambda e: e.memset(b.c_rms_eps[:, :], RMSEPS),
          w=["c_rms_eps"])
    p.add("vector", lambda e: e.memset(b.c_half_pi[:, :], math.pi / 2),
          w=["c_half_pi"])


def _build_trig(b):
    p, nc = b.p, b.nc
    _st = contextlib.ExitStack()
    sb = lambda nm, shp, dt: _st.enter_context(nc.sbuf_tensor(nm, shp, dt))
    b._trig_st = _st
    invf = sb("invf", [1, 32], F32)
    invfb = sb("invfb", [128, 32], F32)
    pos = sb("pos", [128, NT], F32)
    ph = sb("ph", [128, NT * 32], F32)
    kk = sb("trig_k", [128, NT * 32], F32)
    wv = sb("trig_w", [128, NT * 32], F32)
    n = NT * 32
    p.add("gpsimd", lambda e: e.iota(invf[0:1, 0:32], [[1, 32]],
                                     channel_multiplier=0,
                                     allow_small_or_imprecise_dtypes=True),
          w=["invf"])
    p.add("scalar", lambda e: e.activation(invf[0:1, 0:32], invf[0:1, 0:32],
                                           AF.Exp, bias=0.0,
                                           scale=-math.log(10000.0) / 32.0),
          r=["invf"], w=["invf"])
    p.add("tensor", lambda e: e.matmul(b.pse[0][0:128, 0:32],
                                       b.ones_f1[0:1, 0:128],
                                       invf[0:1, 0:32], start=True, stop=True),
          r=["invf", "ones_f1"], w=["PE0"])
    p.add("scalar", lambda e: e.copy(invfb[:, :], b.pse[0][0:128, 0:32]),
          r=["PE0"], w=["invfb"])
    p.add("gpsimd", lambda e: e.iota(pos[:, 0:NT], [[128, NT]],
                                     channel_multiplier=1,
                                     allow_small_or_imprecise_dtypes=True),
          w=["pos"])
    p.add("vector", lambda e: e.tensor_tensor(
        ph[:, :].rearrange("a (t j) -> a t j", j=32),
        pos[:, 0:NT].unsqueeze(2).broadcast_to([128, NT, 32]),
        invfb[:, 0:32].unsqueeze(1).broadcast_to([128, NT, 32]), ALU.mult),
        r=["pos", "invfb"], w=["ph"])
    twopi = 2.0 * math.pi
    p.add("vector", lambda e: e.tensor_scalar(kk[:, :], ph[:, :],
                                              1.0 / twopi, None, ALU.mult),
          r=["ph"], w=["kk"])
    p.add("vector", lambda e: e.tensor_copy(b.rs_i[:, 0:n], kk[:, :]),
          r=["kk"], w=["rs_i"])
    p.add("vector", lambda e: e.tensor_copy(kk[:, :], b.rs_i[:, 0:n]),
          r=["rs_i"], w=["kk"])
    p.add("vector", lambda e: e.scalar_tensor_tensor(
        b.rs_e[:, 0:n], ph[:, :], 1.0 / twopi, kk[:, :],
        ALU.mult, ALU.subtract), r=["ph", "kk"], w=["rs_e"])
    p.add("vector", lambda e: e.tensor_single_scalar(
        b.rs_c[:, 0:n], b.rs_e[:, 0:n], 0.5, ALU.is_gt),
        r=["rs_e"], w=["rs_c"])
    p.add("vector", lambda e: e.tensor_single_scalar(
        b.rs_c2[:, 0:n], b.rs_e[:, 0:n], -0.5, ALU.is_lt),
        r=["rs_e"], w=["rs_c2"])
    p.add("vector", lambda e: e.tensor_tensor(
        kk[:, :], kk[:, :], b.rs_c[:, 0:n], ALU.add),
        r=["kk", "rs_c"], w=["kk"])
    p.add("vector", lambda e: e.tensor_tensor(
        kk[:, :], kk[:, :], b.rs_c2[:, 0:n], ALU.subtract),
        r=["kk", "rs_c2"], w=["kk"])
    p.add("vector", lambda e: e.scalar_tensor_tensor(
        wv[:, :], kk[:, :], -twopi, ph[:, :], ALU.mult, ALU.add),
        r=["kk", "ph"], w=["wv"])
    p.add("vector", lambda e: e.tensor_scalar(
        wv[:, :], wv[:, :], -math.pi, math.pi, ALU.max, ALU.min),
        r=["wv"], w=["wv"])
    p.add("scalar", lambda e: e.activation(b.sinc[:, :], wv[:, :], AF.Sin,
                                           bias=0.0, scale=1.0),
          r=["wv"], w=["sinc"])
    p.add("scalar", lambda e: e.activation(kk[:, :], wv[:, :], AF.Abs,
                                           bias=0.0, scale=1.0),
          r=["wv"], w=["kk"])
    p.add("scalar", lambda e: e.activation(b.cosc[:, :], kk[:, :], AF.Sin,
                                           bias=b.c_half_pi[:, 0:1],
                                           scale=-1.0),
          r=["kk", "c_half_pi"], w=["cosc"])
    p.barrier()
    _st.close()


def _build_relpos(b):
    p, d = b.p, b.dram
    _st = contextlib.ExitStack()
    sb = lambda nm, shp, dt: _st.enter_context(b.nc.sbuf_tensor(nm, shp, dt))
    relr = sb("relr", [1, 17], F32)
    ind = sb("indsb", [128, 17 * 128], F32)
    gr = sb("gainr", [1, 16], F32)
    p.add("sync", lambda e: e.dma_start(relr[:, :], d["rel_emb"]),
          w=["relr"], dma=True)
    p.add("tensor", lambda e: e.matmul(b.pse[0][0:128, 0:17],
                                       b.ones_f1[0:1, 0:128],
                                       relr[0:1, 0:17], start=True, stop=True),
          r=["relr", "ones_f1"], w=["PE0"])
    p.add("scalar", lambda e: e.copy(b.relb[:, :], b.pse[0][0:128, 0:17]),
          r=["PE0"], w=["relb"])
    p.add("vector", lambda e: e.tensor_scalar(
        b.a16_8[:, :], b.relb[:, 16:17], 0.125, None, ALU.mult),
        r=["relb"], w=["a16_8"])
    p.add("scalar", lambda e: e.activation(b.absa16_8[:, :], b.relb[:, 16:17],
                                           AF.Abs, bias=0.0, scale=0.125),
          r=["relb"], w=["absa16_8"])
    for nm, dst, kd in [("ind0", b.d0_8, "Dind0"), ("ind128", b.d128_8,
                                                    "Dind128")]:
        p.add("sync", lambda e, nm=nm: e.dma_start(ind[:, :], d[nm]),
              w=["indsb"], dma=True)
        p.add("vector", lambda e, dst=dst: e.memset(dst[:, :], 0.0), w=[kd])
        for v in range(17):
            p.add("vector", lambda e, v=v, dst=dst: e.scalar_tensor_tensor(
                dst[:, :], ind[:, v * 128:(v + 1) * 128], b.relb[:, v:v + 1],
                dst[:, :], ALU.mult, ALU.add),
                r=["indsb", "relb", kd], w=[kd])
        p.add("vector", lambda e, dst=dst: e.tensor_scalar(
            dst[:, :], dst[:, :], 0.125, None, ALU.mult), r=[kd], w=[kd])
    for src, dst, ks, kd in [(b.d0_8, b.d0t_8, "Dind0", "Dt0"),
                             (b.d128_8, b.d128t_8, "Dind128", "Dt128")]:
        p.add("tensor", lambda e, src=src: e.transpose(
            b.pse[1][0:128, 0:128].bitcast(F32) if False else
            b.pse[1][0:128, 0:128], src[:, :], b.ident_f[:, :]),
            r=[ks, "ident_f"], w=["PE1"])
        p.add("scalar", lambda e, dst=dst: e.copy(
            dst[:, :], b.pse[1][0:128, 0:128]), r=["PE1"], w=[kd])
    p.add("sync", lambda e: e.dma_start(gr[:, :], d["gain_l0aa"]),
          w=["gainr"], dma=True)
    p.add("tensor", lambda e: e.matmul(b.pse[0][0:128, 0:16],
                                       b.ones_f1[0:1, 0:128],
                                       gr[0:1, 0:16], start=True, stop=True),
          r=["gainr", "ones_f1"], w=["PE0"])
    p.add("scalar", lambda e: e.copy(b.gainb[:, :], b.pse[0][0:128, 0:16]),
          r=["PE0"], w=["gainb"])
    p.barrier()
    _st.close()



def rint_ladder(b, src_key, src_ap_fn, out_ap_fn, n, out_key,
                lo=None, hi=None):
    p = b.p
    p.add("vector", lambda e: e.tensor_copy(b.rs_i[:, 0:n], src_ap_fn()),
          r=[src_key], w=["rs_i"])
    p.add("vector", lambda e: e.tensor_tensor(
        b.rs_e[:, 0:n], src_ap_fn(), b.rs_i[:, 0:n], ALU.subtract),
        r=[src_key, "rs_i"], w=["rs_e"])
    p.add("vector", lambda e: e.tensor_single_scalar(
        b.rs_c[:, 0:n], b.rs_e[:, 0:n], 0.5, ALU.is_gt),
        r=["rs_e"], w=["rs_c"])
    p.add("vector", lambda e: e.tensor_single_scalar(
        b.rs_c2[:, 0:n], b.rs_e[:, 0:n], -0.5, ALU.is_lt),
        r=["rs_e"], w=["rs_c2"])
    p.add("vector", lambda e: e.tensor_tensor(
        b.rs_c[:, 0:n], b.rs_c[:, 0:n], b.rs_c2[:, 0:n], ALU.subtract),
        r=["rs_c", "rs_c2"], w=["rs_c"])
    p.add("vector", lambda e: e.tensor_tensor(
        b.rs_c[:, 0:n], b.rs_c[:, 0:n], b.rs_i[:, 0:n], ALU.add),
        r=["rs_c", "rs_i"], w=["rs_c"])
    if lo is not None:
        p.add("vector", lambda e: e.tensor_scalar(
            out_ap_fn(), b.rs_c[:, 0:n], float(lo), float(hi),
            ALU.max, ALU.min), r=["rs_c"], w=[out_key])
    else:
        p.add("vector", lambda e: e.tensor_copy(out_ap_fn(), b.rs_c[:, 0:n]),
              r=["rs_c"], w=[out_key])


def ln_tile(b, t, out_ap_fn, out_key):
    p = b.p
    st = b.stat
    Rt = lambda: b.R[:, t * D:(t + 1) * D]
    rk = f"R.{t}"
    p.add("vector", lambda e: e.reduce_sum(st[:, 0:1], Rt(), AX.X),
          r=[rk], w=["st0"])
    p.add("scalar", lambda e: e.activation(b.sq_dump[:, :], Rt(), AF.Square,
                                           bias=0.0, scale=1.0,
                                           accum_out=st[:, 1:2]),
          r=[rk], w=["st1", "sq_dump"])
    p.add("vector", lambda e: e.tensor_scalar(st[:, 2:3], st[:, 0:1],
                                              1.0 / D, None, ALU.mult),
          r=["st0"], w=["st2"])
    p.add("vector", lambda e: e.tensor_scalar(st[:, 3:4], st[:, 2:3],
                                              st[:, 2:3], None, ALU.mult),
          r=["st2"], w=["st3"])
    p.add("vector", lambda e: e.scalar_tensor_tensor(
        st[:, 4:5], st[:, 1:2], 1.0 / D, st[:, 3:4], ALU.mult, ALU.subtract),
        r=["st1", "st3"], w=["st4"])
    p.add("scalar", lambda e: e.activation(st[:, 5:6], st[:, 4:5], AF.Sqrt,
                                           bias=b.c_ln_eps[:, 0:1], scale=1.0),
          r=["st4", "c_ln_eps"], w=["st5"])
    p.add("vector", lambda e: e.reciprocal(st[:, 6:7], st[:, 5:6]),
          r=["st5"], w=["st6"])
    p.add("vector", lambda e: e.tensor_scalar(
        out_ap_fn(), Rt(), st[:, 2:3], st[:, 6:7], ALU.subtract, ALU.mult),
        r=[rk, "st2", "st6"], w=[out_key])


def transpose_tile(b, srcf, src_key, dst_ap_fn, dst_key, nchunk=NC, pi=0):
    p = b.p
    ps = b.psb[0]
    pk = "PSB0"
    for c in range(nchunk):
        p.add("tensor", lambda e, c=c: e.transpose(
            ps[:, c * 128:(c + 1) * 128], srcf(c), b.ident_b[:, :]),
            r=[src_key, "ident_b", pk + "r"], w=[pk], tag="tp")
    p.add("scalar", lambda e: e.copy(dst_ap_fn(), ps[:, 0:nchunk * 128]),
          r=[pk], w=[dst_key, pk + "r"])



def mha_stage(b, seq, kind):
    p, d, nc = b.p, b.dram, b.nc
    stx = contextlib.ExitStack()
    mk = lambda stk: (lambda nm, shp, dt: stk.enter_context(
        nc.sbuf_tensor(f"{kind}s{seq}_{nm}", shp, dt)))
    sb = mk(stx)

    causal = kind in ("l0aa", "l1ab")
    rope = {"l0aa": "neg_half", "l0ab": "inter", "l1aa": "pos_half",
            "l1ab": None}[kind]
    gqa = kind == "l0aa"
    nkv = 4 if gqa else NHEAD
    bit = kind == "l0aa"
    rel = kind == "l1ab"
    scale = 1.0 if bit else 0.125
    wq_nm = {"l0aa": "w_l0aa_qkv", "l0ab": "w_l0ab_qkv",
             "l1aa": "w_l1aa_qkv", "l1ab": "w_l1ab_qkv"}[kind]
    wo_nm = {"l0aa": "w_l0aa_p", "l0ab": "w_l0ab_o",
             "l1aa": "w_l1aa_o", "l1ab": "w_l1ab_o"}[kind]
    bqkv_nm = "b_l1aa_qkv" if kind == "l1aa" else None
    bo_nm = {"l1aa": "b_l1aa_o", "l1ab": "b_l1ab_o"}.get(kind)
    NDOUT = 1536 if bit else 3 * D

    attn = sb("attn", [128, NC * S], BF16)
    gscl = sb("gscl", [128, 4], F32)
    wsb_p = sb("wp", [128, NC * D], BF16) if bit else None
    ascl = sb("ascl", [128, 4], F32) if bit else None
    b._cur_attn = attn
    b._cur_wp = wsb_p
    b._cur_ascl = ascl
    b._cur_gscl = gscl

    stA = contextlib.ExitStack()
    sbA = mk(stA)
    wsb = sbA("w", [128, NC * NDOUT], BF16)
    nb_ = 2
    nq_ = 1 if bit else 2
    xt2 = [sbA("xt0", [128, D], F32)]
    xb2 = [sbA(f"xb{i}", [128, D], BF16) for i in range(nb_)]
    xf2 = [sbA(f"xf{i}", [128, NC * 128], BF16) for i in range(nb_)]
    qkvt2 = [sbA("qkvt0", [128, NDOUT], F32)]
    spill2 = [sbA("spill0", [128, D], BF16)]
    if nq_ == 2:
        qkvt2.append(attn[:, 4096:4096 + 2 * NDOUT].bitcast(F32))
        spill2.append(attn[:, 4096 + 2 * NDOUT:4096 + 2 * NDOUT + D])
    ga2 = [sbA(f"ga{i}", [128, 4], F32) for i in range(nb_)] if bit else None
    bias_r = attn[0:1, 0:3 * D] if bqkv_nm else None

    if not bit:
        p.add("gpsimd", lambda e: e.dma_start(
            wsb[:, :].rearrange("a (c n) -> a c n", c=NC),
            d[wq_nm].rearrange("(c a) n -> a c n", c=NC)),
            w=["Wqkv"], dma=True)
    if bqkv_nm:
        p.add("gpsimd", lambda e: e.dma_start(bias_r[:, :], d[bqkv_nm]),
              w=["biasr"], dma=True)

    if bit:
        if not getattr(b, "_l0aa_w_done", False):
            _l0aa_quant_weights(b, sbA, wsb, wsb_p, ascl)
            p.add("sync", lambda e: e.dma_start(d["wt_scr"], wsb[:, :]),
                  r=["Wqkv"], w=["wt_scr"], dma=True)
            p.add("sync", lambda e: e.dma_start(d["wtp_scr"], wsb_p[:, :]),
                  r=["Wp"], w=["wtp_scr"], dma=True)
            p.add("sync", lambda e: e.dma_start(d["ascl_scr"], ascl[:, :]),
                  r=["ascl"], w=["ascl_scr"], dma=True)
            b._l0aa_w_done = True
        else:
            p.add("gpsimd", lambda e: e.dma_start(wsb[:, :], d["wt_scr"]),
                  r=["wt_scr"], w=["Wqkv"], dma=True)
            p.add("gpsimd", lambda e: e.dma_start(wsb_p[:, :], d["wtp_scr"]),
                  r=["wtp_scr"], w=["Wp"], dma=True)
            p.add("gpsimd", lambda e: e.dma_start(ascl[:, :], d["ascl_scr"]),
                  r=["ascl_scr"], w=["ascl"], dma=True)

    def emit_proj(t, xf, xfk, qkvt, qk_, ga, gk):
        nsl = NDOUT // 512
        for s_ in range(nsl):
            ps = b.pse[s_ % 2]
            pk = f"PE{s_ % 2}"
            for c in range(NC):
                p.add("tensor", lambda e, c=c, s_=s_, ps=ps, xf=xf: e.matmul(
                    ps[:, :], xf[:, c * 128:(c + 1) * 128],
                    wsb[:, c * NDOUT + s_ * 512: c * NDOUT + s_ * 512 + 512],
                    start=(c == 0), stop=(c == NC - 1 and not bqkv_nm)),
                    r=[xfk, "Wqkv", pk + "r"] if c == 0 else [xfk, "Wqkv"],
                    w=[pk], tag="mm")
            if bqkv_nm:
                p.add("tensor", lambda e, s_=s_, ps=ps: e.matmul(
                    ps[:, :], b.ones_b1[0:1, 0:128],
                    bias_r[0:1, s_ * 512:s_ * 512 + 512],
                    start=False, stop=True),
                    r=["biasr", "ones_b1", pk], w=[pk])
            if not bit:
                p.add("scalar", lambda e, s_=s_, ps=ps, qkvt=qkvt: e.copy(
                    qkvt[:, s_ * 512:s_ * 512 + 512], ps[:, :]),
                    r=[pk, qk_], w=[qk_, pk + "r"])
            else:
                for (a0, a1, idx) in [(0, 1024, 0), (1024, 1280, 1),
                                      (1280, 1536, 2)]:
                    lo = max(s_ * 512, a0)
                    hi = min(s_ * 512 + 512, a1)
                    if lo >= hi:
                        continue
                    p.add("vector", lambda e, idx=idx, ga=ga: e.tensor_scalar(
                        ga[:, 3:4], ga[:, 1:2], ascl[:, idx:idx + 1],
                        None, ALU.mult), r=[gk, "ascl"], w=[gk])
                    p.add("vector",
                          lambda e, lo=lo, hi=hi, s_=s_, ps=ps, qkvt=qkvt,
                          ga=ga:
                          e.tensor_scalar(
                              qkvt[:, lo:hi],
                              ps[:, lo - s_ * 512:hi - s_ * 512],
                              ga[:, 3:4], None, ALU.mult),
                          r=[pk, gk, qk_], w=[qk_, pk + "r"])

    pend = None
    for t in range(NT):
        i = t % nb_
        iq = t % nq_
        xt, xb, xf = xt2[0], xb2[i], xf2[i]
        qkvt, spill = qkvt2[iq], spill2[iq]
        xtk, xbk, xfk = "xt0", f"xb{i}", f"xf{i}"
        qk_, spk = f"qkvt{iq}", f"spill{iq}"
        ln_tile(b, t, lambda xt=xt: xt[:, :], xtk)
        ga = gk = None
        if bit:
            ga = ga2[i]
            gk = f"ga{i}"
            p.add("vector", lambda e, xt=xt, ga=ga: e.reduce_max(
                ga[:, 0:1], xt[:, :], AX.X, apply_absolute_value=True),
                r=[xtk, gk], w=[gk])
            p.add("vector", lambda e, ga=ga: e.tensor_scalar(
                ga[:, 1:2], ga[:, 0:1], BITEPS, 1.0 / 127.0,
                ALU.max, ALU.mult), r=[gk], w=[gk])
            p.add("vector", lambda e, ga=ga: e.reciprocal(
                ga[:, 2:3], ga[:, 1:2]), r=[gk], w=[gk])
            p.add("vector", lambda e, xt=xt, ga=ga: e.tensor_scalar(
                b.rs_f[:, 0:D], xt[:, :], ga[:, 2:3], None, ALU.mult),
                r=[xtk, gk], w=["rs_f"])
            rint_ladder(b, "rs_f", lambda: b.rs_f[:, 0:D],
                        lambda xb=xb: xb[:, :], D, xbk, lo=-128, hi=127)
        else:
            p.add("vector", lambda e, xb=xb, xt=xt: e.tensor_copy(
                xb[:, :], xt[:, :]), r=[xtk], w=[xbk])
        transpose_tile(b, lambda c, xb=xb: xb[:, c * 128:(c + 1) * 128], xbk,
                       lambda xf=xf: xf[:, :], xfk, pi=i)
        if bit:
            if pend is not None:
                _rope_and_spill(b, seq, *pend)
            emit_proj(t, xf, xfk, qkvt, qk_, ga, gk)
        else:
            emit_proj(t, xf, xfk, qkvt, qk_, ga, gk)
            if pend is not None:
                _rope_and_spill(b, seq, *pend)
        pend = (t, kind, qkvt, spill, nkv, rope, qk_, spk)
    _rope_and_spill(b, seq, *pend)
    p.barrier()
    stA.close()

    stB = contextlib.ExitStack()
    _attention(b, mk(stB), seq, kind, causal, gqa, rel, scale)
    p.barrier()
    stB.close()

    dbg_tap(b, "attn", lambda: attn[:, 0:16384], ["attn"], 16384, cast=True)
    stC = contextlib.ExitStack()
    _out_proj(b, mk(stC), seq, kind, wo_nm, bo_nm, bit)
    p.barrier()
    stC.close()
    stx.close()


def _l0aa_quant_weights(b, sb, wsb, wsb_p, ascl):
    p, d = b.p, b.dram
    wf = b._cur_attn[:, 0:5120].bitcast(F32)
    aw = b._cur_attn[:, 5120:10240].bitcast(F32)
    alph = sb("alph", [128, 8], F32)
    colsum = sb("colsum", [128, 4], F32)
    tot = sb("tot", [1, 8], F32)
    segs = [(0, 1024, D * D), (1024, 1280, 256 * D),
            (1280, 1536, 256 * D), (1536, 2560, D * D)]

    def load_chunk(c):
        p.add("sync", lambda e, c=c: e.dma_start(
            wf[:, 0:1536], d["w_l0aa_qkv"][c * 128:(c + 1) * 128, :]),
            w=["WF"], dma=True)
        p.add("sync", lambda e, c=c: e.dma_start(
            wf[:, 1536:2560], d["w_l0aa_p"][c * 128:(c + 1) * 128, :]),
            w=["WF"], dma=True)

    p.add("vector", lambda e: e.memset(colsum[:, :], 0.0), w=["colsum"])
    for c in range(NC):
        load_chunk(c)
        p.add("scalar", lambda e: e.activation(aw[:, :], wf[:, :], AF.Abs,
                                               bias=0.0, scale=1.0),
              r=["WF"], w=["aw"])
        for i, (j0, j1, cnt) in enumerate(segs):
            p.add("vector", lambda e, i=i, j0=j0, j1=j1: e.reduce_sum(
                b.stat[:, 48 + i:49 + i], aw[:, j0:j1], AX.X),
                r=["aw"], w=["qs"])
        p.add("vector", lambda e: e.tensor_tensor(
            colsum[:, :], colsum[:, :], b.stat[:, 48:52], ALU.add),
            r=["colsum", "qs"], w=["colsum"])
    p.add("tensor", lambda e: e.matmul(
        b.pse[0][0:1, 0:4], b.ones_cf[:, 0:1], colsum[:, 0:4],
        start=True, stop=True), r=["colsum", "ones_cf", "PE0r"], w=["PE0"])
    p.add("scalar", lambda e: e.copy(tot[0:1, 0:4], b.pse[0][0:1, 0:4]),
          r=["PE0"], w=["tot", "PE0r"])
    for i, (j0, j1, cnt) in enumerate(segs):
        p.add("vector", lambda e, i=i, cnt=cnt: e.tensor_scalar(
            tot[0:1, 4 + i:5 + i], tot[0:1, i:i + 1], 1.0 / cnt, BITEPS,
            ALU.mult, ALU.max), r=["tot"], w=["tot"])
    p.add("vector", lambda e: e.reciprocal(tot[0:1, 0:4], tot[0:1, 4:8]),
          r=["tot"], w=["tot"])
    p.add("tensor", lambda e: e.matmul(
        b.pse[0][0:128, 0:8], b.ones_f1[0:1, 0:128], tot[0:1, 0:8],
        start=True, stop=True), r=["tot", "ones_f1", "PE0r"], w=["PE0"])
    p.add("scalar", lambda e: e.copy(alph[:, 0:8], b.pse[0][0:128, 0:8]),
          r=["PE0"], w=["alph", "PE0r"])
    p.add("scalar", lambda e: e.copy(ascl[:, 0:4], alph[:, 4:8]),
          r=["alph"], w=["ascl"])
    for c in range(NC):
        load_chunk(c)
        for i, (j0, j1, cnt) in enumerate(segs):
            p.add("vector", lambda e, i=i, j0=j0, j1=j1: e.tensor_scalar(
                aw[:, j0:j1], wf[:, j0:j1], alph[:, i:i + 1], None,
                ALU.mult), r=["WF", "alph"], w=["aw"])
        p.add("vector", lambda e: e.tensor_single_scalar(
            wf[:, :], aw[:, :], 0.5, ALU.is_ge), r=["aw"], w=["WF"])
        p.add("vector", lambda e: e.tensor_single_scalar(
            aw[:, :], aw[:, :], -0.5, ALU.is_le), r=["aw"], w=["aw"])
        p.add("vector", lambda e, c=c: e.tensor_tensor(
            wsb[:, c * 1536:(c + 1) * 1536], wf[:, 0:1536], aw[:, 0:1536],
            ALU.subtract), r=["WF", "aw"], w=["Wqkv"])
        p.add("vector", lambda e, c=c: e.tensor_tensor(
            wsb_p[:, c * D:(c + 1) * D], wf[:, 1536:2560], aw[:, 1536:2560],
            ALU.subtract), r=["WF", "aw"], w=["Wp"])


def _rope_and_spill(b, seq, t, kind, qkvt, spill, nkv, rope,
                    qk_="qkvt", spk="spill"):
    p, d = b.p, b.dram
    stt = b.stat
    if kind == "l0aa":
        q3 = lambda: qkvt[:, 0:D].rearrange("a (h e) -> a h e", h=16)
        k3 = lambda: qkvt[:, D:D + 256].rearrange("a (h e) -> a h e", h=4)
        p.add("vector", lambda e: e.scalar_tensor_tensor(
            b.sq_dump[:, 0:D].rearrange("a (h e) -> a h e", h=16),
            q3(), 1.0, q3(), ALU.mult, ALU.mult), r=[qk_], w=["sq_dump"])
        p.add("vector", lambda e: e.reduce_sum(
            stt[:, 16:32], b.sq_dump[:, 0:D].rearrange(
                "a (h e) -> a h e", h=16), AX.X), r=["sq_dump"], w=["rmsq"])
        p.add("scalar", lambda e: e.activation(
            stt[:, 16:32], stt[:, 16:32], AF.Sqrt,
            bias=b.c_rms_eps[:, 0:1], scale=1.0 / 64.0),
            r=["rmsq", "c_rms_eps"], w=["rmsq"])
        p.add("vector", lambda e: e.reciprocal(stt[:, 16:32], stt[:, 16:32]),
              r=["rmsq"], w=["rmsq"])
        p.add("vector", lambda e: e.tensor_tensor(
            stt[:, 16:32], stt[:, 16:32], b.gainb[:, 0:16], ALU.mult),
            r=["rmsq", "gainb"], w=["rmsq"])
        p.add("vector", lambda e: e.tensor_scalar(
            stt[:, 16:32], stt[:, 16:32], 0.125, None, ALU.mult),
            r=["rmsq"], w=["rmsq"])
        p.add("vector", lambda e: e.tensor_tensor(
            q3(), q3(), stt[:, 16:32].unsqueeze(2).broadcast_to(
                [128, 16, 64]), ALU.mult), r=[qk_, "rmsq"], w=[qk_])
        p.add("vector", lambda e: e.scalar_tensor_tensor(
            b.sq_dump[:, 0:256].rearrange("a (h e) -> a h e", h=4),
            k3(), 1.0, k3(), ALU.mult, ALU.mult), r=[qk_], w=["sq_dump"])
        p.add("vector", lambda e: e.reduce_sum(
            stt[:, 32:36], b.sq_dump[:, 0:256].rearrange(
                "a (h e) -> a h e", h=4), AX.X), r=["sq_dump"], w=["rmsk"])
        p.add("scalar", lambda e: e.activation(
            stt[:, 32:36], stt[:, 32:36], AF.Sqrt,
            bias=b.c_rms_eps[:, 0:1], scale=1.0 / 64.0),
            r=["rmsk", "c_rms_eps"], w=["rmsk"])
        p.add("vector", lambda e: e.reciprocal(stt[:, 32:36], stt[:, 32:36]),
              r=["rmsk"], w=["rmsk"])
        p.add("vector", lambda e: e.tensor_tensor(
            k3(), k3(), stt[:, 32:36].unsqueeze(2).broadcast_to(
                [128, 4, 64]), ALU.mult), r=[qk_, "rmsk"], w=[qk_])

    if rope is not None:
        cosb = lambda: b.cosc[:, t * 32:(t + 1) * 32]
        sinb = lambda: b.sinc[:, t * 32:(t + 1) * 32]
        for off, nh in [(0, NHEAD), (D, nkv)]:
            h3 = lambda off=off, nh=nh: qkvt[:, off:off + nh * 64].rearrange(
                "a (h e) -> a h e", h=nh)
            cb = lambda nh=nh: cosb().unsqueeze(1).broadcast_to([128, nh, 32])
            sbb = lambda nh=nh: sinb().unsqueeze(1).broadcast_to([128, nh, 32])
            t1 = lambda nh=nh: b.rs_f[:, 0:nh * 32].rearrange(
                "a (h e) -> a h e", h=nh)
            t2 = lambda nh=nh: b.rs_e[:, 0:nh * 32].rearrange(
                "a (h e) -> a h e", h=nh)
            t3 = lambda nh=nh: b.rs_c[:, 0:nh * 32].rearrange(
                "a (h e) -> a h e", h=nh)
            if rope in ("neg_half", "pos_half"):
                x1 = lambda h3=h3: h3()[:, :, 0:32]
                x2 = lambda h3=h3: h3()[:, :, 32:64]
                sgn = 1.0 if rope == "neg_half" else -1.0
                p.add("vector", lambda e, x1=x1, cb=cb, t1=t1: e.tensor_tensor(
                    t1(), x1(), cb(), ALU.mult), r=[qk_, "cosc"], w=["rt1"])
                p.add("vector", lambda e, x2=x2, sbb=sbb, t2=t2:
                      e.tensor_tensor(t2(), x2(), sbb(), ALU.mult),
                      r=[qk_, "sinc"], w=["rt2"])
                p.add("vector", lambda e, x1=x1, sbb=sbb, t3=t3:
                      e.tensor_tensor(t3(), x1(), sbb(), ALU.mult),
                      r=[qk_, "sinc"], w=["rt3"])
                p.add("vector", lambda e, x2=x2, cb=cb: e.tensor_tensor(
                    x2(), x2(), cb(), ALU.mult), r=[qk_, "cosc"],
                    w=[qk_])
                p.add("vector", lambda e, x2=x2, t3=t3, sgn=sgn:
                      e.scalar_tensor_tensor(x2(), t3(), -sgn, x2(),
                                             ALU.mult, ALU.add),
                      r=["rt3", qk_], w=[qk_])
                p.add("vector", lambda e, x1=x1, t1=t1, t2=t2, sgn=sgn:
                      e.scalar_tensor_tensor(x1(), t2(), sgn, t1(),
                                             ALU.mult, ALU.add),
                      r=["rt1", "rt2"], w=[qk_])
            else:
                ev = lambda h3=h3: h3().rearrange(
                    "a h (e two) -> a h e two", two=2)[:, :, :, 0:1]
                od = lambda h3=h3: h3().rearrange(
                    "a h (e two) -> a h e two", two=2)[:, :, :, 1:2]
                cb4 = lambda nh=nh: cosb().unsqueeze(1).unsqueeze(
                    3).broadcast_to([128, nh, 32, 1])
                sb4 = lambda nh=nh: sinb().unsqueeze(1).unsqueeze(
                    3).broadcast_to([128, nh, 32, 1])
                t14 = lambda t1=t1: t1().unsqueeze(3)
                t24 = lambda t2=t2: t2().unsqueeze(3)
                t34 = lambda t3=t3: t3().unsqueeze(3)
                p.add("vector", lambda e, ev=ev, cb4=cb4, t14=t14:
                      e.tensor_tensor(t14(), ev(), cb4(), ALU.mult),
                      r=[qk_, "cosc"], w=["rt1"])
                p.add("vector", lambda e, od=od, sb4=sb4, t24=t24:
                      e.tensor_tensor(t24(), od(), sb4(), ALU.mult),
                      r=[qk_, "sinc"], w=["rt2"])
                p.add("vector", lambda e, ev=ev, sb4=sb4, t34=t34:
                      e.tensor_tensor(t34(), ev(), sb4(), ALU.mult),
                      r=[qk_, "sinc"], w=["rt3"])
                p.add("vector", lambda e, od=od, cb4=cb4: e.tensor_tensor(
                    od(), od(), cb4(), ALU.mult), r=[qk_, "cosc"],
                    w=[qk_])
                p.add("vector", lambda e, od=od, t34=t34: e.tensor_tensor(
                    od(), od(), t34(), ALU.add), r=["rt3", qk_],
                    w=[qk_])
                p.add("vector", lambda e, ev=ev, t14=t14, t24=t24:
                      e.tensor_tensor(ev(), t14(), t24(), ALU.subtract),
                      r=["rt1", "rt2"], w=[qk_])

    def spill_seg(src_off, n_pairs, scr_nm):
        p.add("vector", lambda e: e.tensor_copy(
            spill[:, 0:n_pairs * 128],
            qkvt[:, src_off:src_off + n_pairs * 128]),
            r=[qk_], w=[spk])
        p.add("sync", lambda e, t=t, n_pairs=n_pairs, scr_nm=scr_nm:
              e.dma_start(
                  d[scr_nm][:, t * 128:(t + 1) * 128, :]
                  .transpose([1, 0, 2])[0:128, 0:n_pairs, 0:128],
                  spill[:, 0:n_pairs * 128].rearrange(
                      "a (c n) -> a c n", c=n_pairs)),
              r=[spk], w=[scr_nm], dma=True)
    spill_seg(0, NC, "q_scr")
    spill_seg(D, nkv // 2, "k_scr")
    spill_seg(D + nkv * 64, nkv // 2, "v_scr")


def _attention(b, sb, seq, kind, causal, gqa, rel, scale):
    p, d, nc = b.p, b.dram, b.nc
    attn = b._cur_attn
    nkv_ = 1 if (gqa or rel) else 2
    qp2 = [sb(f"qp{i}", [128, NT * 128], BF16) for i in range(nkv_)]
    kp2 = [sb(f"kp{i}", [128, NT * 128], BF16) for i in range(nkv_)]
    vp2 = [sb(f"vp{i}", [128, NT * 128], BF16) for i in range(nkv_)]
    qq = [sb(f"qq{i}", [128, S], BF16) for i in range(2)]
    kq = [sb(f"kq{i}", [128, S], BF16) for i in range(2)]
    va = [sb(f"va{i}", [128, NT * 65], BF16) for i in range(2)]
    expT = sb("expT", [128, NT * 512], BF16)
    mrow = sb("mrow", [128, 1], F32)
    mbf = sb("mbf", [128, 1], BF16)
    srow = sb("srow", [1, 512], F32)
    rb = sb("rb", [64, 512], F32)
    ksqb = sb("ksqb", [64, S], BF16)
    kmx = sb("kmx", [64, 1], F32)
    ks1 = sb("ks1", [1, 4], F32)
    stg = sb("stg", [128, S], BF16)
    if rel:
        shs = sb("shs", [128, 512], F32)
        m8row = sb("m8row", [1, 2048], F32)
        m8b = sb("m8b", [128, 512], F32)
        bandt = sb("bandt", [128, 256], F32)

    for hp in range(NC):
        hpi = (hp % 2) % nkv_
        kvi = hpi
        qp, kp, vp = qp2[hpi], kp2[kvi], vp2[kvi]
        qpk, kpk, vpk = f"qp{hpi}", f"kp{kvi}", f"vp{kvi}"
        p.add("sync", lambda e, hp=hp, qp=qp: e.dma_start(
            qp[:, :].rearrange("a (t n) -> a t n", t=NT),
            d["q_scr"][hp].rearrange("(t a) n -> a t n", t=NT)),
            r=["q_scr", qpk], w=[qpk], dma=True)
        kvp = (hp // 2) // 2 if gqa else hp
        if not gqa or hp % 4 == 0:
            p.add("sync", lambda e, kvp=kvp, kp=kp: e.dma_start(
                kp[:, :].rearrange("a (t n) -> a t n", t=NT),
                d["k_scr"][kvp].rearrange("(t a) n -> a t n", t=NT)),
                r=["k_scr", kpk], w=[kpk], dma=True)
            p.add("sync", lambda e, kvp=kvp, vp=vp: e.dma_start(
                vp[:, :].rearrange("a (t n) -> a t n", t=NT),
                d["v_scr"][kvp].rearrange("(t a) n -> a t n", t=NT)),
                r=["v_scr", vpk], w=[vpk], dma=True)

        for src, bufs, srck, basek in ((qp, qq, qpk, "QQ"),
                                       (kp, kq, kpk, "KB")):
            for t in range(NT):
                ps = b.psb[0]
                p.add("tensor", lambda e, src=src, t=t, ps=ps: e.transpose(
                    ps[:, 0:128], src[:, t * 128:(t + 1) * 128],
                    b.ident_b[:, :]),
                    r=[srck, "ident_b", "PSB0r"], w=["PSB0"])
                p.add("scalar", lambda e, bufs=bufs, t=t, ps=ps: e.copy(
                    bufs[0][0:64, t * 128:(t + 1) * 128], ps[0:64, 0:128]),
                    r=["PSB0"], w=[basek + "0"])
                p.add("scalar", lambda e, t=t, ps=ps: e.copy(
                    stg[64:128, t * 128:(t + 1) * 128], ps[64:128, 0:128]),
                    r=["PSB0"], w=["stg", "PSB0r"])
            p.add("sync", lambda e: e.dma_start(
                d["shift_scr"][:, :], stg[64:128, :]),
                r=["stg"], w=["shift_scr"], dma=True)
            p.add("sync", lambda e, bufs=bufs, basek=basek: e.dma_start(
                bufs[1][0:64, :], d["shift_scr"][:, :]),
                r=["shift_scr"], w=[basek + "1"], dma=True)
        for i in range(2):
            p.add("vector", lambda e, i=i: e.memset(kq[i][64:65, :], -1.0),
                  w=[f"KB{i}"])
            p.add("vector", lambda e, i=i, vp=vp: e.tensor_copy(
                va[i][:, :].rearrange("a (t n) -> a t n", n=65)[:, :, 0:64],
                vp[:, :].rearrange("a (t n) -> a t n", n=128)
                [:, :, i * 64:(i + 1) * 64]),
                r=[vpk], w=[f"VA{i}"])
            p.add("vector", lambda e, i=i: e.memset(
                va[i][:, :].rearrange("a (t n) -> a t n", n=65)[:, :, 64:65],
                1.0), w=[f"VA{i}"])

        if True:
            for i in range(2):
                p.add("vector", lambda e, i=i: e.tensor_tensor(
                    ksqb[0:64, :], kq[i][0:64, :], kq[i][0:64, :], ALU.mult),
                    r=[f"KB{i}", "ksqb"], w=["ksqb"])
                for kb in range(4):
                    p.add("tensor", lambda e, kb=kb: e.matmul(
                        b.PS4[0:64, kb * 512:(kb + 1) * 512],
                        b.ones_b64[0:64, 0:64],
                        ksqb[0:64, kb * 512:(kb + 1) * 512],
                        start=True, stop=True),
                        r=["ksqb", "ones_b64", f"PQ{kb}r"], w=[f"PQ{kb}"])
                p.add("vector", lambda e: e.reduce_max(
                    kmx[0:64, 0:1], b.PS4[0:64, 0:S], AX.X),
                    r=[f"PQ{kb}" for kb in range(4)],
                    w=["kmx"] + [f"PQ{kb}r" for kb in range(4)])
                p.add("tensor", lambda e: e.transpose(
                    b.pse[0][0:1, 0:64], kmx[0:64, 0:1],
                    b.ident_f[0:64, 0:64]),
                    r=["kmx", "ident_f", "PE0r"], w=["PE0"])
                p.add("vector", lambda e, i=i: e.reduce_max(
                    ks1[0:1, i:i + 1], b.pse[0][0:1, 0:64], AX.X),
                    r=["PE0", "ks1"], w=["ks1", "PE0r"])
            for hi in range(2 if not rel else 0):
                ki_ = (((2 * hp + hi) // 4) % 2) if gqa else hi
                QQ, qqk = qq[hi], f"QQ{hi}"
                p.add("vector", lambda e, QQ=QQ, qqk=qqk: e.tensor_tensor(
                    ksqb[0:64, :], QQ[0:64, :], QQ[0:64, :], ALU.mult),
                    r=[qqk, "ksqb"], w=["ksqb"])
                for kb in range(4):
                    p.add("tensor", lambda e, kb=kb: e.matmul(
                        b.pse[1][0:1, 0:512], b.ones_colb[0:64, 0:1],
                        ksqb[0:64, kb * 512:(kb + 1) * 512],
                        start=True, stop=True),
                        r=["ksqb", "ones_colb", "PE1r"], w=["PE1"])
                    p.add("scalar", lambda e, kb=kb, QQ=QQ, ki_=ki_:
                          e.activation(
                              QQ[64:65, kb * 512:(kb + 1) * 512],
                              b.pse[1][0:1, 0:512], AF.Sqrt,
                              bias=0.0, scale=ks1[0:1, ki_:ki_ + 1]),
                          r=["PE1", "ks1"], w=[qqk + "m", "PE1r"])
        for hi in range(2):
            h = 2 * hp + hi
            ki_ = ((h // 4) % 2) if gqa else hi
            QQ, KQB, VAB = qq[hi], kq[ki_], va[ki_]
            qqk, kqk, vak = f"QQ{hi}", f"KB{ki_}", f"VA{ki_}"
            if rel:
                p.add("vector", lambda e, QQ=QQ, qqk=qqk: e.tensor_tensor(
                    ksqb[0:64, :], QQ[0:64, :], QQ[0:64, :], ALU.mult),
                    r=[qqk, "ksqb"], w=["ksqb"])
                for kb in range(4):
                    p.add("tensor", lambda e, kb=kb: e.matmul(
                        b.pse[1][0:1, 0:512], b.ones_colb[0:64, 0:1],
                        ksqb[0:64, kb * 512:(kb + 1) * 512],
                        start=True, stop=True),
                        r=["ksqb", "ones_colb", "PE1r"], w=["PE1"])
                    p.add("scalar", lambda e, kb=kb, ki_=ki_: e.activation(
                        m8row[0:1, kb * 512:(kb + 1) * 512],
                        b.pse[1][0:1, 0:512], AF.Sqrt,
                        bias=0.0, scale=ks1[0:1, ki_:ki_ + 1]),
                        r=["PE1", "ks1", "m8row"], w=["m8row", "PE1r"])
                p.add("vector", lambda e: e.tensor_scalar(
                    m8row[0:1, :], m8row[0:1, :], b.absa16_8[0:1, 0:1],
                    None, ALU.mult), r=["m8row", "absa16_8"], w=["m8row"])
                for qt in range(NT):
                    ps = b.pse[qt % 2]
                    pk = f"PE{qt % 2}"
                    k0 = max(qt - 1, 0) * 128
                    wv = (qt + 1) * 128 - k0
                    off = wv - 128
                    p.add("tensor",
                          lambda e, qt=qt, k0=k0, wv=wv, QQ=QQ, KQB=KQB,
                          ps=ps:
                          e.matmul(ps[:, 0:wv],
                                   QQ[0:64, qt * 128:(qt + 1) * 128],
                                   KQB[0:64, k0:k0 + wv],
                                   start=True, stop=True),
                          r=[qqk, kqk, pk + "r"], w=[pk], tag="qkb")
                    p.add("vector", lambda e, ps=ps, off=off: e.tensor_tensor(
                        bandt[:, off:off + 128], ps[:, off:off + 128],
                        b.d0_8[:, :], ALU.mult),
                        r=[pk, "Dind0", "bandt"], w=["bandt"])
                    p.add("vector", lambda e, off=off: e.tensor_tensor(
                        bandt[:, off:off + 128], bandt[:, off:off + 128],
                        b.cmask[:, :], ALU.add),
                        r=["bandt", "cmask"], w=["bandt"])
                    if off:
                        p.add("vector", lambda e, ps=ps: e.tensor_tensor(
                            bandt[:, 0:128], ps[:, 0:128], b.d128_8[:, :],
                            ALU.mult), r=[pk, "Dind128", "bandt"],
                            w=["bandt"])
                    p.add("vector", lambda e, wv=wv, pk=pk: e.reduce_max(
                        mrow[:, 0:1], bandt[:, 0:wv], AX.X),
                        r=["bandt"], w=["mrow", pk + "r"])
                    p.add("vector", lambda e: e.tensor_scalar(
                        mbf[:, 0:1], mrow[:, 0:1], 1.0, None, ALU.mult),
                        r=["mrow"], w=["mbf"])
                    p.add("tensor", lambda e: e.transpose(
                        b.psb[0][0:1, 0:128], mbf[:, 0:1], b.ident_b[:, :]),
                        r=["mbf", "ident_b", "PSB0r"], w=["PSB0"])
                    p.add("vector", lambda e, qt=qt: e.tensor_tensor(
                        m8row[0:1, qt * 128:(qt + 1) * 128],
                        m8row[0:1, qt * 128:(qt + 1) * 128],
                        b.psb[0][0:1, 0:128], ALU.max),
                        r=["PSB0", "m8row"], w=["m8row", "PSB0r"])

            for qg in range(4):
                q0t = qg * 4
                kt_hi = q0t + 4 if causal else NT
                if rel:
                    p.add("tensor", lambda e, qg=qg: e.matmul(
                        b.pse[0][:, :], b.ones_f1[0:1, 0:128],
                        m8row[0:1, qg * 512:(qg + 1) * 512],
                        start=True, stop=True),
                        r=["m8row", "ones_f1", "PE0r"], w=["PE0"])
                    p.add("scalar", lambda e: e.copy(m8b[:, :],
                                                     b.pse[0][:, :]),
                          r=["PE0"], w=["m8b", "PE0r"])
                for kt in range(kt_hi):
                    ps, pk = [(b.pse[0], "PE0"), (b.pse[1], "PE1"),
                              (PQ(b, 0), "PQ0")][kt % 3]
                    kcon = 64 if rel else 65
                    p.add("tensor",
                          lambda e, kt=kt, qg=qg, ps=ps, KQB=KQB, QQ=QQ,
                          kcon=kcon:
                          e.matmul(ps[:, :],
                                   KQB[0:kcon, kt * 128:(kt + 1) * 128],
                                   QQ[0:kcon, qg * 512:(qg + 1) * 512],
                                   start=True, stop=True),
                          r=[kqk, qqk, qqk + "m", pk + "r"], w=[pk],
                          tag="kq")
                    if rel:
                        _l1ab_expprep(b, kt, qg, q0t, ps, pk, shs, m8b)
                        p.add("scalar", lambda e, kt=kt: e.activation(
                            expT[:, kt * 512:(kt + 1) * 512], shs[:, :],
                            AF.Exp, bias=0.0, scale=1.0),
                            r=["shs"], w=["expT", pk + "r"])
                    else:
                        if causal:
                            for j in range(4):
                                if q0t + j == kt:
                                    p.add("vector",
                                          lambda e, j=j, ps=ps:
                                          e.tensor_tensor(
                                              ps[:, j * 128:(j + 1) * 128],
                                              ps[:, j * 128:(j + 1) * 128],
                                              b.cmask_t[:, :], ALU.add),
                                          r=[pk, "cmask_t"], w=[pk])
                            if kt > q0t:
                                nmask = min(kt - q0t, 4)
                                p.add("vector",
                                      lambda e, nmask=nmask, ps=ps:
                                      e.memset(ps[:, 0:nmask * 128],
                                               MASKNEG), w=[pk])
                        p.add("scalar", lambda e, kt=kt, ps=ps: e.activation(
                            expT[:, kt * 512:(kt + 1) * 512], ps[:, :],
                            AF.Exp, bias=0.0, scale=scale),
                            r=[pk], w=["expT", pk + "r"])
                nkt = kt_hi
                avc = 1024 if qg % 2 == 0 else 512
                avk = "PQ2" if qg % 2 == 0 else "PQ1"
                for kt in range(nkt):
                    p.add("tensor",
                          lambda e, kt=kt, VAB=VAB, nkt=nkt, avc=avc:
                          e.matmul(b.PS4[0:65, avc:avc + 512],
                                   VAB[:, kt * 65:(kt + 1) * 65],
                                   expT[:, kt * 512:(kt + 1) * 512],
                                   start=(kt == 0), stop=(kt == nkt - 1)),
                          r=[vak, "expT", avk + "r"] if kt == 0 else
                          [vak, "expT"], w=[avk], tag="av")
                p.add("scalar", lambda e, avc=avc: e.copy(
                    srow[0:1, :], b.PS4[64:65, avc:avc + 512]),
                    r=[avk], w=["srow"])
                p.add("tensor", lambda e: e.matmul(
                    b.PS4[0:64, 1536:2048], b.ones_f1[0:1, 0:64],
                    srow[0:1, :], start=True, stop=True),
                    r=["srow", "ones_f1", "PQ3r"], w=["PQ3"])
                p.add("vector", lambda e: e.reciprocal(
                    rb[:, :], b.PS4[0:64, 1536:2048]),
                    r=["PQ3"], w=["rb", "PQ3r"])
                p.add("vector", lambda e, hi=hi, hp=hp, qg=qg, avc=avc:
                      e.tensor_tensor(
                          attn[hi * 64:(hi + 1) * 64,
                               hp * S + qg * 512: hp * S + (qg + 1) * 512],
                          b.PS4[0:64, avc:avc + 512], rb[:, :], ALU.mult),
                      r=[avk, "rb"], w=["attn", avk + "r"])


def _l1ab_maxpass(b, qt, mrow, bandt):
    p = b.p
    sc = b.PS4
    p.add("vector", lambda e, qt=qt: e.tensor_tensor(
        bandt[:, 0:128], sc[:, qt * 128:(qt + 1) * 128], b.d0_8[:, :],
        ALU.mult), r=[f"PQ{qt // 4}", "Dind0"], w=["bandt"])
    p.add("vector", lambda e: e.tensor_tensor(
        bandt[:, 0:128], bandt[:, 0:128], b.cmask[:, :], ALU.add),
        r=["bandt", "cmask"], w=["bandt"])
    nb = 128
    if qt > 0:
        p.add("vector", lambda e, qt=qt: e.tensor_tensor(
            bandt[:, 128:256], sc[:, (qt - 1) * 128: qt * 128],
            b.d128_8[:, :], ALU.mult),
            r=[f"PQ{(qt - 1) // 4}", "Dind128"], w=["bandt"])
        nb = 256
    p.add("vector", lambda e, nb=nb: e.reduce_max(
        mrow[:, 0:1], bandt[:, 0:nb], AX.X), r=["bandt"], w=["mrow"])
    nkb = (qt * 128 + 128 + 511) // 512
    p.add("vector", lambda e, qt=qt, nkb=nkb: e.reduce_max(
        b.stat[:, 40:41], sc[:, 0:(qt + 1) * 128], AX.X,
        apply_absolute_value=True),
        r=[f"PQ{kb}" for kb in range(nkb)], w=["obmax", "PQr"])
    p.add("vector", lambda e: e.tensor_scalar(
        b.stat[:, 41:42], b.stat[:, 40:41], b.absa16_8[:, 0:1], None,
        ALU.mult), r=["obmax", "absa16_8"], w=["obmax2"])
    p.add("vector", lambda e: e.tensor_tensor(
        mrow[:, 0:1], mrow[:, 0:1], b.stat[:, 41:42], ALU.max),
        r=["mrow", "obmax2"], w=["mrow"])


def _l1ab_expprep(b, kt, qg, q0t, ps, pk, shs, m8b):
    p = b.p
    p.add("vector", lambda e, ps=ps: e.scalar_tensor_tensor(
        shs[:, :], ps[:, :], b.a16_8[:, 0:1], m8b[:, :],
        ALU.mult, ALU.subtract), r=[pk, "a16_8", "m8b"], w=["shs"])
    for j in range(4):
        qt = q0t + j
        if qt == kt:
            p.add("vector", lambda e, j=j, ps=ps: e.tensor_tensor(
                shs[:, j * 128:(j + 1) * 128], ps[:, j * 128:(j + 1) * 128],
                b.d0t_8[:, :], ALU.mult), r=[pk, "Dt0"], w=["shs"])
            p.add("vector", lambda e, j=j: e.tensor_tensor(
                shs[:, j * 128:(j + 1) * 128],
                shs[:, j * 128:(j + 1) * 128],
                b.cmask_t[:, :], ALU.add), r=["shs", "cmask_t"], w=["shs"])
            p.add("vector", lambda e, j=j: e.tensor_tensor(
                shs[:, j * 128:(j + 1) * 128],
                shs[:, j * 128:(j + 1) * 128],
                m8b[:, j * 128:(j + 1) * 128], ALU.subtract),
                r=["shs", "m8b"], w=["shs"])
        elif qt == kt + 1:
            p.add("vector", lambda e, j=j, ps=ps: e.tensor_tensor(
                shs[:, j * 128:(j + 1) * 128], ps[:, j * 128:(j + 1) * 128],
                b.d128t_8[:, :], ALU.mult), r=[pk, "Dt128"], w=["shs"])
            p.add("vector", lambda e, j=j: e.tensor_tensor(
                shs[:, j * 128:(j + 1) * 128],
                shs[:, j * 128:(j + 1) * 128],
                m8b[:, j * 128:(j + 1) * 128], ALU.subtract),
                r=["shs", "m8b"], w=["shs"])
        elif qt < kt:
            p.add("vector", lambda e, j=j: e.memset(
                shs[:, j * 128:(j + 1) * 128], MASKNEG), w=["shs"])


def _out_proj(b, sb, seq, kind, wo_nm, bo_nm, bit):
    p, d = b.p, b.dram
    attn = b._cur_attn
    if bit:
        _l0aa_outproj(b, sb, seq)
        return
    wo = sb("wo", [128, NC * D], BF16)
    bo_r = sb("bor", [1, D], BF16) if bo_nm else None
    if bo_nm:
        p.add("gpsimd", lambda e: e.dma_start(bo_r[:, :], d[bo_nm]),
              w=["bor"], dma=True)
    p.add("gpsimd", lambda e: e.dma_start(
        wo[:, :].rearrange("a (c n) -> a c n", c=NC),
        d[wo_nm].rearrange("(c a) n -> a c n", c=NC)),
        w=["Wo"], dma=True)
    for t in range(NT):
        for half in range(2):
            ps = b.pse[half]
            pk = f"PE{half}"
            for c in range(NC):
                p.add("tensor", lambda e, t=t, c=c, half=half, ps=ps:
                      e.matmul(
                          ps[:, :],
                          attn[:, c * S + t * 128:c * S + (t + 1) * 128],
                          wo[:, c * D + half * 512: c * D + half * 512 + 512],
                          start=(c == 0),
                          stop=(c == NC - 1 and bo_r is None)),
                      r=["attn", "Wo", pk + "r"] if c == 0 else
                      ["attn", "Wo"], w=[pk], tag="op")
            if bo_r is not None:
                p.add("tensor", lambda e, half=half, ps=ps: e.matmul(
                    ps[:, :], b.ones_b1[0:1, 0:128],
                    bo_r[0:1, half * 512: half * 512 + 512],
                    start=False, stop=True),
                    r=["bor", "ones_b1", pk], w=[pk])
            rk = f"R.{t}"
            p.add("vector", lambda e, t=t, half=half, ps=ps: e.tensor_tensor(
                b.R[:, t * D + half * 512: t * D + half * 512 + 512],
                b.R[:, t * D + half * 512: t * D + half * 512 + 512],
                ps[:, :], ALU.add), r=[pk, rk], w=[rk, pk + "r"])


def _l0aa_outproj(b, sb, seq):
    p = b.p
    attn = b._cur_attn
    yt = sb("yt", [128, D], F32)
    yb = sb("yb", [128, D], BF16)
    yf = sb("yf", [128, NC * 128], BF16)
    gscl = b._cur_gscl
    wsb_p = b._cur_wp
    ascl = b._cur_ascl
    for t in range(NT):
        ps = b.psb[0]
        for c in range(NC):
            p.add("tensor", lambda e, c=c, t=t, ps=ps: e.transpose(
                ps[:, c * 128:(c + 1) * 128],
                attn[:, c * S + t * 128: c * S + (t + 1) * 128],
                b.ident_b[:, :]), r=["attn", "ident_b", "PSB0r"],
                w=["PSB0"], tag="tb")
        p.add("scalar", lambda e, ps=ps: e.copy(yt[:, :], ps[:, 0:D]),
              r=["PSB0"], w=["yt", "PSB0r"])
        p.add("vector", lambda e: e.reduce_max(
            gscl[:, 0:1], yt[:, :], AX.X, apply_absolute_value=True),
            r=["yt"], w=["g0"])
        p.add("vector", lambda e: e.tensor_scalar(
            gscl[:, 1:2], gscl[:, 0:1], BITEPS, 1.0 / 127.0,
            ALU.max, ALU.mult), r=["g0"], w=["g1"])
        p.add("vector", lambda e: e.reciprocal(gscl[:, 2:3], gscl[:, 1:2]),
              r=["g1"], w=["g2"])
        p.add("vector", lambda e: e.tensor_scalar(
            b.rs_f[:, 0:D], yt[:, :], gscl[:, 2:3], None, ALU.mult),
            r=["yt", "g2"], w=["rs_f"])
        rint_ladder(b, "rs_f", lambda: b.rs_f[:, 0:D],
                    lambda: yb[:, :], D, "yb", lo=-128, hi=127)
        transpose_tile(b, lambda c: yb[:, c * 128:(c + 1) * 128], "yb",
                       lambda: yf[:, :], "yf")
        p.add("vector", lambda e: e.tensor_scalar(
            gscl[:, 3:4], gscl[:, 1:2], ascl[:, 3:4], None, ALU.mult),
            r=["g1", "ascl"], w=["g3"])
        for half in range(2):
            ps2 = b.pse[half]
            pk = f"PE{half}"
            for c in range(NC):
                p.add("tensor", lambda e, c=c, half=half, ps2=ps2: e.matmul(
                    ps2[:, :], yf[:, c * 128:(c + 1) * 128],
                    wsb_p[:, c * D + half * 512: c * D + half * 512 + 512],
                    start=(c == 0), stop=(c == NC - 1)),
                    r=["yf", "Wp", pk + "r"] if c == 0 else ["yf", "Wp"],
                    w=[pk], tag="pp")
            rk = f"R.{t}"
            p.add("vector", lambda e, t=t, half=half, ps2=ps2:
                  e.scalar_tensor_tensor(
                      b.R[:, t * D + half * 512: t * D + half * 512 + 512],
                      ps2[:, :], gscl[:, 3:4],
                      b.R[:, t * D + half * 512: t * D + half * 512 + 512],
                      ALU.mult, ALU.add), r=[pk, "g3", rk],
                  w=[rk, pk + "r"])



def ffn_quant_stage(b, seq, stg_i):
    p, d, nc = b.p, b.dram, b.nc
    stx = contextlib.ExitStack()
    mk = lambda stk: (lambda nm, shp, dt: stk.enter_context(
        nc.sbuf_tensor(f"qf{stg_i}s{seq}_{nm}", shp, dt)))
    sb = mk(stx)
    xq = sb("xq", [128, NC * S], BF16)
    bq16x = sb("bq16x", [128, 32], F32)
    bq2r = sb("bq2r", [1, D], F32)
    bq2b = sb("bq2b", [1, D], BF16)

    st0 = contextlib.ExitStack()
    sb0 = mk(st0)
    xt2 = [sb0(f"xt{i}", [128, D], F32) for i in range(2)]
    xb2 = [sb0(f"xb{i}", [128, D], BF16) for i in range(2)]
    for t in range(NT):
        i = t % 2
        xt, xb = xt2[i], xb2[i]
        xtk, xbk = f"xt{i}", f"xb{i}"
        ln_tile(b, t, lambda xt=xt: xt[:, :], xtk)
        p.add("vector", lambda e, xt=xt: e.tensor_scalar(
            b.rs_f[:, 0:D], xt[:, :], 16.0, None, ALU.mult),
            r=[xtk], w=["rs_f"])
        rint_ladder(b, "rs_f", lambda: b.rs_f[:, 0:D],
                    lambda xb=xb: xb[:, :], D, xbk, lo=-128, hi=127)
        transpose_tile(b, lambda c, xb=xb: xb[:, c * 128:(c + 1) * 128], xbk,
                       lambda t=t: xq[:, :].rearrange(
                           "a (c n) -> a c n", c=NC)[:, :, t * 128:(t + 1) * 128],
                       "XQ", pi=i)
    b1s = sb0("b1s", [128, 32], F32)
    p.add("sync", lambda e: e.dma_start(b1s[:, :], d["b_l1ffn_1"]),
          w=["b1s"], dma=True)
    p.add("vector", lambda e: e.tensor_scalar(
        b.rs_f[:, 0:32], b1s[:, :], 64.0, None, ALU.mult),
        r=["b1s"], w=["rs_f"])
    p.add("vector", lambda e: e.tensor_copy(b.rs_i[:, 0:32], b.rs_f[:, 0:32]),
          r=["rs_f"], w=["rs_i"])
    p.add("vector", lambda e: e.tensor_scalar(
        bq16x[:, :], b.rs_i[:, 0:32], 0.25, None, ALU.mult),
        r=["rs_i"], w=["bq16x"])
    p.add("sync", lambda e: e.dma_start(bq2r[:, :], d["b_l1ffn_2"]),
          w=["bq2r"], dma=True)
    p.add("vector", lambda e: e.tensor_scalar(
        bq2r[:, :], bq2r[:, :], 64.0, None, ALU.mult), r=["bq2r"], w=["bq2r"])
    p.add("vector", lambda e: e.tensor_copy(b.rs_i[0:1, 0:D], bq2r[:, :]),
          r=["bq2r"], w=["rs_i"])
    p.add("vector", lambda e: e.tensor_scalar(
        bq2b[:, :], b.rs_i[0:1, 0:D], 16.0, None, ALU.mult),
        r=["rs_i"], w=["bq2b"])
    p.barrier()
    st0.close()

    for qr in range(4):
        st1 = contextlib.ExitStack()
        sb1 = mk(st1)
        w1q = sb1(f"w1q{qr}", [128, NC * 1024], BF16)
        w2q = sb1(f"w2q{qr}", [128, 8 * D], BF16)
        wst2 = [sb1(f"wst{qr}{i}", [128, 1024], F32) for i in range(2)]
        hq = sb1(f"hq{qr}", [128, 8 * 512], BF16)
        tq2 = [sb1(f"tq{qr}{i}", [128, 512], F32) for i in range(2)]

        def quantize_block(wi, dst_ap_fn, dk, nn=1024):
            wk = f"wstq{wi}"
            wst = wst2[wi]
            p.add("vector", lambda e, nn=nn, wst=wst: e.tensor_scalar(
                b.rs_f[:, 0:nn], wst[:, 0:nn], 64.0, None, ALU.mult),
                r=[wk], w=["rs_f"])
            p.add("vector", lambda e, nn=nn: e.tensor_copy(
                b.rs_i[:, 0:nn], b.rs_f[:, 0:nn]), r=["rs_f"], w=["rs_i"])
            p.add("vector", lambda e, nn=nn: e.tensor_copy(
                b.rs_f[:, 0:nn], b.rs_i[:, 0:nn]), r=["rs_i"], w=["rs_f"])
            p.add("vector", lambda e, nn=nn: e.tensor_scalar(
                dst_ap_fn(), b.rs_f[:, 0:nn], -128.0, 127.0,
                ALU.max, ALU.min), r=["rs_f"], w=[dk])

        if not getattr(b, "_qw_done", False):
            for c in range(NC):
                wi = c % 2
                p.add("sync", lambda e, c=c, qr=qr, wi=wi: e.dma_start(
                    wst2[wi][:, :], d["w_l1ffn_1"][c * 128:(c + 1) * 128,
                                                   qr * 1024:(qr + 1) * 1024]),
                    r=[f"wstq{wi}"], w=[f"wstq{wi}"], dma=True)
                quantize_block(wi, lambda c=c: w1q[:, c * 1024:(c + 1) * 1024],
                               "W1Q")
            for hc in range(8):
                wi = hc % 2
                p.add("sync", lambda e, hc=hc, qr=qr, wi=wi: e.dma_start(
                    wst2[wi][:, :],
                    d["w_l1ffn_2"][qr * 1024 + hc * 128:
                                   qr * 1024 + (hc + 1) * 128, :]),
                    r=[f"wstq{wi}"], w=[f"wstq{wi}"], dma=True)
                quantize_block(wi, lambda hc=hc: w2q[:, hc * D:(hc + 1) * D],
                               "W2Q")
            p.add("sync", lambda e, qr=qr: e.dma_start(
                d["w1q_scr"][qr], w1q[:, :]), r=["W1Q"], w=["w1qs"], dma=True)
            p.add("sync", lambda e, qr=qr: e.dma_start(
                d["w2q_scr"][qr], w2q[:, :]), r=["W2Q"], w=["w2qs"], dma=True)
        else:
            p.add("gpsimd", lambda e, qr=qr: e.dma_start(
                w1q[:, :], d["w1q_scr"][qr]), r=["w1qs"], w=["W1Q"], dma=True)
            p.add("gpsimd", lambda e, qr=qr: e.dma_start(
                w2q[:, :], d["w2q_scr"][qr]), r=["w2qs"], w=["W2Q"], dma=True)

        for tg in range(4):
            for hc in range(8):
                ps = b.pse[hc % 2]
                pk = f"PE{hc % 2}"
                for fc in range(NC):
                    p.add("tensor",
                          lambda e, fc=fc, hc=hc, tg=tg, ps=ps: e.matmul(
                              ps[:, :],
                              w1q[:, fc * 1024 + hc * 128:
                                  fc * 1024 + hc * 128 + 128],
                              xq[:, fc * S + tg * 512: fc * S + tg * 512 + 512],
                              start=(fc == 0), stop=(fc == NC - 1)),
                          r=["W1Q", "XQ", pk + "r"] if fc == 0 else
                          ["W1Q", "XQ"], w=[pk], tag="m1")
                hcg = qr * 8 + hc
                tq = tq2[hc % 2]
                tk = f"tq{hc % 2}"
                p.add("scalar", lambda e, hcg=hcg, ps=ps, tq=tq: e.activation(
                    tq[:, :], ps[:, :], AF.Identity,
                    bias=bq16x[:, hcg:hcg + 1], scale=2.0 ** -6),
                    r=[pk, "bq16x", tk], w=[tk, pk + "r"])
                rint_ladder(b, tk, lambda tq=tq: tq[:, :],
                            lambda hc=hc: hq[:, hc * 512:(hc + 1) * 512],
                            512, "HQ", lo=0, hi=127)
            for ttl in range(4):
                tt = tg * 4 + ttl
                for dh in range(2):
                    pq = PQ(b, dh)
                    pk = f"PQ{dh}"
                    for hc in range(8):
                        p.add("tensor",
                              lambda e, hc=hc, ttl=ttl, dh=dh, pq=pq:
                              e.matmul(
                                  pq[:, :],
                                  hq[:, hc * 512 + ttl * 128:
                                     hc * 512 + ttl * 128 + 128],
                                  w2q[:, hc * D + dh * 512:
                                      hc * D + dh * 512 + 512],
                                  start=(hc == 0),
                                  stop=(hc == 7 and qr != 0)),
                              r=["HQ", "W2Q", pk + "r"] if hc == 0 else
                              ["HQ", "W2Q"], w=[pk], tag="m2")
                    if qr == 0:
                        p.add("tensor", lambda e, dh=dh, pq=pq: e.matmul(
                            pq[:, :], b.ones_b1[0:1, 0:128],
                            bq2b[0:1, dh * 512: dh * 512 + 512],
                            start=False, stop=True),
                            r=["bq2b", "ones_b1", pk], w=[pk])
                    rk = f"R.{tt}"
                    p.add("vector", lambda e, tt=tt, dh=dh, pq=pq:
                          e.scalar_tensor_tensor(
                              b.R[:, tt * D + dh * 512: tt * D + dh * 512 + 512],
                              pq[:, :], 2.0 ** -10,
                              b.R[:, tt * D + dh * 512: tt * D + dh * 512 + 512],
                              ALU.mult, ALU.add),
                          r=[pk, rk], w=[rk, pk + "r"])
        p.barrier()
        st1.close()
    b._qw_done = True
    p.barrier()
    stx.close()


def ffn_fff_stage(b, seq, stg_i):
    p, d, nc = b.p, b.dram, b.nc
    stx = contextlib.ExitStack()
    mk = lambda stk: (lambda nm, shp, dt: stk.enter_context(
        nc.sbuf_tensor(f"fff{stg_i}s{seq}_{nm}", shp, dt)))
    sb = mk(stx)
    qsqr = sb("qsqr", [1, S], BF16)
    ksqr = sb("ksqr", [1, S], BF16)
    ks1f = sb("ks1f", [1, 1], F32)
    stRF = contextlib.ExitStack()
    sbRF = mk(stRF)
    xf = sbRF("xf", [128, NC * S], BF16)
    scb = sbRF("scb", [128, 45 * NT], F32)
    leafs = sbRF("leafs", [128, 3 * NT], F32)

    st0 = contextlib.ExitStack()
    sb0 = mk(st0)
    xt2 = [sb0(f"xt{i}", [128, D], F32) for i in range(2)]
    xb2 = [sb0(f"xb{i}", [128, D], BF16) for i in range(2)]
    xff2 = [sb0(f"xff{i}", [128, D], F32) for i in range(2)]
    wn = sb0("wn", [128, NC * 45], F32)
    nbr = sb0("nbr", [1, 45], F32)
    selw = sb0("selw", [128, 10 * NT], F32)
    p.add("gpsimd", lambda e: e.dma_start(
        wn[:, :].rearrange("a (c n) -> a c n", c=NC),
        d["w_ffn_node"].rearrange("(c a) n -> a c n", c=NC)),
        w=["WN"], dma=True)
    p.add("gpsimd", lambda e: e.dma_start(nbr[:, :], d["b_ffn_node"]),
          w=["NB"], dma=True)
    for t in range(NT):
        i = t % 2
        xt, xb, xff = xt2[i], xb2[i], xff2[i]
        xtk, xbk, xffk = f"xt{i}", f"xb{i}", f"xff{i}"
        pq0, pq1 = (0, 1) if i == 0 else (2, 3)
        psc = b.pse[i]
        pck = f"PE{i}"
        ln_tile(b, t, lambda xt=xt: xt[:, :], xtk)
        p.add("vector", lambda e, xb=xb, xt=xt: e.tensor_copy(
            xb[:, :], xt[:, :]), r=[xtk], w=[xbk])
        transpose_tile(b, lambda c, xb=xb: xb[:, c * 128:(c + 1) * 128], xbk,
                       lambda t=t: xf[:, :].rearrange(
                           "a (c n) -> a c n", c=NC)
                       [:, :, t * 128:(t + 1) * 128], "XF", pi=i)
        for c in range(NC):
            p.add("tensor", lambda e, c=c, xt=xt, i=i: e.transpose(
                b.PS4[:, i * 1024 + c * 128: i * 1024 + (c + 1) * 128],
                xt[:, c * 128:(c + 1) * 128], b.ident_f[:, :]),
                r=[xtk, "ident_f", f"PQ{pq0}r", f"PQ{pq1}r"] if c == 0 else
                [xtk, "ident_f"], w=[f"PQ{pq0 + c // 4}"], tag="ft")
        p.add("scalar", lambda e, xff=xff, i=i: e.copy(
            xff[:, :], b.PS4[:, i * 1024: i * 1024 + D]),
            r=[f"PQ{pq0}", f"PQ{pq1}"],
            w=[xffk, f"PQ{pq0}r", f"PQ{pq1}r"])
        for c in range(NC):
            p.add("tensor", lambda e, c=c, xff=xff, psc=psc: e.matmul(
                psc[:, 0:45], xff[:, c * 128:(c + 1) * 128],
                wn[:, c * 45:(c + 1) * 45], start=(c == 0), stop=False),
                r=[xffk, "WN", pck + "r"] if c == 0 else [xffk, "WN"],
                w=[pck], tag="sc")
        p.add("tensor", lambda e, psc=psc: e.matmul(
            psc[:, 0:45], b.ones_f1[0:1, 0:128], nbr[0:1, 0:45],
            start=False, stop=True), r=["NB", "ones_f1", pck], w=[pck])
        p.add("scalar", lambda e, t=t, psc=psc: e.copy(
            scb[:, :].rearrange("a (n t) -> a n t", t=NT)[:, :, t:t + 1],
            psc[:, 0:45].unsqueeze(2)), r=[pck], w=["scb", pck + "r"])
    dbg_tap(b, "scb", lambda: scb[:, 0:45 * NT], ["scb"], 45 * NT)

    Wv = lambda j: selw[:, j * NT:(j + 1) * NT]

    def NV(fam, i):
        o = 15 * fam + i
        return scb[:, o * NT:(o + 1) * NT]

    def vop(fn, rr, ww):
        p.add("vector", fn, r=rr, w=ww)

    def mux(dst_j, a_hi_fn, a_lo_fn, bit_j, rr):
        vop(lambda e: e.tensor_tensor(Wv(dst_j), a_hi_fn(), a_lo_fn(),
                                      ALU.subtract), rr + ["selw"], ["selw"])
        vop(lambda e: e.tensor_tensor(Wv(dst_j), Wv(dst_j), Wv(bit_j),
                                      ALU.mult), ["selw"], ["selw"])
        vop(lambda e: e.tensor_tensor(Wv(dst_j), Wv(dst_j), a_lo_fn(),
                                      ALU.add), rr + ["selw"], ["selw"])

    for fam in range(3):
        vop(lambda e, fam=fam: e.tensor_single_scalar(
            Wv(0), NV(fam, 0), 0.0, ALU.is_ge), ["scb", "selw"], ["selw"])
        mux(3, lambda fam=fam: NV(fam, 2), lambda fam=fam: NV(fam, 1), 0,
            ["scb"])
        vop(lambda e: e.tensor_single_scalar(Wv(1), Wv(3), 0.0, ALU.is_ge),
            ["selw"], ["selw"])
        mux(4, lambda fam=fam: NV(fam, 4), lambda fam=fam: NV(fam, 3), 1,
            ["scb"])
        mux(5, lambda fam=fam: NV(fam, 6), lambda fam=fam: NV(fam, 5), 1,
            ["scb"])
        mux(3, lambda: Wv(5), lambda: Wv(4), 0, [])
        vop(lambda e: e.tensor_single_scalar(Wv(2), Wv(3), 0.0, ALU.is_ge),
            ["selw"], ["selw"])
        mux(4, lambda fam=fam: NV(fam, 8), lambda fam=fam: NV(fam, 7), 2,
            ["scb"])
        mux(5, lambda fam=fam: NV(fam, 10), lambda fam=fam: NV(fam, 9), 2,
            ["scb"])
        mux(6, lambda fam=fam: NV(fam, 12), lambda fam=fam: NV(fam, 11), 2,
            ["scb"])
        mux(7, lambda fam=fam: NV(fam, 14), lambda fam=fam: NV(fam, 13), 2,
            ["scb"])
        mux(8, lambda: Wv(5), lambda: Wv(4), 1, [])
        mux(9, lambda: Wv(7), lambda: Wv(6), 1, [])
        mux(3, lambda: Wv(9), lambda: Wv(8), 0, [])
        vop(lambda e: e.tensor_single_scalar(Wv(4), Wv(3), 0.0, ALU.is_ge),
            ["selw"], ["selw"])
        vop(lambda e: e.tensor_scalar(Wv(5), Wv(0), 8.0, None, ALU.mult),
            ["selw"], ["selw"])
        vop(lambda e: e.scalar_tensor_tensor(Wv(5), Wv(1), 4.0, Wv(5),
                                             ALU.mult, ALU.add),
            ["selw"], ["selw"])
        vop(lambda e: e.scalar_tensor_tensor(Wv(5), Wv(2), 2.0, Wv(5),
                                             ALU.mult, ALU.add),
            ["selw"], ["selw"])
        vop(lambda e, fam=fam: e.tensor_tensor(
            leafs[:, fam * NT:(fam + 1) * NT], Wv(5), Wv(4), ALU.add),
            ["selw"], ["leafs"])
    dbg_tap(b, "leafs", lambda: leafs[:, 0:3 * NT], ["leafs"], 3 * NT)
    p.barrier()
    st0.close()

    stF = contextlib.ExitStack()
    sbF = mk(stF)
    selT = sbF("selT", [16, S], BF16)
    selm1 = sbF("selm1", [16, S], BF16)
    seltb = sbF("seltb", [128, 16], BF16)
    hfm = sbF("hfm", [128, NC * S], BF16)
    w1hb2 = [sbF(f"w1hb{i}", [128, NC * 128], BF16) for i in range(2)]
    w2cb2 = [sbF(f"w2cb{i}", [128, NC * 128], BF16) for i in range(2)]
    wst2 = [sbF(f"wst{i}", [128, D], F32) for i in range(2)]
    b2s = sbF("b2s", [16, D], BF16)
    b1rb = sbF("b1rb", [1, D], BF16)
    oo = [sbF(f"oo{i}", [128, 512], BF16) for i in range(2)]
    oq = [sbF(f"oq{i}", [128, 512], BF16) for i in range(2)]
    vtk = sbF("vtk", [128, 512], BF16)

    for fam in (2, 1, 0):
        pp = "qkv"[fam]
        scr_nm = {0: "qf_scr", 1: "kf_scr"}.get(fam)
        for t in range(NT):
            p.add("vector", lambda e, fam=fam, t=t: e.tensor_tensor(
                seltb[:, :],
                leafs[:, fam * NT + t: fam * NT + t + 1]
                .broadcast_to([128, 16]), b.lvconst[:, 0:16], ALU.is_equal),
                r=["leafs", "lvconst", "seltb"], w=["seltb"])
            p.add("tensor", lambda e: e.transpose(
                b.psb[0][0:16, 0:128], seltb[:, :], b.ident_b[:, :]),
                r=["seltb", "ident_b", "PSB0r"], w=["PSB0"])
            p.add("scalar", lambda e, t=t: e.copy(
                selT[0:16, t * 128:(t + 1) * 128], b.psb[0][0:16, 0:128]),
                r=["PSB0"], w=["selT", "PSB0r"])
        p.add("vector", lambda e: e.tensor_scalar(
            selm1[0:16, :], selT[0:16, :], -1.0, 1.0e6,
            ALU.add, ALU.mult), r=["selT", "selm1"], w=["selm1"])
        p.add("gpsimd", lambda e, pp=pp: e.dma_start(
            wst2[0][0:16, :], d[f"b2_{pp}"]), r=["wstb0"], w=["wstb0"],
            dma=True)
        p.add("vector", lambda e: e.tensor_copy(b2s[:, :], wst2[0][0:16, :]),
              r=["wstb0", "b2s"], w=["b2s"])
        p.add("gpsimd", lambda e, pp=pp: e.dma_start(
            wst2[1][0:1, :], d[f"b1r_{pp}"]), r=["wstb1"], w=["wstb1"],
            dma=True)
        p.add("vector", lambda e: e.tensor_copy(b1rb[:, :], wst2[1][0:1, :]),
              r=["wstb1", "b1rb"], w=["b1rb"])
        for hb in range(NC):
            wi = hb % 2
            wst, wsk = wst2[wi], f"wstb{wi}"
            w1hb, w1k = w1hb2[wi], f"w1hb{wi}"
            p.add("gpsimd", lambda e, pp=pp, hb=hb, wst=wst: e.dma_start(
                wst[:, :].rearrange("a (c n) -> a c n", c=NC),
                d[f"w_ffn{pp}_1"][:, hb * 128:(hb + 1) * 128]
                .rearrange("(c a) n -> a c n", c=NC)),
                r=[wsk], w=[wsk], dma=True)
            p.add("vector", lambda e, w1hb=w1hb, wst=wst: e.tensor_copy(
                w1hb[:, :], wst[:, :]), r=[wsk, w1k], w=[w1k])
            for ts in range(4):
                ps = b.pse[ts % 2]
                pk = f"PE{ts % 2}"
                for c in range(NC):
                    p.add("tensor",
                          lambda e, c=c, ts=ts, ps=ps, w1hb=w1hb: e.matmul(
                              ps[:, :], w1hb[:, c * 128:(c + 1) * 128],
                              xf[:, c * S + ts * 512: c * S + ts * 512 + 512],
                              start=(c == 0), stop=False),
                          r=[w1k, "XF", pk + "r"] if c == 0 else
                          [w1k, "XF"], w=[pk], tag="h1")
                p.add("tensor", lambda e, hb=hb, ps=ps: e.matmul(
                    ps[:, :], b1rb[0:1, hb * 128:(hb + 1) * 128],
                    b.ones_b512[0:1, 0:512], start=False, stop=False),
                    r=["b1rb", "ones_b512", pk], w=[pk])
                p.add("tensor", lambda e, hb=hb, ts=ts, ps=ps: e.matmul(
                    ps[:, :], b.e16[0:16, hb * 128:(hb + 1) * 128],
                    selm1[0:16, ts * 512: ts * 512 + 512],
                    start=False, stop=True),
                    r=["e16", "selm1", pk], w=[pk])
                p.add("scalar", lambda e, hb=hb, ts=ts, ps=ps: e.activation(
                    hfm[:, hb * S + ts * 512: hb * S + ts * 512 + 512],
                    ps[:, :], AF.Relu, bias=0.0, scale=1.0),
                    r=[pk], w=["HFM", pk + "r"])
        for c in range(NC):
            wi = c % 2
            wst, wsk = wst2[wi], f"wstb{wi}"
            w2cb, w2k = w2cb2[wi], f"w2cb{wi}"
            p.add("gpsimd", lambda e, pp=pp, c=c, wst=wst: e.dma_start(
                wst[:, :].rearrange("a (h n) -> a h n", h=NC),
                d[f"w_ffn{pp}_2"][:, c * 128:(c + 1) * 128]
                .rearrange("(h a) n -> a h n", h=NC)),
                r=[wsk], w=[wsk], dma=True)
            p.add("vector", lambda e, w2cb=w2cb, wst=wst: e.tensor_copy(
                w2cb[:, :], wst[:, :]), r=[wsk, w2k], w=[w2k])
            for ts in range(4):
                ps = b.pse[ts % 2]
                pk = f"PE{ts % 2}"
                ob = oo[ts % 2]
                ok = f"oo{ts % 2}"
                for hc in range(NC):
                    p.add("tensor",
                          lambda e, hc=hc, ts=ts, ps=ps, w2cb=w2cb: e.matmul(
                              ps[:, :], w2cb[:, hc * 128:(hc + 1) * 128],
                              hfm[:, hc * S + ts * 512:
                                  hc * S + ts * 512 + 512],
                              start=(hc == 0), stop=False),
                          r=[w2k, "HFM", pk + "r"] if hc == 0 else
                          [w2k, "HFM"], w=[pk], tag="h2")
                p.add("tensor", lambda e, c=c, ts=ts, ps=ps: e.matmul(
                    ps[:, :], b2s[0:16, c * 128:(c + 1) * 128],
                    selT[0:16, ts * 512: ts * 512 + 512],
                    start=False, stop=True),
                    r=["b2s", "selT", pk], w=[pk])
                p.add("scalar", lambda e, ps=ps, ob=ob: e.copy(
                    ob[:, :], ps[:, :]), r=[pk, ok], w=[ok, pk + "r"])
                if fam != 2:
                    p.add("sync", lambda e, c=c, ts=ts, scr_nm=scr_nm, ob=ob:
                          e.dma_start(
                              d[scr_nm][c][:, ts * 512:(ts + 1) * 512],
                              ob[:, :]), r=[ok], w=[scr_nm], dma=True)
                    oqb = oq[ts % 2]
                    oqn = f"oq{ts % 2}"
                    p.add("scalar", lambda e, ob=ob, oqb=oqb: e.activation(
                        oqb[:, :], ob[:, :], AF.Square, bias=0.0, scale=1.0),
                        r=[ok, oqn], w=[oqn])
                    p.add("tensor", lambda e, c=c, ts=ts, oqb=oqb: e.matmul(
                        b.PS4[0:1, ts * 512:(ts + 1) * 512],
                        b.ones_colb[:, 0:1], oqb[:, :],
                        start=(c == 0), stop=(c == NC - 1)),
                        r=[oqn, "ones_colb", f"PQ{ts}r"] if c == 0 else
                        [oqn, "ones_colb"], w=[f"PQ{ts}"], tag="nsq")
                else:
                    for j in range(4):
                        p.add("tensor", lambda e, j=j, ob=ob: e.transpose(
                            b.psb[0][:, j * 128:(j + 1) * 128],
                            ob[:, j * 128:(j + 1) * 128], b.ident_b[:, :]),
                            r=[ok, "ident_b", "PSB0r"], w=["PSB0"])
                    p.add("scalar", lambda e: e.copy(
                        vtk[:, :], b.psb[0][:, 0:512]),
                        r=["PSB0", "vtk"], w=["vtk", "PSB0r"])
                    p.add("sync", lambda e, c=c, ts=ts: e.dma_start(
                        d["v_scr"][c, ts * 512:(ts + 1) * 512, :]
                        .rearrange("(j a) n -> a j n", j=4),
                        vtk[:, :].rearrange("a (j n) -> a j n", j=4)),
                        r=["vtk"], w=["v_scr"], dma=True)
        if fam == 1:
            p.add("scalar", lambda e: e.copy(ksqr[0:1, :], b.PS4[0:1, 0:S]),
                  r=[f"PQ{kb}" for kb in range(4)],
                  w=["ksqr"] + [f"PQ{kb}r" for kb in range(4)])
            p.add("vector", lambda e: e.reduce_max(
                ks1f[0:1, 0:1], ksqr[0:1, :], AX.X), r=["ksqr"], w=["ks1f"])
        elif fam == 0:
            p.add("scalar", lambda e: e.copy(qsqr[0:1, :], b.PS4[0:1, 0:S]),
                  r=[f"PQ{kb}" for kb in range(4)],
                  w=["qsqr"] + [f"PQ{kb}r" for kb in range(4)])
    p.barrier()
    stF.close()
    stRF.close()

    stA = contextlib.ExitStack()
    sbA = mk(stA)
    kfm = sbA("kfm", [128, NC * S], BF16)
    vall = sbA("vall", [128, NT * D], BF16)
    qfg = sbA("qfg", [128, NC * 512], BF16)
    expT = sbA("expT", [128, NT * 512], BF16)
    mposr = sbA("mposr", [1, S], BF16)
    mnegb = sbA("mnegb", [1, S], BF16)
    srow = sbA("srow", [1, 512], F32)
    rcol = sbA("rcol", [128, 4], F32)
    p.add("scalar", lambda e: e.activation(
        mposr[0:1, :], qsqr[0:1, :], AF.Sqrt, bias=0.0,
        scale=ks1f[0:1, 0:1]), r=["qsqr", "ks1f"], w=["mposr"])
    p.add("vector", lambda e: e.tensor_scalar(
        mnegb[0:1, :], mposr[0:1, :], -1.0, None, ALU.mult),
        r=["mposr"], w=["mnegb"])
    for c in range(NC):
        p.add("sync", lambda e, c=c: e.dma_start(
            kfm[:, c * S:(c + 1) * S], d["kf_scr"][c]),
            r=["kf_scr"], w=[f"kfm{c}"], dma=True)
        p.add("gpsimd", lambda e, c=c: e.dma_start(
            vall[:, :].rearrange("a (t m) -> a t m", m=D)
            [:, :, c * 128:(c + 1) * 128],
            d["v_scr"][c].rearrange("(t a) n -> a t n", t=NT)),
            r=["v_scr"], w=["vall"], dma=True)
    for qg in range(4):
        for c in range(NC):
            p.add("sync", lambda e, qg=qg, c=c: e.dma_start(
                qfg[:, c * 512:(c + 1) * 512],
                d["qf_scr"][c][:, qg * 512:(qg + 1) * 512]),
                r=["qf_scr", "qfg"], w=["qfg"], dma=True)
        for kt in range(NT):
            ps, pk = [(b.pse[0], "PE0"), (b.pse[1], "PE1"),
                      (PQ(b, 0), "PQ0")][kt % 3]
            for c in range(NC):
                p.add("tensor", lambda e, kt=kt, c=c, ps=ps: e.matmul(
                    ps[:, :],
                    kfm[:, c * S + kt * 128: c * S + kt * 128 + 128],
                    qfg[:, c * 512:(c + 1) * 512],
                    start=(c == 0), stop=False),
                    r=[f"kfm{c}", "qfg", pk + "r"] if c == 0 else
                    [f"kfm{c}", "qfg"], w=[pk], tag="kq")
            p.add("tensor", lambda e, ps=ps, qg=qg: e.matmul(
                ps[:, :], b.ones_b1[0:1, 0:128],
                mnegb[0:1, qg * 512:(qg + 1) * 512],
                start=False, stop=True), r=["mnegb", "ones_b1", pk], w=[pk])
            p.add("scalar", lambda e, kt=kt, ps=ps: e.activation(
                expT[:, kt * 512:(kt + 1) * 512], ps[:, :], AF.Exp,
                bias=0.0, scale=1.0 / 32.0), r=[pk], w=["expT", pk + "r"])
            p.add("tensor", lambda e, kt=kt: e.matmul(
                b.PS4[0:1, 1536:2048], b.ones_colb[:, 0:1],
                expT[:, kt * 512:(kt + 1) * 512],
                start=(kt == 0), stop=(kt == NT - 1)),
                r=["expT", "ones_colb", "PQ3r"] if kt == 0 else
                ["expT", "ones_colb"], w=["PQ3"], tag="sm")
        p.add("scalar", lambda e: e.copy(srow[0:1, :], b.PS4[0:1, 1536:2048]),
              r=["PQ3"], w=["srow", "PQ3r"])
        for j0 in range(4):
            p.add("tensor", lambda e, j0=j0: e.transpose(
                b.pse[0][:, j0:j0 + 1], srow[0:1, j0 * 128:(j0 + 1) * 128],
                b.ident_f[0:1, 0:1]),
                r=["srow", "ident_f", "PE0r"], w=["PE0"])
        p.add("vector", lambda e: e.reciprocal(
            rcol[:, 0:4], b.pse[0][:, 0:4]),
            r=["PE0", "rcol"], w=["rcol", "PE0r"])
        for j0 in range(4):
            qt = qg * 4 + j0
            for h in range(2):
                pqk = 1 + h
                pq = PQ(b, pqk)
                pk = f"PQ{pqk}"
                for kt in range(NT):
                    p.add("tensor", lambda e, kt=kt, j0=j0, h=h, pq=pq:
                          e.matmul(pq[:, :],
                                   expT[:, kt * 512 + j0 * 128:
                                        kt * 512 + j0 * 128 + 128],
                                   vall[:, kt * D + h * 512:
                                        kt * D + h * 512 + 512],
                                   start=(kt == 0), stop=(kt == NT - 1)),
                          r=["expT", "vall", pk + "r"] if kt == 0 else
                          ["expT", "vall"], w=[pk], tag="av")
                rk = f"R.{qt}"
                p.add("vector", lambda e, qt=qt, h=h, j0=j0, pq=pq:
                      e.scalar_tensor_tensor(
                          b.R[:, qt * D + h * 512: qt * D + h * 512 + 512],
                          pq[:, :], rcol[:, j0:j0 + 1],
                          b.R[:, qt * D + h * 512: qt * D + h * 512 + 512],
                          ALU.mult, ALU.add),
                      r=[pk, "rcol", rk], w=[rk, pk + "r"])
    p.barrier()
    stA.close()
    stx.close()



def _build_body(b, stages, seqs):
    p, d = b.p, b.dram
    _load_consts(b)
    _build_trig(b)
    _build_relpos(b)
    p.barrier()
    for seq in seqs:
        p.add("sync", lambda e, seq=seq: e.dma_start(
            b.R[:, :].rearrange("a (t n) -> a t n", t=NT),
            d["x_in"][seq].rearrange("(t a) n -> a t n", t=NT)),
            r=[], w=[f"R.{t}" for t in range(NT)], dma=True)
        for stg_i in stages:
            if stg_i in (1, 3, 5, 7):
                kind = {1: "l0aa", 3: "l0ab", 5: "l1aa", 7: "l1ab"}[stg_i]
                mha_stage(b, seq, kind)
            elif stg_i in (2, 4):
                ffn_fff_stage(b, seq, stg_i)
            else:
                ffn_quant_stage(b, seq, stg_i)
        p.add("sync", lambda e, seq=seq: e.dma_start(
            d["x_out"][seq].rearrange("(t a) n -> a t n", t=NT),
            b.R[:, :].rearrange("a (t n) -> a t n", t=NT)),
            r=[f"R.{t}" for t in range(NT)], w=[f"XOUT{seq}"], dma=True)
    p.add("sync", lambda e: e.nop(),
          r=[f"XOUT{seq}" for seq in seqs])



def _run(inputs, stages=ALL_STAGES, x_override=None):
    ins = dict(inputs)
    if x_override is not None:
        ins["L0_attn_a_x"] = x_override
    per_core = host_prepare(ins)
    nc = build_nc(stages=stages)
    in_maps = [per_core(c) for c in range(8)]
    res = run_bass_kernel_spmd(nc, in_maps, core_ids=list(range(8)))
    out = np.concatenate([np.asarray(r["x_out"]).reshape(NSEQ, S, D)
                          for r in res.results], axis=0)
    return out


def kernel(**inputs):
    return _run(inputs, stages=ALL_STAGES)
